# version 08
# baseline (speedup 1.0000x reference)
import numpy as np
import ml_dtypes
from contextlib import ExitStack

import concourse.bass as bass
import concourse.mybir as mybir
from concourse.bass_utils import run_bass_kernel_spmd

F32 = mybir.dt.float32
BF16 = mybir.dt.bfloat16
AF = mybir.ActivationFunctionType
ALU = mybir.AluOpType
AX = mybir.AxisListType

SEQ, OWN, D = 4096, 2048, 1024
NDT = 8
NH, HD, NPAIR = 16, 64, 8
NTF, NTO = 32, 16
NCH = 4
NKC = 8
E, F, FS = 8, 1408, 2816
EPS = 1e-5

FFN_UNITS = []
for _e in range(E):
    FFN_UNITS.append(("moe", _e, 0, 6))
    FFN_UNITS.append(("moe", _e, 6, 5))
for _f0, _nf in [(0, 6), (6, 6), (12, 5), (17, 5)]:
    FFN_UNITS.append(("sh", None, _f0, _nf))


class Sched:
    ENGINES = ("sync", "tensor", "scalar", "vector", "gpsimd")

    def __init__(self, nc, ctx):
        self.nc, self.ctx = nc, ctx
        self.q = {e: [] for e in self.ENGINES}
        self.sems = {}
        self.dma_sems = set()

    def _sem(self, name):
        if name not in self.sems:
            self.sems[name] = [self.ctx.enter_context(self.nc.semaphore(name)), 0]
        return self.sems[name]

    def n(self, name):
        return self._sem(name)[1]

    def op(self, eng, meth, *args, waits=(), inc=None, inck=1, **kw):
        ws = []
        for s, v in waits:
            rec = self._sem(s)
            ws.append((rec[0], rec[1] if v is None else v))
        ih = None
        if inc is not None:
            rec = self._sem(inc)
            rec[1] += inck
            ih = (rec[0], inck)
            if inck == 16:
                self.dma_sems.add(inc)
        self.q[eng].append((meth, args, kw, ws, ih))

    def counts(self):
        return {k: v[1] for k, v in self.sems.items()}

    def emit(self, blk):
        for eng in self.ENGINES:
            lst = self.q[eng]
            if not lst:
                continue

            def runner(e, lst=lst):
                for meth, args, kw, ws, ih in lst:
                    for h, v in ws:
                        if v > 0:
                            e.wait_ge(h, v)
                    ins = getattr(e, meth)(*args, **kw)
                    if ih is not None:
                        ins.then_inc(*ih)
            getattr(blk, eng)(runner)
        self.q = {e: [] for e in self.ENGINES}


def build_program(dbg=(), last_phase=6):
    _dbgc = [0]
    nc = bass.Bass("TRN2", target_bir_lowering=False, debug=False)

    xf = nc.dram_tensor("xf", [SEQ, D], F32, kind="ExternalInput").ap()
    cq = nc.dram_tensor("cq", [D, D], F32, kind="ExternalInput").ap()
    ck = nc.dram_tensor("ck", [D, D], F32, kind="ExternalInput").ap()
    cv = nc.dram_tensor("cv", [D, D], F32, kind="ExternalInput").ap()
    co = nc.dram_tensor("co", [D, D], F32, kind="ExternalInput").ap()
    bT = nc.dram_tensor("bT", [128, 4 * NDT], F32, kind="ExternalInput").ap()
    gwT = nc.dram_tensor("gwT", [D, E], F32, kind="ExternalInput").ap()
    wg = nc.dram_tensor("wg", [E, D, F], F32, kind="ExternalInput").ap()
    wu = nc.dram_tensor("wu", [E, D, F], F32, kind="ExternalInput").ap()
    wd = nc.dram_tensor("wd", [E, F, D], F32, kind="ExternalInput").ap()
    swg = nc.dram_tensor("swg", [D, FS], F32, kind="ExternalInput").ap()
    swu = nc.dram_tensor("swu", [D, FS], F32, kind="ExternalInput").ap()
    swd = nc.dram_tensor("swd", [FS, D], F32, kind="ExternalInput").ap()
    out = nc.dram_tensor("out", [OWN, D], F32, kind="ExternalOutput").ap()
    x1s = nc.dram_tensor("x1s", [NTO, 128, D], F32, kind="Internal").ap()

    dbga = {}
    for name, shape, dt in [
        ("xnT", [128, NDT, SEQ], BF16),
        ("qt", [128, OWN], BF16), ("kt", [128, SEQ], BF16),
        ("vt", [128, 32 * 130], BF16),
        ("outT", [128, NPAIR, OWN], BF16),
        ("xn2T", [128, NDT, OWN], BF16),
        ("comb", [128, NTO * E], F32),
        ("yacc", [128, NDT, OWN], F32),
    ]:
        if name in dbg:
            dbga[name] = nc.dram_tensor("dbg_" + name, shape, dt,
                                        kind="ExternalOutput").ap()

    with ExitStack() as ctx:
        En = ctx.enter_context
        S = Sched(nc, ctx)

        consts = En(nc.sbuf_tensor("consts", [128, 8], F32))
        C_EPS = consts[:, 0:1]
        C_NHALF = consts[:, 1:2]
        C_EIGHTH = consts[:, 3:4]
        C_M12 = consts[:, 4:5]
        identB = En(nc.sbuf_tensor("identB", [128, 128], BF16))
        shift64 = En(nc.sbuf_tensor("shift64", [128, 128], BF16))
        onescolF = En(nc.sbuf_tensor("onescolF", [128, 128], F32))
        selmat = En(nc.sbuf_tensor("selmat", [8, E, 128], BF16))
        biasT = En(nc.sbuf_tensor("biasT", [128, 4 * NDT], F32))
        gw_sb = En(nc.sbuf_tensor("gw_sb", [128, NDT, E], BF16))
        m_sum = En(nc.sbuf_tensor("m_sum", [128, NTF], F32))
        m_ssq = En(nc.sbuf_tensor("m_ssq", [128, NTF], F32))
        m_mean = En(nc.sbuf_tensor("m_mean", [128, NTF], F32))
        m_rstd = En(nc.sbuf_tensor("m_rstd", [128, NTF], F32))
        m_negms = En(nc.sbuf_tensor("m_negms", [128, NTF], F32))
        cT_sb = En(nc.sbuf_tensor("cT_sb", [8, OWN], BF16))
        xn2T = En(nc.sbuf_tensor("xn2T", [128, NDT, OWN], BF16))

        S.op("gpsimd", "memset", consts[:, 0:1], EPS)
        S.op("gpsimd", "memset", consts[:, 1:2], -0.5)
        S.op("gpsimd", "memset", consts[:, 2:3], 1.0 / D)
        S.op("gpsimd", "memset", consts[:, 3:4], 0.125)
        S.op("gpsimd", "memset", consts[:, 4:5], -12.0)
        S.op("gpsimd", "memset", identB[:], 0.0, inc="gi")
        S.op("gpsimd", "affine_select", out=identB[:], in_=identB[:],
             compare_op=ALU.not_equal, fill=1.0, base=0,
             pattern=[[-1, 128]], channel_multiplier=1,
             waits=[("gi", None)], inc="gi")
        S.op("gpsimd", "memset", shift64[:], 0.0, inc="gi")
        S.op("gpsimd", "affine_select", out=shift64[:], in_=shift64[:],
             compare_op=ALU.not_equal, fill=1.0, base=-64,
             pattern=[[1, 128]], channel_multiplier=-1,
             waits=[("gi", None)], inc="gi")
        S.op("gpsimd", "memset", onescolF[:], 1.0)
        S.op("gpsimd", "memset", selmat[:], 0.0, inc="gi")
        S.op("gpsimd", "affine_select", out=selmat[:], in_=selmat[:],
             compare_op=ALU.not_equal, fill=1.0, base=0,
             pattern=[[-1, E], [0, 128]], channel_multiplier=1,
             waits=[("gi", None)], inc="gc")
        S.op("gpsimd", "dma_start", out=biasT[:], in_=bT, inc="g0a", inck=16)
        S.op("gpsimd", "dma_start", out=gw_sb[:],
             in_=gwT.rearrange("(t p) e -> p t e", p=128), inc="g0b", inck=16)
        GC0 = S.n("gc")

        outT_stk = ExitStack()
        outT = outT_stk.enter_context(
            nc.sbuf_tensor("outT", [128, NPAIR, OWN], BF16))
        xnT_stk = ExitStack()
        xnT = xnT_stk.enter_context(
            nc.sbuf_tensor("xnT", [128, NDT, SEQ], BF16))
        open_stks = [outT_stk, xnT_stk]

        def finish():
            dws = [(d, None) for d in sorted(S.dma_sems)]
            S.op("sync", "nop", waits=dws)
            S.op("gpsimd", "nop", waits=dws)
            for st in reversed(open_stks):
                st.close()
            b = En(nc.Block())
            S.emit(b)
            return nc

        xr = xf.rearrange("(t p) d -> t p d", p=128)

        with ExitStack() as c1:
            cc = c1.enter_context
            xst = cc(nc.sbuf_tensor("xst", [128, 8, D], F32))
            xnst = cc(nc.sbuf_tensor("xnst", [128, 2, D], BF16))
            sq_scr = cc(nc.sbuf_tensor("sq_scr", [128, D], F32))
            psT = [cc(nc.psum_tensor(f"psT{i}", [128, D], BF16)) for i in range(2)]

            for j in range(NTF // 4):
                ws = []
                if j >= 2:
                    ws = [("aA", 4 * j - 4), ("vA", 4 * j - 4)]
                S.op("sync", "dma_start",
                     out=xst[:, (4 * j) % 8:(4 * j) % 8 + 4, :],
                     in_=xr[4 * j:4 * j + 4].rearrange("t p d -> p t d"),
                     waits=ws, inc=f"xldA{j % 2}", inck=16)
            for t in range(NTF):
                j = t // 4
                wx = (f"xldA{j % 2}", 16 * (j // 2 + 1))
                S.op("vector", "reduce_sum", m_sum[:, t:t + 1], xst[:, t % 8, :],
                     axis=AX.X, waits=[wx], inc="vA")
                S.op("scalar", "activation", sq_scr[:], xst[:, t % 8, :], AF.Square,
                     accum_out=m_ssq[:, t:t + 1],
                     waits=[wx, ("gc", GC0), ("aA", t)],
                     inc="aA")
            S.op("vector", "tensor_scalar_mul", m_mean[:], m_sum[:], 1.0 / D,
                 waits=[("vA", None)], inc="vS")
            S.op("vector", "tensor_mul", m_negms[:], m_mean[:], m_mean[:],
                 waits=[("vS", None)], inc="vS")
            S.op("vector", "scalar_tensor_tensor", m_rstd[:], m_ssq[:], 1.0 / D,
                 m_negms[:], ALU.mult, ALU.subtract,
                 waits=[("aA", None), ("vS", None)], inc="vS")
            S.op("scalar", "activation", m_ssq[:], m_rstd[:], AF.Ln, bias=C_EPS,
                 waits=[("vS", None)], inc="aS")
            S.op("scalar", "activation", m_rstd[:], m_ssq[:], AF.Exp, scale=C_NHALF,
                 waits=[("aS", None)], inc="aS")
            S.op("vector", "scalar_tensor_tensor", m_negms[:], m_mean[:], -1.0,
                 m_rstd[:], ALU.mult, ALU.mult, waits=[("aS", None)], inc="vS")
            STATS1 = S.n("vS")

            for j in range(NTF // 4):
                ws = [("vA", None), ("aA", None)]
                if j >= 2:
                    ws.append(("vN", 4 * j - 4))
                S.op("sync", "dma_start",
                     out=xst[:, (4 * j) % 8:(4 * j) % 8 + 4, :],
                     in_=xr[4 * j:4 * j + 4].rearrange("t p d -> p t d"),
                     waits=ws, inc=f"xldB{j % 2}", inck=16)
            for t in range(NTF):
                jb = t // 4
                ws = [(f"xldB{jb % 2}", 16 * (jb // 2 + 1)), ("vS", STATS1)]
                if t >= 2:
                    ws.append(("pTr", t - 1))
                S.op("vector", "tensor_scalar", xnst[:, t % 2, :], xst[:, t % 8, :],
                     m_rstd[:, t:t + 1], m_negms[:, t:t + 1], ALU.mult, ALU.add,
                     waits=ws, inc="vN")
                for dt in range(NDT):
                    ws = [("vN", S.n("vN")), ("gc", GC0)]
                    if t >= 2:
                        ws.append(("vE", t - 1))
                    S.op("tensor", "transpose",
                         psT[t % 2][:, dt * 128:(dt + 1) * 128],
                         xnst[:, t % 2, dt * 128:(dt + 1) * 128], identB[:],
                         waits=ws if dt == 0 else (),
                         inc="pTr" if dt == NDT - 1 else None)
                S.op("vector", "tensor_copy",
                     xnT[:, :, t * 128:(t + 1) * 128],
                     psT[t % 2][:].rearrange("p (dt c) -> p dt c", c=128),
                     waits=[("pTr", t + 1)], inc="vE")
        XNT = S.n("vE")

        if "xnT" in dbga:
            S.op("gpsimd", "dma_start", out=dbga["xnT"], in_=xnT[:],
                 waits=[("vE", XNT)], inc=f"dbg0", inck=16)

        if last_phase < 2:
            return finish()

        with ExitStack() as c2:
            cc = c2.enter_context
            cqp = [cc(nc.sbuf_tensor(f"cqp{i}", [128, NDT, 128], BF16)) for i in range(2)]
            ckp = [cc(nc.sbuf_tensor(f"ckp{i}", [128, NDT, 128], BF16)) for i in range(2)]
            cvp = [cc(nc.sbuf_tensor(f"cvp{i}", [128, NDT, 128], BF16)) for i in range(2)]
            QTp = [cc(nc.sbuf_tensor(f"QTp{i}", [128, OWN], BF16)) for i in range(2)]
            KTp = [cc(nc.sbuf_tensor(f"KTp{i}", [128, SEQ], BF16)) for i in range(2)]
            Vp = [cc(nc.sbuf_tensor(f"Vp{i}", [128, 32, 130], BF16)) for i in range(2)]
            PT = cc(nc.sbuf_tensor("PT", [128, 3, 1024], BF16))
            lscr = cc(nc.sbuf_tensor("lscr", [128, 2, 512], F32))
            invl = cc(nc.sbuf_tensor("invl", [128, 2, 512], F32))
            stg = cc(nc.sbuf_tensor("stg", [128, 2, 512], BF16))
            psS = [cc(nc.psum_tensor(f"psS{i}", [128, 1024], F32)) for i in range(2)]
            psO = cc(nc.psum_tensor("psO", [128, 1024], F32))
            psP = [cc(nc.psum_tensor(f"psP{i}", [128, 512], F32)) for i in range(2)]

            for i in range(2):
                S.op("gpsimd", "memset", Vp[i][:], 1.0,
                     waits=[("vE", XNT)], inc="gc")
            GV = S.n("gc")

            groups = [(2 * g, 2) for g in range(16)]
            UPC = 32
            pair_end = {}
            proj_pP = {}
            proj_done = {}
            wpair_lv = {}

            def emit_wdma(p):
                bp_ = p % 2
                colsl_ = slice(128 * p, 128 * (p + 1))
                ws = [("g0a", None), ("g0b", None), ("gc", GV), ("vE", XNT)]
                if p >= 2:
                    ws.append(("pP", proj_pP[p - 2]))
                for cn, cmat, dst in (("q", cq, cqp[bp_]), ("k", ck, ckp[bp_]),
                                      ("v", cv, cvp[bp_])):
                    S.op("gpsimd", "dma_start", out=dst[:],
                         in_=cmat.rearrange("(t p) n -> p t n", p=128)[:, :, colsl_],
                         waits=ws, inc=f"wld{cn}{bp_}", inck=16)
                    ws = []
                wpair_lv[p] = [(f"wld{cn}{bp_}", 16 * (p // 2 + 1))
                               for cn in ("q", "k", "v")]

            def make_jobs(p):
                return [(p, "q", c) for c in range(NCH)] + \
                       [(p, "k", kc) for kc in range(NKC)] + \
                       [(p, "v", kt) for kt in range(32)]

            class ProjQ:

                def __init__(self, p, gates):
                    self.p = p
                    self.jobs = make_jobs(p)
                    self.j = 0
                    self.vp0 = S.n("vP")
                    self.gates = gates

                def emit_one(self):
                    p, kind, idx = self.jobs[self.j]
                    bp_ = p % 2
                    pj = psP[self.j % 2][:]
                    ws = list(wpair_lv[p])
                    if self.j >= 2:
                        ws.append(("vP", self.vp0 + self.j - 1))
                    ev_ws = [("pP", None)] + self.gates
                    if kind == "q":
                        for dt in range(NDT):
                            S.op("tensor", "matmul", pj,
                                 cqp[bp_][:, dt, :],
                                 xnT[:, dt, idx * 512:(idx + 1) * 512],
                                 start=(dt == 0), stop=(dt == NDT - 1),
                                 waits=ws if dt == 0 else (),
                                 inc="pP" if dt == NDT - 1 else None)
                        S.op("vector", "tensor_scalar",
                             QTp[bp_][:, idx * 512:(idx + 1) * 512], pj,
                             biasT[:, p:p + 1], None, ALU.add,
                             waits=ev_ws, inc="vP")
                    elif kind == "k":
                        for dt in range(NDT):
                            S.op("tensor", "matmul", pj,
                                 ckp[bp_][:, dt, :],
                                 xnT[:, dt, idx * 512:(idx + 1) * 512],
                                 start=(dt == 0), stop=(dt == NDT - 1),
                                 waits=ws if dt == 0 else (),
                                 inc="pP" if dt == NDT - 1 else None)
                        S.op("vector", "tensor_scalar",
                             KTp[bp_][:, idx * 512:(idx + 1) * 512], pj,
                             biasT[:, NDT + p:NDT + p + 1], None, ALU.add,
                             waits=ev_ws, inc="vP")
                    else:
                        pv = pj[:, 0:128]
                        for dt in range(NDT):
                            S.op("tensor", "matmul", pv,
                                 xnT[:, dt, idx * 128:(idx + 1) * 128],
                                 cvp[bp_][:, dt, :],
                                 start=(dt == 0), stop=(dt == NDT - 1),
                                 waits=ws if dt == 0 else (),
                                 inc="pP" if dt == NDT - 1 else None)
                        dstv = Vp[bp_][:, idx, :].rearrange(
                            "p (a b) -> p a b", b=65)[:, :, 0:64]
                        srcv = pv.rearrange("p (a b) -> p a b", b=64)
                        S.op("vector", "tensor_copy", dstv, srcv,
                             waits=ev_ws, inc="vP")
                    self.j += 1

                def drain_to(self, frac):
                    tgt = min(len(self.jobs), int(len(self.jobs) * frac + 0.999))
                    while self.j < tgt:
                        self.emit_one()

                def finish(self):
                    while self.j < len(self.jobs):
                        self.emit_one()
                    proj_pP[self.p] = S.n("pP")
                    proj_done[self.p] = S.n("vP")

            emit_wdma(0)
            if NPAIR > 1:
                emit_wdma(1)
            q0 = ProjQ(0, [])
            q0.finish()

            for p in range(NPAIR):
                bp = p % 2
                if p + 2 < NPAIR:
                    emit_wdma(p + 2)
                if p + 1 < NPAIR:
                    nq = ProjQ(p + 1, [("pQK", pair_end[p - 1]["pQK"]),
                                       ("pPV", pair_end[p - 1]["pPV"])]
                               if p > 0 else [])
                else:
                    nq = None

                exp_at = {}
                pv_at = {}
                recip_at = {}
                evac_at = {}
                pend = []

                def emit_pv(job):
                    u, cc_, pts, h = job
                    ws = [("aE", exp_at[u])]
                    if u % UPC <= 1:
                        ws.append(("vT", evac_at.get(cc_ - 1, S.n("vT"))
                                   if cc_ > 0 else S.n("vT")))
                    vsl = slice(65 * h, 65 * h + 65)
                    k0_, glen_ = groups[(u % UPC) // 2]
                    for i in range(glen_):
                        kt = k0_ + i
                        S.op("tensor", "matmul",
                             psO[0:65, 512 * h:512 * h + 512],
                             Vp[bp][:, kt, vsl],
                             PT[:, pts, i * 512:(i + 1) * 512],
                             start=(kt == 0), stop=(kt == 31),
                             waits=ws if i == 0 else (),
                             inc="pPV" if i == glen_ - 1 else None)
                    pv_at[u] = S.n("pPV")

                for c in range(NCH):
                    qsl = slice(c * 512, (c + 1) * 512)
                    for gi, (k0, glen) in enumerate(groups):
                        ua = c * UPC + 2 * gi
                        ub = ua + 1
                        wsa = []
                        wsb = []
                        if ua >= 2:
                            wsa.append(("aE", exp_at[ua - 2]))
                            wsb.append(("aE", exp_at[ub - 2]))
                        else:
                            wsa = [("vT", S.n("vT")), ("vP", proj_done[p])]
                            wsb = list(wsa)
                        if c > 0 and ua == c * UPC:
                            wsb.append(("vT", recip_at[c - 1]))
                        for h, slot, ws0 in ((0, 0, wsa), (1, 1, wsb)):
                            rowsl = slice(64 * h, 64 * (h + 1))
                            for i in range(glen):
                                kt = k0 + i
                                S.op("tensor", "matmul",
                                     psS[slot][:, i * 512:(i + 1) * 512],
                                     KTp[bp][rowsl, kt * 128:(kt + 1) * 128],
                                     QTp[bp][rowsl, qsl],
                                     start=True, stop=True,
                                     tile_position=(64 * h, 0),
                                     waits=ws0 if i == 0 else (),
                                     inc="pQK" if i == glen - 1 else None)
                        nqk_b = S.n("pQK")
                        while pend:
                            emit_pv(pend.pop(0))
                        for h, u in ((0, ua), (1, ub)):
                            ws = [("pQK", nqk_b - (1 if h == 0 else 0))]
                            if u >= 3:
                                ws.append(("pPV", pv_at[u - 3]))
                            S.op("scalar", "activation",
                                 PT[:, u % 3, 0:glen * 512],
                                 psS[u % 2][:, 0:glen * 512],
                                 AF.Exp, bias=C_M12, scale=C_EIGHTH,
                                 waits=ws, inc="aE")
                            exp_at[u] = S.n("aE")
                        pend.append((ua, c, ua % 3, 0))
                        pend.append((ub, c, ub % 3, 1))
                        if nq is not None:
                            nq.drain_to((2 * gi + 2) / UPC / NCH + c / NCH)
                    while pend:
                        emit_pv(pend.pop(0))
                    S.op("vector", "tensor_copy", lscr[64:65, 0, :],
                         psO[64:65, 0:512],
                         waits=[("pPV", None), ("pT", None)], inc="vT")
                    S.op("vector", "tensor_copy", lscr[64:65, 1, :],
                         psO[64:65, 512:1024], inc="vT")
                    S.op("tensor", "matmul", psS[1][0:64, 0:512],
                         onescolF[64:65, 0:64], lscr[64:65, 0, :],
                         start=True, stop=True,
                         waits=[("vT", None), ("aE", None)], inc="pT")
                    S.op("tensor", "matmul", psS[1][0:64, 512:1024],
                         onescolF[64:65, 0:64], lscr[64:65, 1, :],
                         start=True, stop=True, inc="pT")
                    S.op("vector", "reciprocal",
                         invl[0:64, :, :].rearrange("p a b -> p (a b)"),
                         psS[1][0:64, 0:1024],
                         waits=[("pT", None)], inc="vT")
                    recip_at[c] = S.n("vT")
                    S.op("vector", "tensor_tensor", stg[0:64, 0, :],
                         psO[0:64, 0:512], invl[0:64, 0, :], ALU.mult,
                         waits=[("vT", None), ("pT", None)], inc="vT")
                    S.op("vector", "tensor_tensor", stg[0:64, 1, :],
                         psO[0:64, 512:1024], invl[0:64, 1, :], ALU.mult,
                         inc="vT")
                    S.op("tensor", "matmul", psO[:, 0:512], identB[0:64, 0:128],
                         stg[0:64, 0, :], start=True, stop=False,
                         waits=[("vT", None)])
                    S.op("tensor", "matmul", psO[:, 0:512], shift64[0:64, 0:128],
                         stg[0:64, 1, :], start=False, stop=True, inc="pT")
                    S.op("vector", "tensor_scalar", outT[:, p, qsl], psO[:, 0:512],
                         biasT[:, 2 * NDT + p:2 * NDT + p + 1], None, ALU.add,
                         waits=[("pT", None)], inc="vT")
                    evac_at[c] = S.n("vT")
                if nq is not None:
                    nq.finish()
                pair_end[p] = {"pQK": S.n("pQK"), "pPV": S.n("pPV"),
                               "vT": S.n("vT")}

                if p == 0:
                    if "qt" in dbga:
                        S.op("gpsimd", "dma_start", out=dbga["qt"], in_=QTp[0][:],
                             waits=[("vP", proj_done[0])], inc="dbg90", inck=16)
                    if "kt" in dbga:
                        S.op("gpsimd", "dma_start", out=dbga["kt"], in_=KTp[0][:],
                             waits=[("vP", proj_done[0])], inc="dbg91", inck=16)
                    if "vt" in dbga:
                        S.op("gpsimd", "dma_start", out=dbga["vt"],
                             in_=Vp[0][:].rearrange("p a b -> p (a b)"),
                             waits=[("vP", proj_done[0])], inc="dbg92", inck=16)
            ATT = S.counts()

        if "outT" in dbga:
            S.op("gpsimd", "dma_start", out=dbga["outT"], in_=outT[:],
                 waits=[("vT", ATT["vT"])], inc=f"dbg4", inck=16)

        xnT_stk.close()
        open_stks.remove(xnT_stk)

        if last_phase < 3:
            return finish()

        with ExitStack() as c3:
            cc = c3.enter_context
            co_sb = cc(nc.sbuf_tensor("co_sb", [128, NDT, D], BF16))
            aost = cc(nc.sbuf_tensor("aost", [128, NDT, 512], BF16))
            xo = cc(nc.sbuf_tensor("xo", [128, 8, D], F32))
            x1t = cc(nc.sbuf_tensor("x1t", [128, NTO, D], F32))
            xn2st = cc(nc.sbuf_tensor("xn2st", [128, 2, D], BF16))
            sq2 = cc(nc.sbuf_tensor("sq2", [128, D], F32))
            psC = [cc(nc.psum_tensor(f"psC{i}", [128, 512], F32)) for i in range(2)]
            psT3 = [cc(nc.psum_tensor(f"psT3{i}", [128, D], BF16)) for i in range(2)]

            S.op("gpsimd", "dma_start", out=co_sb[:],
                 in_=co.rearrange("(t p) n -> p t n", p=128),
                 waits=[("vT", ATT["vT"]), ("g0a", None), ("g0b", None)],
                 inc="wldc", inck=16)
            def load_xo(j):
                ws = [("vT", ATT["vT"])] if j <= 1 else [("vAdd", 4 * (j - 1))]
                S.op("sync", "dma_start",
                     out=xo[:, (4 * j) % 8:(4 * j) % 8 + 4, :],
                     in_=xr[4 * j:4 * j + 4].rearrange("t p d -> p t d"),
                     waits=ws, inc=f"xoL{j % 2}", inck=16)

            load_xo(0)
            load_xo(1)

            for c in range(NCH):
                qsl = slice(c * 512, (c + 1) * 512)
                for dout in range(NDT):
                    pj = psC[dout % 2][:]
                    ws = [("wldc", 16), ("vT", ATT["vT"])]
                    if c * NDT + dout >= 2:
                        ws.append(("aCo", S.n("aCo") - 1))
                    for dt in range(NDT):
                        S.op("tensor", "matmul", pj,
                             co_sb[:, dt, dout * 128:(dout + 1) * 128],
                             outT[:, dt, qsl],
                             start=(dt == 0), stop=(dt == NDT - 1),
                             waits=ws if dt == 0 else (),
                             inc="pCo" if dt == NDT - 1 else None)
                    S.op("scalar", "activation", aost[:, dout, :], pj, AF.Identity,
                         bias=biasT[:, 3 * NDT + dout:3 * NDT + dout + 1],
                         waits=[("pCo", None), ("pT3", None)], inc="aCo")
                ACO = S.n("aCo")
                for tt in range(4):
                    T = c * 4 + tt
                    for dt in range(NDT):
                        ws = [("aCo", ACO), ("vAdd", max(0, T - 1))]
                        S.op("tensor", "transpose",
                             psT3[T % 2][:, dt * 128:(dt + 1) * 128],
                             aost[:, dt, tt * 128:(tt + 1) * 128], identB[:],
                             waits=ws if dt == 0 else (),
                             inc="pT3" if dt == NDT - 1 else None)
                    S.op("vector", "tensor_tensor", x1t[:, T, :], xo[:, T % 8, :],
                         psT3[T % 2][:], ALU.add,
                         waits=[("pT3", None),
                                (f"xoL{(T // 4) % 2}", 16 * (T // 4 // 2 + 1))],
                         inc="vAdd")
                    S.op("sync", "dma_start", out=x1s[T], in_=x1t[:, T, :],
                         waits=[("vAdd", None)], inc=f"stX{T % 2}", inck=16)
                    S.op("vector", "reduce_sum", m_sum[:, T:T + 1], x1t[:, T, :],
                         axis=AX.X, waits=[("vAdd", None)], inc="vL")
                    S.op("scalar", "activation", sq2[:], x1t[:, T, :], AF.Square,
                         accum_out=m_ssq[:, T:T + 1],
                         waits=[("vAdd", None), ("aL", None)], inc="aL")
                if c + 2 < NCH:
                    load_xo(c + 2)
            s16 = slice(0, NTO)
            S.op("vector", "tensor_scalar_mul", m_mean[:, s16], m_sum[:, s16],
                 1.0 / D, waits=[("vL", None)], inc="vS2")
            S.op("vector", "tensor_mul", m_negms[:, s16], m_mean[:, s16],
                 m_mean[:, s16], waits=[("vS2", None)], inc="vS2")
            S.op("vector", "scalar_tensor_tensor", m_rstd[:, s16], m_ssq[:, s16],
                 1.0 / D, m_negms[:, s16], ALU.mult, ALU.subtract,
                 waits=[("aL", None), ("vS2", None)], inc="vS2")
            S.op("scalar", "activation", m_ssq[:, s16], m_rstd[:, s16], AF.Ln,
                 bias=C_EPS, waits=[("vS2", None)], inc="aS2")
            S.op("scalar", "activation", m_rstd[:, s16], m_ssq[:, s16], AF.Exp,
                 scale=C_NHALF, waits=[("aS2", None)], inc="aS2")
            S.op("vector", "scalar_tensor_tensor", m_negms[:, s16], m_mean[:, s16],
                 -1.0, m_rstd[:, s16], ALU.mult, ALU.mult,
                 waits=[("aS2", None)], inc="vS2")
            STATS2 = S.n("vS2")
            for T in range(NTO):
                ws = [("vS2", STATS2)]
                if T >= 2:
                    ws.append(("pT4", T - 1))
                S.op("vector", "tensor_scalar", xn2st[:, T % 2, :], x1t[:, T, :],
                     m_rstd[:, T:T + 1], m_negms[:, T:T + 1], ALU.mult, ALU.add,
                     waits=ws, inc="vN2")
                for dt in range(NDT):
                    ws = [("vN2", S.n("vN2"))]
                    if T >= 2:
                        ws.append(("vE2", T - 1))
                    S.op("tensor", "transpose",
                         psT3[T % 2][:, dt * 128:(dt + 1) * 128],
                         xn2st[:, T % 2, dt * 128:(dt + 1) * 128], identB[:],
                         waits=ws if dt == 0 else (),
                         inc="pT4" if dt == NDT - 1 else None)
                S.op("vector", "tensor_copy",
                     xn2T[:, :, T * 128:(T + 1) * 128],
                     psT3[T % 2][:].rearrange("p (dt c2) -> p dt c2", c2=128),
                     waits=[("pT4", T + 1)], inc="vE2")
            P3 = S.counts()

        if "xn2T" in dbga:
            S.op("gpsimd", "dma_start", out=dbga["xn2T"], in_=xn2T[:],
                 waits=[("vE2", P3["vE2"])], inc=f"dbg5", inck=16)

        outT_stk.close()
        open_stks.remove(outT_stk)

        if last_phase < 4:
            return finish()

        with ExitStack() as c4:
            cc = c4.enter_context
            identF = cc(nc.sbuf_tensor("identF", [128, 128], F32))
            lg_sb = cc(nc.sbuf_tensor("lg_sb", [8, NCH, 512], F32))
            lgt = cc(nc.sbuf_tensor("lgt", [128, NTO, E], F32))
            rt_is1 = cc(nc.sbuf_tensor("rt_is1", [128, NTO, E], F32))
            rt_is2 = cc(nc.sbuf_tensor("rt_is2", [128, NTO, E], F32))
            rt_sh = cc(nc.sbuf_tensor("rt_sh", [128, NTO, E], F32))
            rt_m1 = cc(nc.sbuf_tensor("rt_m1", [128, NTO], F32))
            rt_m2 = cc(nc.sbuf_tensor("rt_m2", [128, NTO], F32))
            rt_z = cc(nc.sbuf_tensor("rt_z", [128, NTO], F32))
            rt_e2 = cc(nc.sbuf_tensor("rt_e2", [128, NTO], F32))
            rt_w1 = cc(nc.sbuf_tensor("rt_w1", [128, NTO], F32))
            rt_w2 = cc(nc.sbuf_tensor("rt_w2", [128, NTO], F32))
            comb_bf = cc(nc.sbuf_tensor("comb_bf", [128, NTO, E], BF16))
            psG = cc(nc.psum_tensor("psG", [8, NCH * 512], F32))
            psR = cc(nc.psum_tensor("psR", [128, NTO * 8], F32))
            psR2 = cc(nc.psum_tensor("psR2", [8, NTO * 128], BF16))

            S.op("gpsimd", "memset", identF[:], 0.0,
                 waits=[("vE2", P3["vE2"]), ("stX0", 16 * (NTO // 2)),
                        ("stX1", 16 * (NTO // 2))], inc="gF")
            S.op("gpsimd", "affine_select", out=identF[:], in_=identF[:],
                 compare_op=ALU.not_equal, fill=1.0, base=0,
                 pattern=[[-1, 128]], channel_multiplier=1,
                 waits=[("gF", None)], inc="gF")

            for ch in range(NCH):
                for dt in range(NDT):
                    S.op("tensor", "matmul", psG[:, ch * 512:(ch + 1) * 512],
                         gw_sb[:, dt, :], xn2T[:, dt, ch * 512:(ch + 1) * 512],
                         start=(dt == 0), stop=(dt == NDT - 1),
                         waits=[("vE2", P3["vE2"]), ("g0b", None)]
                         if (ch == 0 and dt == 0) else (),
                         inc="pG" if dt == NDT - 1 else None)
            S.op("vector", "tensor_copy",
                 lg_sb[:].rearrange("p a b -> p (a b)"), psG[:],
                 waits=[("pG", None)], inc="vR")
            lg2 = lg_sb[:].rearrange("p a b -> p (a b)")
            for T in range(NTO):
                S.op("tensor", "transpose", psR[:, T * 8:(T + 1) * 8],
                     lg2[0:8, T * 128:(T + 1) * 128], identF[0:8, 0:8],
                     waits=[("vR", None), ("gF", None)] if T == 0 else (),
                     inc="pR" if T == NTO - 1 else None)
            S.op("vector", "tensor_copy", lgt[:].rearrange("p a b -> p (a b)"),
                 psR[:], waits=[("pR", None)], inc="vR")
            for T in range(NTO):
                S.op("vector", "reduce_max", rt_m1[:, T:T + 1], lgt[:, T, :],
                     axis=AX.X, waits=[("vR", None)], inc="vR")
                S.op("vector", "tensor_scalar", rt_is1[:, T, :], lgt[:, T, :],
                     rt_m1[:, T:T + 1], None, ALU.is_ge,
                     waits=[("vR", None)], inc="vR")
                S.op("vector", "scalar_tensor_tensor", rt_sh[:, T, :],
                     rt_is1[:, T, :], -1e30, lgt[:, T, :], ALU.mult, ALU.add,
                     waits=[("vR", None)], inc="vR")
                S.op("vector", "reduce_max", rt_m2[:, T:T + 1], rt_sh[:, T, :],
                     axis=AX.X, waits=[("vR", None)], inc="vR")
                S.op("vector", "tensor_scalar", rt_is2[:, T, :], rt_sh[:, T, :],
                     rt_m2[:, T:T + 1], None, ALU.is_ge,
                     waits=[("vR", None)], inc="vR")
                S.op("vector", "tensor_scalar", rt_sh[:, T, :], lgt[:, T, :],
                     rt_m1[:, T:T + 1], None, ALU.subtract,
                     waits=[("vR", None)], inc="vR")
            S.op("scalar", "activation", rt_sh[:], rt_sh[:], AF.Exp,
                 waits=[("vR", None), ("aL", P3["aL"])], inc="aR")
            S.op("vector", "reduce_sum", rt_z[:], rt_sh[:], axis=AX.X,
                 waits=[("aR", None)], inc="vR")
            S.op("vector", "tensor_sub", rt_e2[:], rt_m2[:], rt_m1[:],
                 waits=[("vR", None)], inc="vR")
            S.op("scalar", "activation", rt_e2[:], rt_e2[:], AF.Exp,
                 waits=[("vR", None)], inc="aR")
            S.op("vector", "scalar_tensor_tensor", rt_w1[:], rt_z[:], 1e-20,
                 rt_e2[:], ALU.mult, ALU.add,
                 waits=[("aR", None), ("vR", None)], inc="vR")
            S.op("vector", "tensor_scalar", rt_w1[:], rt_w1[:], 1.0, None, ALU.add,
                 waits=[("vR", None)], inc="vR")
            S.op("vector", "reciprocal", rt_w1[:], rt_w1[:],
                 waits=[("vR", None)], inc="vR")
            S.op("vector", "tensor_scalar", rt_w1[:], rt_w1[:], 2.5, None, ALU.mult,
                 waits=[("vR", None)], inc="vR")
            S.op("vector", "tensor_mul", rt_w2[:], rt_e2[:], rt_w1[:],
                 waits=[("vR", None)], inc="vR")
            b1, b2 = bass.broadcast_tensor_aps(
                rt_is1[:], rt_w1[:].rearrange("p (t o) -> p t o", o=1))
            S.op("vector", "tensor_tensor", rt_is1[:], b1, b2, ALU.mult,
                 waits=[("vR", None)], inc="vR")
            b1, b2 = bass.broadcast_tensor_aps(
                rt_is2[:], rt_w2[:].rearrange("p (t o) -> p t o", o=1))
            S.op("vector", "tensor_tensor", rt_is2[:], b1, b2, ALU.mult,
                 waits=[("vR", None)], inc="vR")
            S.op("vector", "tensor_add", lgt[:], rt_is1[:], rt_is2[:],
                 waits=[("vR", None)], inc="vR")
            if "comb" in dbga:
                S.op("gpsimd", "dma_start", out=dbga["comb"],
                     in_=lgt[:].rearrange("p a b -> p (a b)"),
                     waits=[("vR", None)], inc=f"dbg6", inck=16)
            S.op("vector", "tensor_copy", comb_bf[:], lgt[:],
                 waits=[("vR", None)], inc="vR")
            for T in range(NTO):
                S.op("tensor", "transpose", psR2[:, T * 128:(T + 1) * 128],
                     comb_bf[:, T, :], identB[:],
                     waits=[("vR", None)] if T == 0 else (),
                     inc="pR2" if T == NTO - 1 else None)
            S.op("vector", "tensor_copy", cT_sb[:], psR2[:],
                 waits=[("pR2", None)], inc="vR")
            P4 = S.counts()

        if last_phase < 5:
            return finish()

        yacc = En(nc.sbuf_tensor("yacc", [128, NDT, OWN], F32))
        with ExitStack() as c5:
            cc = c5.enter_context
            wgh = [cc(nc.sbuf_tensor(f"wgh{i}", [128, NDT, 768], BF16)) for i in range(2)]
            wuh = [cc(nc.sbuf_tensor(f"wuh{i}", [128, NDT, 768], BF16)) for i in range(2)]
            wdh = [cc(nc.sbuf_tensor(f"wdh{i}", [128, 6, D], BF16)) for i in range(2)]
            prod = cc(nc.sbuf_tensor("prod", [128, 6, OWN // 2], BF16))
            gs_sb = cc(nc.sbuf_tensor("gs_sb", [128, 2, 512], BF16))
            us_sb = cc(nc.sbuf_tensor("us_sb", [128, 2, 512], BF16))
            ce_sb = cc(nc.sbuf_tensor("ce_sb", [128, OWN], BF16))
            psGU = [cc(nc.psum_tensor(f"psGU{i}", [128, 512], F32)) for i in range(4)]
            psD = [cc(nc.psum_tensor(f"psD{i}", [128, 512], F32)) for i in range(2)]
            psCE = cc(nc.psum_tensor("psCE", [128, 512], F32))

            unit_end = {}
            jig = 0
            for ui, (kind, e, f0, nf) in enumerate(FFN_UNITS):
                bu = ui % 2
                ws = [("vT", ATT["vT"])]
                if ui >= 2:
                    ws.append(("pD", unit_end[ui - 2]["pD"]))
                    ws.append(("pGU", unit_end[ui - 2]["pGU"]))
                if kind == "moe":
                    srcg = wg[e].rearrange("(t p) f -> p t f", p=128)[:, :, f0 * 128:(f0 + nf) * 128]
                    srcu = wu[e].rearrange("(t p) f -> p t f", p=128)[:, :, f0 * 128:(f0 + nf) * 128]
                    srcd = wd[e, f0 * 128:(f0 + nf) * 128, :].rearrange("(t p) d2 -> p t d2", p=128)
                else:
                    srcg = swg.rearrange("(t p) f -> p t f", p=128)[:, :, f0 * 128:(f0 + nf) * 128]
                    srcu = swu.rearrange("(t p) f -> p t f", p=128)[:, :, f0 * 128:(f0 + nf) * 128]
                    srcd = swd[f0 * 128:(f0 + nf) * 128, :].rearrange("(t p) d2 -> p t d2", p=128)
                S.op("gpsimd", "dma_start", out=wgh[bu][:, :, 0:nf * 128], in_=srcg,
                     waits=ws, inc=f"wfg{bu}", inck=16)
                S.op("gpsimd", "dma_start", out=wuh[bu][:, :, 0:nf * 128], in_=srcu,
                     inc=f"wfu{bu}", inck=16)
                S.op("gpsimd", "dma_start", out=wdh[bu][:, 0:nf, :], in_=srcd,
                     inc=f"wfd{bu}", inck=16)
                lvl = 16 * (ui // 2 + 1)
                WUg = (f"wfg{bu}", lvl)
                WUu = (f"wfu{bu}", lvl)
                WUd = (f"wfd{bu}", lvl)

                if kind == "moe" and f0 == 0:
                    for ch in range(NCH):
                        ws = [("vUs", None), ("vCE", None)]
                        if ui == 0 and ch == 0:
                            ws += [("vR", P4["vR"])]
                        S.op("tensor", "matmul", psCE[:],
                             selmat[:, e, :], cT_sb[:, ch * 512:(ch + 1) * 512],
                             start=True, stop=True, waits=ws, inc="pCE")
                        S.op("vector", "tensor_copy",
                             ce_sb[:, ch * 512:(ch + 1) * 512], psCE[:],
                             waits=[("pCE", None)], inc="vCE")
                CE = S.n("vCE")

                for th in range(2):
                    for ft in range(nf):
                        for chl in range(2):
                            ch = 2 * th + chl
                            pg = psGU[(2 * jig) % 4][:]
                            pu = psGU[(2 * jig + 1) % 4][:]
                            ws = [WUg, ("vE2", P3["vE2"])]
                            if ui == 0 and th == 0 and ft == 0 and chl == 0:
                                ws.append(("vR", P4["vR"]))
                            if jig >= 2:
                                ws.append(("aSil", jig - 1))
                            for dt in range(NDT):
                                S.op("tensor", "matmul", pg,
                                     wgh[bu][:, dt, ft * 128:(ft + 1) * 128],
                                     xn2T[:, dt, ch * 512:(ch + 1) * 512],
                                     start=(dt == 0), stop=(dt == NDT - 1),
                                     waits=ws if dt == 0 else (),
                                     inc="pGU" if dt == NDT - 1 else None)
                            ws2 = [WUu]
                            if jig >= 2:
                                ws2.append(("vUs", jig - 1))
                            for dt in range(NDT):
                                S.op("tensor", "matmul", pu,
                                     wuh[bu][:, dt, ft * 128:(ft + 1) * 128],
                                     xn2T[:, dt, ch * 512:(ch + 1) * 512],
                                     start=(dt == 0), stop=(dt == NDT - 1),
                                     waits=ws2 if dt == 0 else (),
                                     inc="pGU" if dt == NDT - 1 else None)
                            PGU = S.n("pGU")
                            S.op("scalar", "activation", gs_sb[:, jig % 2, :], pg,
                                 AF.Silu,
                                 waits=[("pGU", PGU - 1), ("vPr", max(0, jig - 1))],
                                 inc="aSil")
                            if kind == "moe":
                                S.op("vector", "tensor_tensor", us_sb[:, jig % 2, :],
                                     pu, ce_sb[:, ch * 512:(ch + 1) * 512], ALU.mult,
                                     waits=[("pGU", PGU), ("vCE", CE),
                                            ("vPr", max(0, S.n("vPr") - 1))],
                                     inc="vUs")
                            else:
                                S.op("vector", "tensor_copy", us_sb[:, jig % 2, :],
                                     pu,
                                     waits=[("pGU", PGU),
                                            ("vPr", max(0, S.n("vPr") - 1))],
                                     inc="vUs")
                            S.op("vector", "tensor_mul",
                                 prod[:, ft, chl * 512:(chl + 1) * 512],
                                 gs_sb[:, jig % 2, :], us_sb[:, jig % 2, :],
                                 waits=[("aSil", None), ("vUs", None), ("pD", None)],
                                 inc="vPr")
                            jig += 1
                    PRODV = S.n("vPr")
                    for dout in range(NDT):
                        for chl in range(2):
                            ch = 2 * th + chl
                            ji = dout * 2 + chl
                            pd = psD[ji % 2][:]
                            ws = [("vPr", PRODV), WUd]
                            if ui == 0 and th == 0 and ji < 2:
                                ws.append(("vR", P4["vR"]))
                            if ji >= 2 or ui > 0 or th > 0:
                                ws.append(("vF2", S.n("vF2") - 1))
                            for ft in range(nf):
                                S.op("tensor", "matmul", pd,
                                     wdh[bu][:, ft, dout * 128:(dout + 1) * 128],
                                     prod[:, ft, chl * 512:(chl + 1) * 512],
                                     start=(ft == 0), stop=(ft == nf - 1),
                                     waits=ws if ft == 0 else (),
                                     inc="pD" if ft == nf - 1 else None)
                            if ui == 0:
                                S.op("vector", "tensor_copy",
                                     yacc[:, dout, ch * 512:(ch + 1) * 512], pd,
                                     waits=[("pD", None)], inc="vF2")
                            else:
                                S.op("vector", "tensor_tensor",
                                     yacc[:, dout, ch * 512:(ch + 1) * 512],
                                     yacc[:, dout, ch * 512:(ch + 1) * 512], pd,
                                     ALU.add, waits=[("pD", None), ("vF2", None)],
                                     inc="vF2")
                unit_end[ui] = {"pD": S.n("pD"), "pGU": S.n("pGU")}
            P5 = S.counts()

        if "yacc" in dbga:
            S.op("gpsimd", "dma_start", out=dbga["yacc"], in_=yacc[:],
                 waits=[("vF2", P5["vF2"])], inc=f"dbg7", inck=16)

        if last_phase < 6:
            return finish()

        with ExitStack() as c6:
            cc = c6.enter_context
            ybf = cc(nc.sbuf_tensor("ybf", [128, 2, NDT, 512], BF16))
            x1r = cc(nc.sbuf_tensor("x1r", [128, 4, D], F32))
            obuf = cc(nc.sbuf_tensor("obuf", [128, 4, D], F32))
            psT6 = [cc(nc.psum_tensor(f"psT6{i}", [128, D], BF16)) for i in range(2)]

            outr = out.rearrange("(t p) d -> t p d", p=128)

            def load_x1(T, ws):
                S.op("sync", "dma_start", out=x1r[:, T % 4, :], in_=x1s[T],
                     waits=ws, inc=f"x1L{T % 4}", inck=16)

            for T in range(4):
                load_x1(T, [("stX0", 16 * (NTO // 2)), ("stX1", 16 * (NTO // 2)),
                            ("vF2", P5["vF2"]), ("aSil", P5["aSil"]),
                            ("vPr", P5["vPr"])])
            for c in range(NCH):
                ws = [("vF2", P5["vF2"])]
                if c >= 2:
                    ws.append(("pT6", 4 * (c - 1)))
                S.op("vector", "tensor_copy", ybf[:, c % 2, :, :],
                     yacc[:, :, c * 512:(c + 1) * 512],
                     waits=ws, inc="vY")
                VY = S.n("vY")
                for tt in range(4):
                    T = c * 4 + tt
                    for dt in range(NDT):
                        ws = [("vY", VY)]
                        if T >= 2:
                            ws.append(("vO", T - 1))
                        S.op("tensor", "transpose",
                             psT6[T % 2][:, dt * 128:(dt + 1) * 128],
                             ybf[:, c % 2, dt, tt * 128:(tt + 1) * 128], identB[:],
                             waits=ws if dt == 0 else (),
                             inc="pT6" if dt == NDT - 1 else None)
                    S.op("vector", "tensor_tensor", obuf[:, T % 4, :],
                         x1r[:, T % 4, :], psT6[T % 2][:], ALU.add,
                         waits=[("pT6", None), (f"x1L{T % 4}", 16 * (T // 4 + 1)),
                                (f"stO{T % 4}", 16 * (T // 4))],
                         inc="vO")
                    S.op("sync", "dma_start", out=outr[T], in_=obuf[:, T % 4, :],
                         waits=[("vO", None)], inc=f"stO{T % 4}", inck=16)
                    if T + 4 < NTO:
                        load_x1(T + 4, [("vO", None)])
            fin = [(d, None) for d in sorted(S.dma_sems)]
            S.op("sync", "nop", waits=fin)
            S.op("gpsimd", "nop", waits=fin)

        blk = En(nc.Block())
        S.emit(blk)

    return nc


def _circulant(c):
    c = np.asarray(c, np.float32)
    idx = (np.arange(D)[None, :] - np.arange(D)[:, None]) % D
    return np.ascontiguousarray(c[idx])


def _biasT(b):
    return np.ascontiguousarray(np.asarray(b, np.float32).reshape(NDT, 128).T)


def make_inputs(inputs):
    x = np.ascontiguousarray(np.asarray(inputs["L0_attn_x"], np.float32))
    base = {
        "cq": _circulant(inputs["L0_attn_wq_c"]),
        "ck": _circulant(inputs["L0_attn_wk_c"]),
        "cv": _circulant(inputs["L0_attn_wv_c"]),
        "co": _circulant(inputs["L0_attn_wo_c"]),
        "bT": np.ascontiguousarray(np.concatenate(
            [_biasT(inputs["L0_attn_wq_b"]), _biasT(inputs["L0_attn_wk_b"]),
             _biasT(inputs["L0_attn_wv_b"]), _biasT(inputs["L0_attn_wo_b"])],
            axis=1)),
        "gwT": np.ascontiguousarray(
            np.asarray(inputs["L0_ffn_gate_w"], np.float32).T),
        "wg": np.ascontiguousarray(np.asarray(inputs["L0_ffn_w_gate"], np.float32)),
        "wu": np.ascontiguousarray(np.asarray(inputs["L0_ffn_w_up"], np.float32)),
        "wd": np.ascontiguousarray(np.asarray(inputs["L0_ffn_w_down"], np.float32)),
        "swg": np.ascontiguousarray(np.asarray(inputs["L0_ffn_sw_gate"], np.float32)),
        "swu": np.ascontiguousarray(np.asarray(inputs["L0_ffn_sw_up"], np.float32)),
        "swd": np.ascontiguousarray(np.asarray(inputs["L0_ffn_sw_down"], np.float32)),
    }
    in_maps = []
    for core in range(8):
        b, h = core // 2, core % 2
        xb = x[b]
        m = dict(base)
        m["xf"] = np.ascontiguousarray(
            np.concatenate([xb[h * OWN:(h + 1) * OWN],
                            xb[(1 - h) * OWN:(2 - h) * OWN]], axis=0))
        in_maps.append(m)
    return in_maps


def kernel(**inputs):
    nc = build_program()
    in_maps = make_inputs(inputs)
    res = run_bass_kernel_spmd(nc, in_maps, core_ids=list(range(8)))
    y = np.empty((4, SEQ, D), np.float32)
    for core in range(8):
        b, h = core // 2, core % 2
        y[b, h * OWN:(h + 1) * OWN] = res.results[core]["out"]
    return y


# version 10
# speedup vs baseline: 1.0894x; 1.0894x over previous
import numpy as np
import ml_dtypes
from contextlib import ExitStack

import concourse.bass as bass
import concourse.mybir as mybir
from concourse.bass_utils import run_bass_kernel_spmd

F32 = mybir.dt.float32
BF16 = mybir.dt.bfloat16
AF = mybir.ActivationFunctionType
ALU = mybir.AluOpType
AX = mybir.AxisListType

SEQ, OWN, D = 4096, 2048, 1024
NDT = 8
NH, HD, NPAIR = 16, 64, 8
NTF, NTO = 32, 16
NCH = 4
NKC = 8
E, F, FS = 8, 1408, 2816
EPS = 1e-5

FFN_UNITS = []
for _e in range(E):
    FFN_UNITS.append(("moe", _e, 0, 6))
    FFN_UNITS.append(("moe", _e, 6, 5))
for _f0, _nf in [(0, 6), (6, 6), (12, 5), (17, 5)]:
    FFN_UNITS.append(("sh", None, _f0, _nf))


class Sched:
    ENGINES = ("sync", "tensor", "scalar", "vector", "gpsimd")

    def __init__(self, nc, ctx):
        self.nc, self.ctx = nc, ctx
        self.q = {e: [] for e in self.ENGINES}
        self.sems = {}
        self.dma_sems = set()

    def _sem(self, name):
        if name not in self.sems:
            self.sems[name] = [self.ctx.enter_context(self.nc.semaphore(name)), 0]
        return self.sems[name]

    def n(self, name):
        return self._sem(name)[1]

    def op(self, eng, meth, *args, waits=(), inc=None, inck=1, **kw):
        ws = []
        for s, v in waits:
            rec = self._sem(s)
            ws.append((rec[0], rec[1] if v is None else v))
        ih = None
        if inc is not None:
            rec = self._sem(inc)
            rec[1] += inck
            ih = (rec[0], inck)
            if inck == 16:
                self.dma_sems.add(inc)
        self.q[eng].append((meth, args, kw, ws, ih))

    def counts(self):
        return {k: v[1] for k, v in self.sems.items()}

    def emit(self, blk):
        for eng in self.ENGINES:
            lst = self.q[eng]
            if not lst:
                continue

            def runner(e, lst=lst):
                for meth, args, kw, ws, ih in lst:
                    for h, v in ws:
                        if v > 0:
                            e.wait_ge(h, v)
                    ins = getattr(e, meth)(*args, **kw)
                    if ih is not None:
                        ins.then_inc(*ih)
            getattr(blk, eng)(runner)
        self.q = {e: [] for e in self.ENGINES}


def build_program(dbg=(), last_phase=6):
    _dbgc = [0]
    nc = bass.Bass("TRN2", target_bir_lowering=False, debug=False)

    xf = nc.dram_tensor("xf", [SEQ, D], F32, kind="ExternalInput").ap()
    cq = nc.dram_tensor("cq", [D, D], F32, kind="ExternalInput").ap()
    ck = nc.dram_tensor("ck", [D, D], F32, kind="ExternalInput").ap()
    cv = nc.dram_tensor("cv", [D, D], F32, kind="ExternalInput").ap()
    co = nc.dram_tensor("co", [D, D], F32, kind="ExternalInput").ap()
    bT = nc.dram_tensor("bT", [128, 4 * NDT], F32, kind="ExternalInput").ap()
    gwT = nc.dram_tensor("gwT", [D, E], F32, kind="ExternalInput").ap()
    wg = nc.dram_tensor("wg", [E, D, F], F32, kind="ExternalInput").ap()
    wu = nc.dram_tensor("wu", [E, D, F], F32, kind="ExternalInput").ap()
    wd = nc.dram_tensor("wd", [E, F, D], F32, kind="ExternalInput").ap()
    swg = nc.dram_tensor("swg", [D, FS], F32, kind="ExternalInput").ap()
    swu = nc.dram_tensor("swu", [D, FS], F32, kind="ExternalInput").ap()
    swd = nc.dram_tensor("swd", [FS, D], F32, kind="ExternalInput").ap()
    out = nc.dram_tensor("out", [OWN, D], F32, kind="ExternalOutput").ap()
    x1s = nc.dram_tensor("x1s", [NTO, 128, D], F32, kind="Internal").ap()

    dbga = {}
    for name, shape, dt in [
        ("xnT", [128, NDT, SEQ], BF16),
        ("qt", [128, OWN], BF16), ("kt", [128, SEQ], BF16),
        ("vt", [128, 32 * 130], BF16),
        ("outT", [128, NPAIR, OWN], BF16),
        ("xn2T", [128, NDT, OWN], BF16),
        ("comb", [128, NTO * E], F32),
        ("yacc", [128, NDT, OWN], F32),
    ]:
        if name in dbg:
            dbga[name] = nc.dram_tensor("dbg_" + name, shape, dt,
                                        kind="ExternalOutput").ap()

    with ExitStack() as ctx:
        En = ctx.enter_context
        S = Sched(nc, ctx)

        consts = En(nc.sbuf_tensor("consts", [128, 8], F32))
        C_EPS = consts[:, 0:1]
        C_NHALF = consts[:, 1:2]
        C_EIGHTH = consts[:, 3:4]
        C_M12 = consts[:, 4:5]
        identB = En(nc.sbuf_tensor("identB", [128, 128], BF16))
        shift64 = En(nc.sbuf_tensor("shift64", [128, 128], BF16))
        onescolF = En(nc.sbuf_tensor("onescolF", [128, 128], F32))
        selmat = En(nc.sbuf_tensor("selmat", [8, E, 128], BF16))
        biasT = En(nc.sbuf_tensor("biasT", [128, 4 * NDT], F32))
        gw_sb = En(nc.sbuf_tensor("gw_sb", [128, NDT, E], BF16))
        m_sum = En(nc.sbuf_tensor("m_sum", [128, NTF], F32))
        m_ssq = En(nc.sbuf_tensor("m_ssq", [128, NTF], F32))
        m_mean = En(nc.sbuf_tensor("m_mean", [128, NTF], F32))
        m_rstd = En(nc.sbuf_tensor("m_rstd", [128, NTF], F32))
        m_negms = En(nc.sbuf_tensor("m_negms", [128, NTF], F32))
        cT_sb = En(nc.sbuf_tensor("cT_sb", [8, OWN], BF16))
        xn2T = En(nc.sbuf_tensor("xn2T", [128, NDT, OWN], BF16))

        S.op("gpsimd", "memset", consts[:, 0:1], EPS)
        S.op("gpsimd", "memset", consts[:, 1:2], -0.5)
        S.op("gpsimd", "memset", consts[:, 2:3], 1.0 / D)
        S.op("gpsimd", "memset", consts[:, 3:4], 0.125)
        S.op("gpsimd", "memset", consts[:, 4:5], -12.0)
        S.op("gpsimd", "memset", identB[:], 0.0, inc="gi")
        S.op("gpsimd", "affine_select", out=identB[:], in_=identB[:],
             compare_op=ALU.not_equal, fill=1.0, base=0,
             pattern=[[-1, 128]], channel_multiplier=1,
             waits=[("gi", None)], inc="gi")
        S.op("gpsimd", "memset", shift64[:], 0.0, inc="gi")
        S.op("gpsimd", "affine_select", out=shift64[:], in_=shift64[:],
             compare_op=ALU.not_equal, fill=1.0, base=-64,
             pattern=[[1, 128]], channel_multiplier=-1,
             waits=[("gi", None)], inc="gi")
        S.op("gpsimd", "memset", onescolF[:], 1.0)
        S.op("gpsimd", "memset", selmat[:], 0.0, inc="gi")
        S.op("gpsimd", "affine_select", out=selmat[:], in_=selmat[:],
             compare_op=ALU.not_equal, fill=1.0, base=0,
             pattern=[[-1, E], [0, 128]], channel_multiplier=1,
             waits=[("gi", None)], inc="gc")
        S.op("gpsimd", "dma_start", out=biasT[:], in_=bT, inc="g0a", inck=16)
        S.op("gpsimd", "dma_start", out=gw_sb[:],
             in_=gwT.rearrange("(t p) e -> p t e", p=128), inc="g0b", inck=16)
        GC0 = S.n("gc")

        outT_stk = ExitStack()
        outT = outT_stk.enter_context(
            nc.sbuf_tensor("outT", [128, NPAIR, OWN], BF16))
        xnT_stk = ExitStack()
        xnT = xnT_stk.enter_context(
            nc.sbuf_tensor("xnT", [128, NDT, SEQ], BF16))
        open_stks = [outT_stk, xnT_stk]

        def finish():
            dws = [(d, None) for d in sorted(S.dma_sems)]
            S.op("sync", "nop", waits=dws)
            S.op("gpsimd", "nop", waits=dws)
            for st in reversed(open_stks):
                st.close()
            b = En(nc.Block())
            S.emit(b)
            return nc

        xr = xf.rearrange("(t p) d -> t p d", p=128)

        with ExitStack() as c1:
            cc = c1.enter_context
            xst = cc(nc.sbuf_tensor("xst", [128, 8, D], F32))
            xnst = cc(nc.sbuf_tensor("xnst", [128, 2, D], BF16))
            sq_scr = cc(nc.sbuf_tensor("sq_scr", [128, D], F32))
            psT = [cc(nc.psum_tensor(f"psT{i}", [128, D], BF16)) for i in range(2)]

            for j in range(NTF // 4):
                ws = []
                if j >= 2:
                    ws = [("aA", 4 * j - 4), ("vA", 4 * j - 4)]
                S.op("sync", "dma_start",
                     out=xst[:, (4 * j) % 8:(4 * j) % 8 + 4, :],
                     in_=xr[4 * j:4 * j + 4].rearrange("t p d -> p t d"),
                     waits=ws, inc=f"xldA{j % 2}", inck=16)
            for t in range(NTF):
                j = t // 4
                wx = (f"xldA{j % 2}", 16 * (j // 2 + 1))
                S.op("vector", "reduce_sum", m_sum[:, t:t + 1], xst[:, t % 8, :],
                     axis=AX.X, waits=[wx], inc="vA")
                S.op("scalar", "activation", sq_scr[:], xst[:, t % 8, :], AF.Square,
                     accum_out=m_ssq[:, t:t + 1],
                     waits=[wx, ("gc", GC0), ("aA", t)],
                     inc="aA")
            S.op("vector", "tensor_scalar_mul", m_mean[:], m_sum[:], 1.0 / D,
                 waits=[("vA", None)], inc="vS")
            S.op("vector", "tensor_mul", m_negms[:], m_mean[:], m_mean[:],
                 waits=[("vS", None)], inc="vS")
            S.op("vector", "scalar_tensor_tensor", m_rstd[:], m_ssq[:], 1.0 / D,
                 m_negms[:], ALU.mult, ALU.subtract,
                 waits=[("aA", None), ("vS", None)], inc="vS")
            S.op("scalar", "activation", m_ssq[:], m_rstd[:], AF.Ln, bias=C_EPS,
                 waits=[("vS", None)], inc="aS")
            S.op("scalar", "activation", m_rstd[:], m_ssq[:], AF.Exp, scale=C_NHALF,
                 waits=[("aS", None)], inc="aS")
            S.op("vector", "scalar_tensor_tensor", m_negms[:], m_mean[:], -1.0,
                 m_rstd[:], ALU.mult, ALU.mult, waits=[("aS", None)], inc="vS")
            STATS1 = S.n("vS")

            for j in range(NTF // 4):
                ws = [("vA", None), ("aA", None)]
                if j >= 2:
                    ws.append(("vN", 4 * j - 4))
                S.op("sync", "dma_start",
                     out=xst[:, (4 * j) % 8:(4 * j) % 8 + 4, :],
                     in_=xr[4 * j:4 * j + 4].rearrange("t p d -> p t d"),
                     waits=ws, inc=f"xldB{j % 2}", inck=16)
            for t in range(NTF):
                jb = t // 4
                ws = [(f"xldB{jb % 2}", 16 * (jb // 2 + 1)), ("vS", STATS1)]
                if t >= 2:
                    ws.append(("pTr", t - 1))
                S.op("vector", "tensor_scalar", xnst[:, t % 2, :], xst[:, t % 8, :],
                     m_rstd[:, t:t + 1], m_negms[:, t:t + 1], ALU.mult, ALU.add,
                     waits=ws, inc="vN")
                for dt in range(NDT):
                    ws = [("vN", S.n("vN")), ("gc", GC0)]
                    if t >= 2:
                        ws.append(("vE", t - 1))
                    S.op("tensor", "transpose",
                         psT[t % 2][:, dt * 128:(dt + 1) * 128],
                         xnst[:, t % 2, dt * 128:(dt + 1) * 128], identB[:],
                         waits=ws if dt == 0 else (),
                         inc="pTr" if dt == NDT - 1 else None)
                S.op("vector", "tensor_copy",
                     xnT[:, :, t * 128:(t + 1) * 128],
                     psT[t % 2][:].rearrange("p (dt c) -> p dt c", c=128),
                     waits=[("pTr", t + 1)], inc="vE")
        XNT = S.n("vE")

        if "xnT" in dbga:
            S.op("gpsimd", "dma_start", out=dbga["xnT"], in_=xnT[:],
                 waits=[("vE", XNT)], inc=f"dbg0", inck=16)

        if last_phase < 2:
            return finish()

        with ExitStack() as c2:
            cc = c2.enter_context
            cqp = [cc(nc.sbuf_tensor(f"cqp{i}", [128, NDT, 128], BF16)) for i in range(2)]
            ckp = [cc(nc.sbuf_tensor(f"ckp{i}", [128, NDT, 128], BF16)) for i in range(2)]
            cvp = [cc(nc.sbuf_tensor(f"cvp{i}", [128, NDT, 128], BF16)) for i in range(2)]
            QTp = [cc(nc.sbuf_tensor(f"QTp{i}", [128, OWN], BF16)) for i in range(2)]
            KTp = [cc(nc.sbuf_tensor(f"KTp{i}", [128, SEQ], BF16)) for i in range(2)]
            Vp = [cc(nc.sbuf_tensor(f"Vp{i}", [128, 32, 130], BF16)) for i in range(2)]
            PT = cc(nc.sbuf_tensor("PT", [128, 3, 1024], BF16))
            lscr = cc(nc.sbuf_tensor("lscr", [128, 2, 512], F32))
            invl = cc(nc.sbuf_tensor("invl", [128, 2, 512], F32))
            stg = cc(nc.sbuf_tensor("stg", [128, 2, 512], BF16))
            psS = [cc(nc.psum_tensor(f"psS{i}", [128, 1024], F32)) for i in range(2)]
            psO = cc(nc.psum_tensor("psO", [128, 1024], F32))
            psP = [cc(nc.psum_tensor(f"psP{i}", [128, 512], F32)) for i in range(2)]

            for i in range(2):
                S.op("gpsimd", "memset", Vp[i][:], 1.0,
                     waits=[("vE", XNT)], inc="gc")
            GV = S.n("gc")

            groups = [(2 * g, 2) for g in range(16)]
            UPC = 32
            pair_end = {}
            proj_pP = {}
            proj_done = {}
            wpair_lv = {}

            def emit_wdma(p):
                bp_ = p % 2
                colsl_ = slice(128 * p, 128 * (p + 1))
                ws = [("g0a", None), ("g0b", None), ("gc", GV), ("vE", XNT)]
                if p >= 2:
                    ws.append(("pP", proj_pP[p - 2]))
                for cn, cmat, dst in (("q", cq, cqp[bp_]), ("k", ck, ckp[bp_]),
                                      ("v", cv, cvp[bp_])):
                    S.op("gpsimd", "dma_start", out=dst[:],
                         in_=cmat.rearrange("(t p) n -> p t n", p=128)[:, :, colsl_],
                         waits=ws, inc=f"wld{cn}{bp_}", inck=16)
                    ws = []
                wpair_lv[p] = [(f"wld{cn}{bp_}", 16 * (p // 2 + 1))
                               for cn in ("q", "k", "v")]

            def make_jobs(p):
                return [(p, "q", c) for c in range(NCH)] + \
                       [(p, "k", kc) for kc in range(NKC)] + \
                       [(p, "v", kt) for kt in range(32)]

            class ProjQ:

                def __init__(self, p, gates):
                    self.p = p
                    self.jobs = make_jobs(p)
                    self.j = 0
                    self.vp0 = S.n("vP")
                    self.gates = gates

                def emit_one(self):
                    p, kind, idx = self.jobs[self.j]
                    bp_ = p % 2
                    pj = psP[self.j % 2][:]
                    ws = list(wpair_lv[p])
                    if self.j >= 2:
                        ws.append(("vP", self.vp0 + self.j - 1))
                    ev_ws = [("pP", None)] + self.gates
                    if kind == "q":
                        for dt in range(NDT):
                            S.op("tensor", "matmul", pj,
                                 cqp[bp_][:, dt, :],
                                 xnT[:, dt, idx * 512:(idx + 1) * 512],
                                 start=(dt == 0), stop=(dt == NDT - 1),
                                 waits=ws if dt == 0 else (),
                                 inc="pP" if dt == NDT - 1 else None)
                        S.op("vector", "tensor_scalar",
                             QTp[bp_][:, idx * 512:(idx + 1) * 512], pj,
                             biasT[:, p:p + 1], None, ALU.add,
                             waits=ev_ws, inc="vP")
                    elif kind == "k":
                        for dt in range(NDT):
                            S.op("tensor", "matmul", pj,
                                 ckp[bp_][:, dt, :],
                                 xnT[:, dt, idx * 512:(idx + 1) * 512],
                                 start=(dt == 0), stop=(dt == NDT - 1),
                                 waits=ws if dt == 0 else (),
                                 inc="pP" if dt == NDT - 1 else None)
                        S.op("vector", "tensor_scalar",
                             KTp[bp_][:, idx * 512:(idx + 1) * 512], pj,
                             biasT[:, NDT + p:NDT + p + 1], None, ALU.add,
                             waits=ev_ws, inc="vP")
                    else:
                        pv = pj[:, 0:128]
                        for dt in range(NDT):
                            S.op("tensor", "matmul", pv,
                                 xnT[:, dt, idx * 128:(idx + 1) * 128],
                                 cvp[bp_][:, dt, :],
                                 start=(dt == 0), stop=(dt == NDT - 1),
                                 waits=ws if dt == 0 else (),
                                 inc="pP" if dt == NDT - 1 else None)
                        dstv = Vp[bp_][:, idx, :].rearrange(
                            "p (a b) -> p a b", b=65)[:, :, 0:64]
                        srcv = pv.rearrange("p (a b) -> p a b", b=64)
                        S.op("vector", "tensor_copy", dstv, srcv,
                             waits=ev_ws, inc="vP")
                    self.j += 1

                def drain_to(self, frac):
                    tgt = min(len(self.jobs), int(len(self.jobs) * frac + 0.999))
                    while self.j < tgt:
                        self.emit_one()

                def finish(self):
                    while self.j < len(self.jobs):
                        self.emit_one()
                    proj_pP[self.p] = S.n("pP")
                    proj_done[self.p] = S.n("vP")

            emit_wdma(0)
            if NPAIR > 1:
                emit_wdma(1)
            q0 = ProjQ(0, [])
            q0.finish()

            for p in range(NPAIR):
                bp = p % 2
                if p + 2 < NPAIR:
                    emit_wdma(p + 2)
                if p + 1 < NPAIR:
                    nq = ProjQ(p + 1, [("pQK", pair_end[p - 1]["pQK"]),
                                       ("pPV", pair_end[p - 1]["pPV"])]
                               if p > 0 else [])
                else:
                    nq = None

                exp_at = {}
                pv_at = {}
                recip_at = {}
                evac_at = {}
                pvb_free = {}
                pend = []

                def emit_pv(job):
                    u, cc_, pts, h = job
                    ws = [("aE", exp_at[u])]
                    if u % UPC <= 1:
                        if cc_ > 0:
                            gate = evac_at[cc_ - 1] if h == 0 else pvb_free[cc_ - 1]
                        else:
                            gate = S.n("vT")
                        ws.append(("vT", gate))
                    vsl = slice(65 * h, 65 * h + 65)
                    k0_, glen_ = groups[(u % UPC) // 2]
                    for i in range(glen_):
                        kt = k0_ + i
                        S.op("tensor", "matmul",
                             psO[0:65, 512 * h:512 * h + 512],
                             Vp[bp][:, kt, vsl],
                             PT[:, pts, i * 512:(i + 1) * 512],
                             start=(kt == 0), stop=(kt == 31),
                             waits=ws if i == 0 else (),
                             inc="pPV" if i == glen_ - 1 else None)
                    pv_at[u] = S.n("pPV")

                for c in range(NCH):
                    qsl = slice(c * 512, (c + 1) * 512)
                    for gi, (k0, glen) in enumerate(groups):
                        ua = c * UPC + 2 * gi
                        ub = ua + 1
                        wsa = []
                        wsb = []
                        if ua >= 2:
                            wsa.append(("aE", exp_at[ua - 2]))
                            wsb.append(("aE", exp_at[ub - 2]))
                        else:
                            wsa = [("vT", S.n("vT")), ("vP", proj_done[p])]
                            wsb = list(wsa)
                        if c > 0 and ua == c * UPC:
                            wsb.append(("vT", recip_at[c - 1]))
                        for h, slot, ws0 in ((0, 0, wsa), (1, 1, wsb)):
                            rowsl = slice(64 * h, 64 * (h + 1))
                            for i in range(glen):
                                kt = k0 + i
                                S.op("tensor", "matmul",
                                     psS[slot][:, i * 512:(i + 1) * 512],
                                     KTp[bp][rowsl, kt * 128:(kt + 1) * 128],
                                     QTp[bp][rowsl, qsl],
                                     start=True, stop=True,
                                     tile_position=(64 * h, 0),
                                     waits=ws0 if i == 0 else (),
                                     inc="pQK" if i == glen - 1 else None)
                        nqk_b = S.n("pQK")
                        while pend:
                            emit_pv(pend.pop(0))
                        for h, u in ((0, ua), (1, ub)):
                            ws = [("pQK", nqk_b - (1 if h == 0 else 0))]
                            if u >= 3:
                                ws.append(("pPV", pv_at[u - 3]))
                            S.op("scalar", "activation",
                                 PT[:, u % 3, 0:glen * 512],
                                 psS[u % 2][:, 0:glen * 512],
                                 AF.Exp, bias=C_M12, scale=C_EIGHTH,
                                 waits=ws, inc="aE")
                            exp_at[u] = S.n("aE")
                        pend.append((ua, c, ua % 3, 0))
                        pend.append((ub, c, ub % 3, 1))
                        if nq is not None:
                            nq.drain_to((2 * gi + 2) / UPC / NCH + c / NCH)
                    while pend:
                        emit_pv(pend.pop(0))
                    S.op("vector", "tensor_copy", lscr[64:65, 0, :],
                         psO[64:65, 0:512],
                         waits=[("pPV", None), ("pT", None)], inc="vT")
                    S.op("vector", "tensor_copy", lscr[64:65, 1, :],
                         psO[64:65, 512:1024], inc="vT")
                    S.op("tensor", "matmul", psS[1][0:64, 0:512],
                         onescolF[64:65, 0:64], lscr[64:65, 0, :],
                         start=True, stop=True,
                         waits=[("vT", None), ("aE", None)], inc="pT")
                    S.op("tensor", "matmul", psS[1][0:64, 512:1024],
                         onescolF[64:65, 0:64], lscr[64:65, 1, :],
                         start=True, stop=True, inc="pT")
                    S.op("vector", "reciprocal",
                         invl[0:64, :, :].rearrange("p a b -> p (a b)"),
                         psS[1][0:64, 0:1024],
                         waits=[("pT", None)], inc="vT")
                    recip_at[c] = S.n("vT")
                    S.op("vector", "tensor_tensor", stg[0:64, 0, :],
                         psO[0:64, 0:512], invl[0:64, 0, :], ALU.mult,
                         waits=[("vT", None)], inc="vT")
                    S.op("vector", "tensor_scalar", outT[0:64, p, qsl],
                         stg[0:64, 0, :],
                         biasT[0:64, 2 * NDT + p:2 * NDT + p + 1], None, ALU.add,
                         waits=[("vT", None)], inc="vT")
                    AEV = S.n("vT")
                    S.op("vector", "tensor_tensor", stg[0:64, 1, :],
                         psO[0:64, 512:1024], invl[0:64, 1, :], ALU.mult,
                         inc="vT")
                    pvb_free[c] = S.n("vT")
                    S.op("tensor", "matmul", psO[:, 0:512], shift64[0:64, 0:128],
                         stg[0:64, 1, :], start=True, stop=True,
                         waits=[("vT", None)], inc="pT")
                    S.op("vector", "tensor_scalar", outT[64:128, p, qsl],
                         psO[64:128, 0:512],
                         biasT[64:128, 2 * NDT + p:2 * NDT + p + 1], None,
                         ALU.add, waits=[("pT", None)], inc="vT")
                    evac_at[c] = S.n("vT")
                if nq is not None:
                    nq.finish()
                pair_end[p] = {"pQK": S.n("pQK"), "pPV": S.n("pPV"),
                               "vT": S.n("vT")}

                if p == 0:
                    if "qt" in dbga:
                        S.op("gpsimd", "dma_start", out=dbga["qt"], in_=QTp[0][:],
                             waits=[("vP", proj_done[0])], inc="dbg90", inck=16)
                    if "kt" in dbga:
                        S.op("gpsimd", "dma_start", out=dbga["kt"], in_=KTp[0][:],
                             waits=[("vP", proj_done[0])], inc="dbg91", inck=16)
                    if "vt" in dbga:
                        S.op("gpsimd", "dma_start", out=dbga["vt"],
                             in_=Vp[0][:].rearrange("p a b -> p (a b)"),
                             waits=[("vP", proj_done[0])], inc="dbg92", inck=16)
            ATT = S.counts()

        if "outT" in dbga:
            S.op("gpsimd", "dma_start", out=dbga["outT"], in_=outT[:],
                 waits=[("vT", ATT["vT"])], inc=f"dbg4", inck=16)

        xnT_stk.close()
        open_stks.remove(xnT_stk)

        if last_phase < 3:
            return finish()

        with ExitStack() as c3:
            cc = c3.enter_context
            co_sb = cc(nc.sbuf_tensor("co_sb", [128, NDT, D], BF16))
            aost = cc(nc.sbuf_tensor("aost", [128, NDT, 512], BF16))
            xo = cc(nc.sbuf_tensor("xo", [128, 8, D], F32))
            x1t = cc(nc.sbuf_tensor("x1t", [128, NTO, D], F32))
            xn2st = cc(nc.sbuf_tensor("xn2st", [128, 2, D], BF16))
            sq2 = cc(nc.sbuf_tensor("sq2", [128, D], F32))
            psC = [cc(nc.psum_tensor(f"psC{i}", [128, 512], F32)) for i in range(2)]
            psT3 = [cc(nc.psum_tensor(f"psT3{i}", [128, D], BF16)) for i in range(2)]

            S.op("gpsimd", "dma_start", out=co_sb[:],
                 in_=co.rearrange("(t p) n -> p t n", p=128),
                 waits=[("vT", ATT["vT"]), ("g0a", None), ("g0b", None)],
                 inc="wldc", inck=16)
            def load_xo(j):
                ws = [("vT", ATT["vT"])] if j <= 1 else [("vAdd", 4 * (j - 1))]
                S.op("sync", "dma_start",
                     out=xo[:, (4 * j) % 8:(4 * j) % 8 + 4, :],
                     in_=xr[4 * j:4 * j + 4].rearrange("t p d -> p t d"),
                     waits=ws, inc=f"xoL{j % 2}", inck=16)

            load_xo(0)
            load_xo(1)

            for c in range(NCH):
                qsl = slice(c * 512, (c + 1) * 512)
                for dout in range(NDT):
                    pj = psC[dout % 2][:]
                    ws = [("wldc", 16), ("vT", ATT["vT"])]
                    if c * NDT + dout >= 2:
                        ws.append(("aCo", S.n("aCo") - 1))
                    for dt in range(NDT):
                        S.op("tensor", "matmul", pj,
                             co_sb[:, dt, dout * 128:(dout + 1) * 128],
                             outT[:, dt, qsl],
                             start=(dt == 0), stop=(dt == NDT - 1),
                             waits=ws if dt == 0 else (),
                             inc="pCo" if dt == NDT - 1 else None)
                    S.op("scalar", "activation", aost[:, dout, :], pj, AF.Identity,
                         bias=biasT[:, 3 * NDT + dout:3 * NDT + dout + 1],
                         waits=[("pCo", None), ("pT3", None)], inc="aCo")
                ACO = S.n("aCo")
                for tt in range(4):
                    T = c * 4 + tt
                    for dt in range(NDT):
                        ws = [("aCo", ACO), ("vAdd", max(0, T - 1))]
                        S.op("tensor", "transpose",
                             psT3[T % 2][:, dt * 128:(dt + 1) * 128],
                             aost[:, dt, tt * 128:(tt + 1) * 128], identB[:],
                             waits=ws if dt == 0 else (),
                             inc="pT3" if dt == NDT - 1 else None)
                    S.op("vector", "tensor_tensor", x1t[:, T, :], xo[:, T % 8, :],
                         psT3[T % 2][:], ALU.add,
                         waits=[("pT3", None),
                                (f"xoL{(T // 4) % 2}", 16 * (T // 4 // 2 + 1))],
                         inc="vAdd")
                    S.op("sync", "dma_start", out=x1s[T], in_=x1t[:, T, :],
                         waits=[("vAdd", None)], inc=f"stX{T % 2}", inck=16)
                    S.op("vector", "reduce_sum", m_sum[:, T:T + 1], x1t[:, T, :],
                         axis=AX.X, waits=[("vAdd", None)], inc="vL")
                    S.op("scalar", "activation", sq2[:], x1t[:, T, :], AF.Square,
                         accum_out=m_ssq[:, T:T + 1],
                         waits=[("vAdd", None), ("aL", None)], inc="aL")
                if c + 2 < NCH:
                    load_xo(c + 2)
            s16 = slice(0, NTO)
            S.op("vector", "tensor_scalar_mul", m_mean[:, s16], m_sum[:, s16],
                 1.0 / D, waits=[("vL", None)], inc="vS2")
            S.op("vector", "tensor_mul", m_negms[:, s16], m_mean[:, s16],
                 m_mean[:, s16], waits=[("vS2", None)], inc="vS2")
            S.op("vector", "scalar_tensor_tensor", m_rstd[:, s16], m_ssq[:, s16],
                 1.0 / D, m_negms[:, s16], ALU.mult, ALU.subtract,
                 waits=[("aL", None), ("vS2", None)], inc="vS2")
            S.op("scalar", "activation", m_ssq[:, s16], m_rstd[:, s16], AF.Ln,
                 bias=C_EPS, waits=[("vS2", None)], inc="aS2")
            S.op("scalar", "activation", m_rstd[:, s16], m_ssq[:, s16], AF.Exp,
                 scale=C_NHALF, waits=[("aS2", None)], inc="aS2")
            S.op("vector", "scalar_tensor_tensor", m_negms[:, s16], m_mean[:, s16],
                 -1.0, m_rstd[:, s16], ALU.mult, ALU.mult,
                 waits=[("aS2", None)], inc="vS2")
            STATS2 = S.n("vS2")
            for T in range(NTO):
                ws = [("vS2", STATS2)]
                if T >= 2:
                    ws.append(("pT4", T - 1))
                S.op("vector", "tensor_scalar", xn2st[:, T % 2, :], x1t[:, T, :],
                     m_rstd[:, T:T + 1], m_negms[:, T:T + 1], ALU.mult, ALU.add,
                     waits=ws, inc="vN2")
                for dt in range(NDT):
                    ws = [("vN2", S.n("vN2"))]
                    if T >= 2:
                        ws.append(("vE2", T - 1))
                    S.op("tensor", "transpose",
                         psT3[T % 2][:, dt * 128:(dt + 1) * 128],
                         xn2st[:, T % 2, dt * 128:(dt + 1) * 128], identB[:],
                         waits=ws if dt == 0 else (),
                         inc="pT4" if dt == NDT - 1 else None)
                S.op("vector", "tensor_copy",
                     xn2T[:, :, T * 128:(T + 1) * 128],
                     psT3[T % 2][:].rearrange("p (dt c2) -> p dt c2", c2=128),
                     waits=[("pT4", T + 1)], inc="vE2")
            P3 = S.counts()

        if "xn2T" in dbga:
            S.op("gpsimd", "dma_start", out=dbga["xn2T"], in_=xn2T[:],
                 waits=[("vE2", P3["vE2"])], inc=f"dbg5", inck=16)

        outT_stk.close()
        open_stks.remove(outT_stk)

        if last_phase < 4:
            return finish()

        with ExitStack() as c4:
            cc = c4.enter_context
            identF = cc(nc.sbuf_tensor("identF", [128, 128], F32))
            lg_sb = cc(nc.sbuf_tensor("lg_sb", [8, NCH, 512], F32))
            lgt = cc(nc.sbuf_tensor("lgt", [128, NTO, E], F32))
            rt_is1 = cc(nc.sbuf_tensor("rt_is1", [128, NTO, E], F32))
            rt_is2 = cc(nc.sbuf_tensor("rt_is2", [128, NTO, E], F32))
            rt_sh = cc(nc.sbuf_tensor("rt_sh", [128, NTO, E], F32))
            rt_m1 = cc(nc.sbuf_tensor("rt_m1", [128, NTO], F32))
            rt_m2 = cc(nc.sbuf_tensor("rt_m2", [128, NTO], F32))
            rt_z = cc(nc.sbuf_tensor("rt_z", [128, NTO], F32))
            rt_e2 = cc(nc.sbuf_tensor("rt_e2", [128, NTO], F32))
            rt_w1 = cc(nc.sbuf_tensor("rt_w1", [128, NTO], F32))
            rt_w2 = cc(nc.sbuf_tensor("rt_w2", [128, NTO], F32))
            comb_bf = cc(nc.sbuf_tensor("comb_bf", [128, NTO, E], BF16))
            psG = cc(nc.psum_tensor("psG", [8, NCH * 512], F32))
            psR = cc(nc.psum_tensor("psR", [128, NTO * 8], F32))
            psR2 = cc(nc.psum_tensor("psR2", [8, NTO * 128], BF16))

            S.op("gpsimd", "memset", identF[:], 0.0,
                 waits=[("vE2", P3["vE2"]), ("stX0", 16 * (NTO // 2)),
                        ("stX1", 16 * (NTO // 2))], inc="gF")
            S.op("gpsimd", "affine_select", out=identF[:], in_=identF[:],
                 compare_op=ALU.not_equal, fill=1.0, base=0,
                 pattern=[[-1, 128]], channel_multiplier=1,
                 waits=[("gF", None)], inc="gF")

            for ch in range(NCH):
                for dt in range(NDT):
                    S.op("tensor", "matmul", psG[:, ch * 512:(ch + 1) * 512],
                         gw_sb[:, dt, :], xn2T[:, dt, ch * 512:(ch + 1) * 512],
                         start=(dt == 0), stop=(dt == NDT - 1),
                         waits=[("vE2", P3["vE2"]), ("g0b", None)]
                         if (ch == 0 and dt == 0) else (),
                         inc="pG" if dt == NDT - 1 else None)
            S.op("vector", "tensor_copy",
                 lg_sb[:].rearrange("p a b -> p (a b)"), psG[:],
                 waits=[("pG", None)], inc="vR")
            lg2 = lg_sb[:].rearrange("p a b -> p (a b)")
            for T in range(NTO):
                S.op("tensor", "transpose", psR[:, T * 8:(T + 1) * 8],
                     lg2[0:8, T * 128:(T + 1) * 128], identF[0:8, 0:8],
                     waits=[("vR", None), ("gF", None)] if T == 0 else (),
                     inc="pR" if T == NTO - 1 else None)
            S.op("vector", "tensor_copy", lgt[:].rearrange("p a b -> p (a b)"),
                 psR[:], waits=[("pR", None)], inc="vR")
            for T in range(NTO):
                S.op("vector", "reduce_max", rt_m1[:, T:T + 1], lgt[:, T, :],
                     axis=AX.X, waits=[("vR", None)], inc="vR")
                S.op("vector", "tensor_scalar", rt_is1[:, T, :], lgt[:, T, :],
                     rt_m1[:, T:T + 1], None, ALU.is_ge,
                     waits=[("vR", None)], inc="vR")
                S.op("vector", "scalar_tensor_tensor", rt_sh[:, T, :],
                     rt_is1[:, T, :], -1e30, lgt[:, T, :], ALU.mult, ALU.add,
                     waits=[("vR", None)], inc="vR")
                S.op("vector", "reduce_max", rt_m2[:, T:T + 1], rt_sh[:, T, :],
                     axis=AX.X, waits=[("vR", None)], inc="vR")
                S.op("vector", "tensor_scalar", rt_is2[:, T, :], rt_sh[:, T, :],
                     rt_m2[:, T:T + 1], None, ALU.is_ge,
                     waits=[("vR", None)], inc="vR")
                S.op("vector", "tensor_scalar", rt_sh[:, T, :], lgt[:, T, :],
                     rt_m1[:, T:T + 1], None, ALU.subtract,
                     waits=[("vR", None)], inc="vR")
            S.op("scalar", "activation", rt_sh[:], rt_sh[:], AF.Exp,
                 waits=[("vR", None), ("aL", P3["aL"])], inc="aR")
            S.op("vector", "reduce_sum", rt_z[:], rt_sh[:], axis=AX.X,
                 waits=[("aR", None)], inc="vR")
            S.op("vector", "tensor_sub", rt_e2[:], rt_m2[:], rt_m1[:],
                 waits=[("vR", None)], inc="vR")
            S.op("scalar", "activation", rt_e2[:], rt_e2[:], AF.Exp,
                 waits=[("vR", None)], inc="aR")
            S.op("vector", "scalar_tensor_tensor", rt_w1[:], rt_z[:], 1e-20,
                 rt_e2[:], ALU.mult, ALU.add,
                 waits=[("aR", None), ("vR", None)], inc="vR")
            S.op("vector", "tensor_scalar", rt_w1[:], rt_w1[:], 1.0, None, ALU.add,
                 waits=[("vR", None)], inc="vR")
            S.op("vector", "reciprocal", rt_w1[:], rt_w1[:],
                 waits=[("vR", None)], inc="vR")
            S.op("vector", "tensor_scalar", rt_w1[:], rt_w1[:], 2.5, None, ALU.mult,
                 waits=[("vR", None)], inc="vR")
            S.op("vector", "tensor_mul", rt_w2[:], rt_e2[:], rt_w1[:],
                 waits=[("vR", None)], inc="vR")
            b1, b2 = bass.broadcast_tensor_aps(
                rt_is1[:], rt_w1[:].rearrange("p (t o) -> p t o", o=1))
            S.op("vector", "tensor_tensor", rt_is1[:], b1, b2, ALU.mult,
                 waits=[("vR", None)], inc="vR")
            b1, b2 = bass.broadcast_tensor_aps(
                rt_is2[:], rt_w2[:].rearrange("p (t o) -> p t o", o=1))
            S.op("vector", "tensor_tensor", rt_is2[:], b1, b2, ALU.mult,
                 waits=[("vR", None)], inc="vR")
            S.op("vector", "tensor_add", lgt[:], rt_is1[:], rt_is2[:],
                 waits=[("vR", None)], inc="vR")
            if "comb" in dbga:
                S.op("gpsimd", "dma_start", out=dbga["comb"],
                     in_=lgt[:].rearrange("p a b -> p (a b)"),
                     waits=[("vR", None)], inc=f"dbg6", inck=16)
            S.op("vector", "tensor_copy", comb_bf[:], lgt[:],
                 waits=[("vR", None)], inc="vR")
            for T in range(NTO):
                S.op("tensor", "transpose", psR2[:, T * 128:(T + 1) * 128],
                     comb_bf[:, T, :], identB[:],
                     waits=[("vR", None)] if T == 0 else (),
                     inc="pR2" if T == NTO - 1 else None)
            S.op("vector", "tensor_copy", cT_sb[:], psR2[:],
                 waits=[("pR2", None)], inc="vR")
            P4 = S.counts()

        if last_phase < 5:
            return finish()

        yacc = En(nc.sbuf_tensor("yacc", [128, NDT, OWN], F32))
        with ExitStack() as c5:
            cc = c5.enter_context
            wgh = [cc(nc.sbuf_tensor(f"wgh{i}", [128, NDT, 768], BF16)) for i in range(2)]
            wuh = [cc(nc.sbuf_tensor(f"wuh{i}", [128, NDT, 768], BF16)) for i in range(2)]
            wdh = [cc(nc.sbuf_tensor(f"wdh{i}", [128, 6, D], BF16)) for i in range(2)]
            prod = cc(nc.sbuf_tensor("prod", [128, 6, OWN // 2], BF16))
            gs_sb = cc(nc.sbuf_tensor("gs_sb", [128, 2, 512], BF16))
            us_sb = cc(nc.sbuf_tensor("us_sb", [128, 2, 512], BF16))
            ce_sb = cc(nc.sbuf_tensor("ce_sb", [128, OWN], BF16))
            psGU = [cc(nc.psum_tensor(f"psGU{i}", [128, 512], F32)) for i in range(4)]
            psD = [cc(nc.psum_tensor(f"psD{i}", [128, 512], F32)) for i in range(2)]
            psCE = cc(nc.psum_tensor("psCE", [128, 512], F32))

            unit_end = {}
            jig = 0
            for ui, (kind, e, f0, nf) in enumerate(FFN_UNITS):
                bu = ui % 2
                ws = [("vT", ATT["vT"])]
                if ui >= 2:
                    ws.append(("pD", unit_end[ui - 2]["pD"]))
                    ws.append(("pGU", unit_end[ui - 2]["pGU"]))
                if kind == "moe":
                    srcg = wg[e].rearrange("(t p) f -> p t f", p=128)[:, :, f0 * 128:(f0 + nf) * 128]
                    srcu = wu[e].rearrange("(t p) f -> p t f", p=128)[:, :, f0 * 128:(f0 + nf) * 128]
                    srcd = wd[e, f0 * 128:(f0 + nf) * 128, :].rearrange("(t p) d2 -> p t d2", p=128)
                else:
                    srcg = swg.rearrange("(t p) f -> p t f", p=128)[:, :, f0 * 128:(f0 + nf) * 128]
                    srcu = swu.rearrange("(t p) f -> p t f", p=128)[:, :, f0 * 128:(f0 + nf) * 128]
                    srcd = swd[f0 * 128:(f0 + nf) * 128, :].rearrange("(t p) d2 -> p t d2", p=128)
                S.op("gpsimd", "dma_start", out=wgh[bu][:, :, 0:nf * 128], in_=srcg,
                     waits=ws, inc=f"wfg{bu}", inck=16)
                S.op("gpsimd", "dma_start", out=wuh[bu][:, :, 0:nf * 128], in_=srcu,
                     inc=f"wfu{bu}", inck=16)
                S.op("gpsimd", "dma_start", out=wdh[bu][:, 0:nf, :], in_=srcd,
                     inc=f"wfd{bu}", inck=16)
                lvl = 16 * (ui // 2 + 1)
                WUg = (f"wfg{bu}", lvl)
                WUu = (f"wfu{bu}", lvl)
                WUd = (f"wfd{bu}", lvl)

                if kind == "moe" and f0 == 0:
                    for ch in range(NCH):
                        ws = [("vUs", None), ("vCE", None)]
                        if ui == 0 and ch == 0:
                            ws += [("vR", P4["vR"])]
                        S.op("tensor", "matmul", psCE[:],
                             selmat[:, e, :], cT_sb[:, ch * 512:(ch + 1) * 512],
                             start=True, stop=True, waits=ws, inc="pCE")
                        S.op("vector", "tensor_copy",
                             ce_sb[:, ch * 512:(ch + 1) * 512], psCE[:],
                             waits=[("pCE", None)], inc="vCE")
                CE = S.n("vCE")

                for th in range(2):
                    down_q = []
                    dji = [0]

                    def emit_down(job):
                        dout, chl_, lvl = job
                        ch_ = 2 * th + chl_
                        pd = psD[dji[0] % 2][:]
                        ws = [("vPr", lvl), WUd]
                        if ui == 0 and th == 0 and dji[0] < 2:
                            ws.append(("vR", P4["vR"]))
                        if ui > 0 or th > 0 or dji[0] >= 2:
                            ws.append(("vF2", S.n("vF2") - 1))
                        for ft_ in range(nf):
                            S.op("tensor", "matmul", pd,
                                 wdh[bu][:, ft_, dout * 128:(dout + 1) * 128],
                                 prod[:, ft_, chl_ * 512:(chl_ + 1) * 512],
                                 start=(ft_ == 0), stop=(ft_ == nf - 1),
                                 waits=ws if ft_ == 0 else (),
                                 inc="pD" if ft_ == nf - 1 else None)
                        if ui == 0:
                            S.op("vector", "tensor_copy",
                                 yacc[:, dout, ch_ * 512:(ch_ + 1) * 512], pd,
                                 waits=[("pD", None)], inc="vF2")
                        else:
                            S.op("vector", "tensor_tensor",
                                 yacc[:, dout, ch_ * 512:(ch_ + 1) * 512],
                                 yacc[:, dout, ch_ * 512:(ch_ + 1) * 512], pd,
                                 ALU.add, waits=[("pD", None), ("vF2", None)],
                                 inc="vF2")
                        dji[0] += 1

                    for chl in range(2):
                        ch = 2 * th + chl
                        for ft in range(nf):
                            pg = psGU[(2 * jig) % 4][:]
                            pu = psGU[(2 * jig + 1) % 4][:]
                            ws = [WUg, ("vE2", P3["vE2"])]
                            if ui == 0 and th == 0 and ft == 0 and chl == 0:
                                ws.append(("vR", P4["vR"]))
                            if jig >= 2:
                                ws.append(("aSil", jig - 1))
                            for dt in range(NDT):
                                S.op("tensor", "matmul", pg,
                                     wgh[bu][:, dt, ft * 128:(ft + 1) * 128],
                                     xn2T[:, dt, ch * 512:(ch + 1) * 512],
                                     start=(dt == 0), stop=(dt == NDT - 1),
                                     waits=ws if dt == 0 else (),
                                     inc="pGU" if dt == NDT - 1 else None)
                            ws2 = [WUu]
                            if jig >= 2:
                                ws2.append(("vUs", jig - 1))
                            for dt in range(NDT):
                                S.op("tensor", "matmul", pu,
                                     wuh[bu][:, dt, ft * 128:(ft + 1) * 128],
                                     xn2T[:, dt, ch * 512:(ch + 1) * 512],
                                     start=(dt == 0), stop=(dt == NDT - 1),
                                     waits=ws2 if dt == 0 else (),
                                     inc="pGU" if dt == NDT - 1 else None)
                            PGU = S.n("pGU")
                            S.op("scalar", "activation", gs_sb[:, jig % 2, :], pg,
                                 AF.Silu,
                                 waits=[("pGU", PGU - 1), ("vPr", max(0, jig - 1))],
                                 inc="aSil")
                            if kind == "moe":
                                S.op("vector", "tensor_tensor", us_sb[:, jig % 2, :],
                                     pu, ce_sb[:, ch * 512:(ch + 1) * 512], ALU.mult,
                                     waits=[("pGU", PGU), ("vCE", CE),
                                            ("vPr", max(0, S.n("vPr") - 1))],
                                     inc="vUs")
                            else:
                                S.op("vector", "tensor_copy", us_sb[:, jig % 2, :],
                                     pu,
                                     waits=[("pGU", PGU),
                                            ("vPr", max(0, S.n("vPr") - 1))],
                                     inc="vUs")
                            S.op("vector", "tensor_mul",
                                 prod[:, ft, chl * 512:(chl + 1) * 512],
                                 gs_sb[:, jig % 2, :], us_sb[:, jig % 2, :],
                                 waits=[("aSil", None), ("vUs", None), ("pD", None)],
                                 inc="vPr")
                            jig += 1
                            while down_q and len(down_q) > (nf - 1 - ft) * 2 - 1:
                                emit_down(down_q.pop(0))
                        lvl = S.n("vPr")
                        if chl == 0:
                            down_q = [(dout, 0, lvl) for dout in range(NDT)]
                        else:
                            while down_q:
                                emit_down(down_q.pop(0))
                            for dout in range(NDT):
                                emit_down((dout, 1, lvl))
                unit_end[ui] = {"pD": S.n("pD"), "pGU": S.n("pGU")}
            P5 = S.counts()

        if "yacc" in dbga:
            S.op("gpsimd", "dma_start", out=dbga["yacc"], in_=yacc[:],
                 waits=[("vF2", P5["vF2"])], inc=f"dbg7", inck=16)

        if last_phase < 6:
            return finish()

        with ExitStack() as c6:
            cc = c6.enter_context
            ybf = cc(nc.sbuf_tensor("ybf", [128, NDT, 512], BF16))
            x1r = cc(nc.sbuf_tensor("x1r", [128, 4, D], F32))
            obuf = cc(nc.sbuf_tensor("obuf", [128, 2, D], F32))
            psT6 = [cc(nc.psum_tensor(f"psT6{i}", [128, D], BF16)) for i in range(2)]

            outr = out.rearrange("(t p) d -> t p d", p=128)

            def load_x1(T, ws):
                S.op("sync", "dma_start", out=x1r[:, T % 4, :], in_=x1s[T],
                     waits=ws, inc=f"x1L{T % 4}", inck=16)

            for T in range(4):
                load_x1(T, [("stX0", 16 * (NTO // 2)), ("stX1", 16 * (NTO // 2)),
                            ("vF2", P5["vF2"]), ("aSil", P5["aSil"]),
                            ("vPr", P5["vPr"])])
            for c in range(NCH):
                S.op("vector", "tensor_copy", ybf[:],
                     yacc[:, :, c * 512:(c + 1) * 512],
                     waits=[("vF2", P5["vF2"]), ("vO", None)], inc="vY")
                for tt in range(4):
                    T = c * 4 + tt
                    for dt in range(NDT):
                        ws = [("vY", None), ("vF2", P5["vF2"])]
                        if T >= 2:
                            ws.append(("vO", T - 1))
                        S.op("tensor", "transpose",
                             psT6[T % 2][:, dt * 128:(dt + 1) * 128],
                             ybf[:, dt, tt * 128:(tt + 1) * 128], identB[:],
                             waits=ws if dt == 0 else (),
                             inc="pT6" if dt == NDT - 1 else None)
                    S.op("vector", "tensor_tensor", obuf[:, T % 2, :],
                         x1r[:, T % 4, :], psT6[T % 2][:], ALU.add,
                         waits=[("pT6", None), (f"x1L{T % 4}", 16 * (T // 4 + 1)),
                                (f"stO{T % 2}", 16 * (T // 2))], inc="vO")
                    S.op("sync", "dma_start", out=outr[T], in_=obuf[:, T % 2, :],
                         waits=[("vO", None)], inc=f"stO{T % 2}", inck=16)
                    if T + 4 < NTO:
                        load_x1(T + 4, [("vO", None)])
            fin = [(d, None) for d in sorted(S.dma_sems)]
            S.op("sync", "nop", waits=fin)
            S.op("gpsimd", "nop", waits=fin)

        blk = En(nc.Block())
        S.emit(blk)

    return nc


def _circulant(c):
    c = np.asarray(c, np.float32)
    idx = (np.arange(D)[None, :] - np.arange(D)[:, None]) % D
    return np.ascontiguousarray(c[idx])


def _biasT(b):
    return np.ascontiguousarray(np.asarray(b, np.float32).reshape(NDT, 128).T)


def make_inputs(inputs):
    x = np.ascontiguousarray(np.asarray(inputs["L0_attn_x"], np.float32))
    base = {
        "cq": _circulant(inputs["L0_attn_wq_c"]),
        "ck": _circulant(inputs["L0_attn_wk_c"]),
        "cv": _circulant(inputs["L0_attn_wv_c"]),
        "co": _circulant(inputs["L0_attn_wo_c"]),
        "bT": np.ascontiguousarray(np.concatenate(
            [_biasT(inputs["L0_attn_wq_b"]), _biasT(inputs["L0_attn_wk_b"]),
             _biasT(inputs["L0_attn_wv_b"]), _biasT(inputs["L0_attn_wo_b"])],
            axis=1)),
        "gwT": np.ascontiguousarray(
            np.asarray(inputs["L0_ffn_gate_w"], np.float32).T),
        "wg": np.ascontiguousarray(np.asarray(inputs["L0_ffn_w_gate"], np.float32)),
        "wu": np.ascontiguousarray(np.asarray(inputs["L0_ffn_w_up"], np.float32)),
        "wd": np.ascontiguousarray(np.asarray(inputs["L0_ffn_w_down"], np.float32)),
        "swg": np.ascontiguousarray(np.asarray(inputs["L0_ffn_sw_gate"], np.float32)),
        "swu": np.ascontiguousarray(np.asarray(inputs["L0_ffn_sw_up"], np.float32)),
        "swd": np.ascontiguousarray(np.asarray(inputs["L0_ffn_sw_down"], np.float32)),
    }
    in_maps = []
    for core in range(8):
        b, h = core // 2, core % 2
        xb = x[b]
        m = dict(base)
        m["xf"] = np.ascontiguousarray(
            np.concatenate([xb[h * OWN:(h + 1) * OWN],
                            xb[(1 - h) * OWN:(2 - h) * OWN]], axis=0))
        in_maps.append(m)
    return in_maps


def kernel(**inputs):
    nc = build_program()
    in_maps = make_inputs(inputs)
    res = run_bass_kernel_spmd(nc, in_maps, core_ids=list(range(8)))
    y = np.empty((4, SEQ, D), np.float32)
    for core in range(8):
        b, h = core // 2, core % 2
        y[b, h * OWN:(h + 1) * OWN] = res.results[core]["out"]
    return y


# version 11
# speedup vs baseline: 1.1002x; 1.0099x over previous
import numpy as np
import ml_dtypes
from contextlib import ExitStack

import concourse.bass as bass
import concourse.mybir as mybir
from concourse.bass_utils import run_bass_kernel_spmd

F32 = mybir.dt.float32
BF16 = mybir.dt.bfloat16
AF = mybir.ActivationFunctionType
ALU = mybir.AluOpType
AX = mybir.AxisListType

SEQ, OWN, D = 4096, 2048, 1024
NDT = 8
NH, HD, NPAIR = 16, 64, 8
NTF, NTO = 32, 16
NCH = 4
NKC = 8
E, F, FS = 8, 1408, 2816
EPS = 1e-5

FFN_UNITS = []
for _e in range(E):
    FFN_UNITS.append(("moe", _e, 0, 6))
    FFN_UNITS.append(("moe", _e, 6, 5))
for _f0, _nf in [(0, 6), (6, 6), (12, 5), (17, 5)]:
    FFN_UNITS.append(("sh", None, _f0, _nf))


class Sched:
    ENGINES = ("sync", "tensor", "scalar", "vector", "gpsimd")

    def __init__(self, nc, ctx):
        self.nc, self.ctx = nc, ctx
        self.q = {e: [] for e in self.ENGINES}
        self.sems = {}
        self.dma_sems = set()

    def _sem(self, name):
        if name not in self.sems:
            self.sems[name] = [self.ctx.enter_context(self.nc.semaphore(name)), 0]
        return self.sems[name]

    def n(self, name):
        return self._sem(name)[1]

    def op(self, eng, meth, *args, waits=(), inc=None, inck=1, **kw):
        ws = []
        for s, v in waits:
            rec = self._sem(s)
            ws.append((rec[0], rec[1] if v is None else v))
        ih = None
        if inc is not None:
            rec = self._sem(inc)
            rec[1] += inck
            ih = (rec[0], inck)
            if inck == 16:
                self.dma_sems.add(inc)
        self.q[eng].append((meth, args, kw, ws, ih))

    def counts(self):
        return {k: v[1] for k, v in self.sems.items()}

    def emit(self, blk):
        for eng in self.ENGINES:
            lst = self.q[eng]
            if not lst:
                continue

            def runner(e, lst=lst):
                for meth, args, kw, ws, ih in lst:
                    for h, v in ws:
                        if v > 0:
                            e.wait_ge(h, v)
                    ins = getattr(e, meth)(*args, **kw)
                    if ih is not None:
                        ins.then_inc(*ih)
            getattr(blk, eng)(runner)
        self.q = {e: [] for e in self.ENGINES}


def build_program(dbg=(), last_phase=6):
    _dbgc = [0]
    nc = bass.Bass("TRN2", target_bir_lowering=False, debug=False)

    xf = nc.dram_tensor("xf", [SEQ, D], F32, kind="ExternalInput").ap()
    cq = nc.dram_tensor("cq", [D, D], F32, kind="ExternalInput").ap()
    ck = nc.dram_tensor("ck", [D, D], F32, kind="ExternalInput").ap()
    cv = nc.dram_tensor("cv", [D, D], F32, kind="ExternalInput").ap()
    co = nc.dram_tensor("co", [D, D], F32, kind="ExternalInput").ap()
    bT = nc.dram_tensor("bT", [128, 4 * NDT], F32, kind="ExternalInput").ap()
    gwT = nc.dram_tensor("gwT", [D, E], F32, kind="ExternalInput").ap()
    wg = nc.dram_tensor("wg", [E, D, F], F32, kind="ExternalInput").ap()
    wu = nc.dram_tensor("wu", [E, D, F], F32, kind="ExternalInput").ap()
    wd = nc.dram_tensor("wd", [E, F, D], F32, kind="ExternalInput").ap()
    swg = nc.dram_tensor("swg", [D, FS], F32, kind="ExternalInput").ap()
    swu = nc.dram_tensor("swu", [D, FS], F32, kind="ExternalInput").ap()
    swd = nc.dram_tensor("swd", [FS, D], F32, kind="ExternalInput").ap()
    out = nc.dram_tensor("out", [OWN, D], F32, kind="ExternalOutput").ap()
    x1s = nc.dram_tensor("x1s", [NTO, 128, D], F32, kind="Internal").ap()

    dbga = {}
    for name, shape, dt in [
        ("xnT", [128, NDT, SEQ], BF16),
        ("qt", [128, OWN], BF16), ("kt", [128, SEQ], BF16),
        ("vt", [128, 32 * 130], BF16),
        ("outT", [128, NPAIR, OWN], BF16),
        ("xn2T", [128, NDT, OWN], BF16),
        ("comb", [128, NTO * E], F32),
        ("yacc", [128, NDT, OWN], F32),
    ]:
        if name in dbg:
            dbga[name] = nc.dram_tensor("dbg_" + name, shape, dt,
                                        kind="ExternalOutput").ap()

    with ExitStack() as ctx:
        En = ctx.enter_context
        S = Sched(nc, ctx)

        consts = En(nc.sbuf_tensor("consts", [128, 8], F32))
        C_EPS = consts[:, 0:1]
        C_NHALF = consts[:, 1:2]
        C_EIGHTH = consts[:, 3:4]
        C_M12 = consts[:, 4:5]
        identB = En(nc.sbuf_tensor("identB", [128, 128], BF16))
        shift64 = En(nc.sbuf_tensor("shift64", [128, 128], BF16))
        onescolF = En(nc.sbuf_tensor("onescolF", [128, 128], F32))
        selmat = En(nc.sbuf_tensor("selmat", [8, E, 128], BF16))
        biasT = En(nc.sbuf_tensor("biasT", [128, 4 * NDT], F32))
        gw_sb = En(nc.sbuf_tensor("gw_sb", [128, NDT, E], BF16))
        m_sum = En(nc.sbuf_tensor("m_sum", [128, NTF], F32))
        m_ssq = En(nc.sbuf_tensor("m_ssq", [128, NTF], F32))
        m_mean = En(nc.sbuf_tensor("m_mean", [128, NTF], F32))
        m_rstd = En(nc.sbuf_tensor("m_rstd", [128, NTF], F32))
        m_negms = En(nc.sbuf_tensor("m_negms", [128, NTF], F32))
        cT_sb = En(nc.sbuf_tensor("cT_sb", [8, OWN], BF16))
        xn2T = En(nc.sbuf_tensor("xn2T", [128, NDT, OWN], BF16))

        S.op("gpsimd", "memset", consts[:, 0:1], EPS)
        S.op("gpsimd", "memset", consts[:, 1:2], -0.5)
        S.op("gpsimd", "memset", consts[:, 2:3], 1.0 / D)
        S.op("gpsimd", "memset", consts[:, 3:4], 0.125)
        S.op("gpsimd", "memset", consts[:, 4:5], -12.0)
        S.op("gpsimd", "memset", identB[:], 0.0, inc="gi")
        S.op("gpsimd", "affine_select", out=identB[:], in_=identB[:],
             compare_op=ALU.not_equal, fill=1.0, base=0,
             pattern=[[-1, 128]], channel_multiplier=1,
             waits=[("gi", None)], inc="gi")
        S.op("gpsimd", "memset", shift64[:], 0.0, inc="gi")
        S.op("gpsimd", "affine_select", out=shift64[:], in_=shift64[:],
             compare_op=ALU.not_equal, fill=1.0, base=-64,
             pattern=[[1, 128]], channel_multiplier=-1,
             waits=[("gi", None)], inc="gi")
        S.op("gpsimd", "memset", onescolF[:], 1.0)
        S.op("gpsimd", "memset", selmat[:], 0.0, inc="gi")
        S.op("gpsimd", "affine_select", out=selmat[:], in_=selmat[:],
             compare_op=ALU.not_equal, fill=1.0, base=0,
             pattern=[[-1, E], [0, 128]], channel_multiplier=1,
             waits=[("gi", None)], inc="gc")
        S.op("gpsimd", "dma_start", out=biasT[:], in_=bT, inc="g0a", inck=16)
        S.op("gpsimd", "dma_start", out=gw_sb[:],
             in_=gwT.rearrange("(t p) e -> p t e", p=128), inc="g0b", inck=16)
        GC0 = S.n("gc")

        outT_stk = ExitStack()
        outT = outT_stk.enter_context(
            nc.sbuf_tensor("outT", [128, NPAIR, OWN], BF16))
        xnT_stk = ExitStack()
        xnT = xnT_stk.enter_context(
            nc.sbuf_tensor("xnT", [128, NDT, SEQ], BF16))
        open_stks = [outT_stk, xnT_stk]

        def finish():
            dws = [(d, None) for d in sorted(S.dma_sems)]
            S.op("sync", "nop", waits=dws)
            S.op("gpsimd", "nop", waits=dws)
            for st in reversed(open_stks):
                st.close()
            b = En(nc.Block())
            S.emit(b)
            return nc

        xr = xf.rearrange("(t p) d -> t p d", p=128)

        with ExitStack() as c1:
            cc = c1.enter_context
            xst = cc(nc.sbuf_tensor("xst", [128, 8, D], F32))
            xnst = cc(nc.sbuf_tensor("xnst", [128, 2, D], BF16))
            sq_scr = cc(nc.sbuf_tensor("sq_scr", [128, D], F32))
            psT = [cc(nc.psum_tensor(f"psT{i}", [128, D], BF16)) for i in range(2)]

            for j in range(NTF // 4):
                ws = []
                if j >= 2:
                    ws = [("aA", 4 * j - 4), ("vA", 4 * j - 4)]
                S.op("sync", "dma_start",
                     out=xst[:, (4 * j) % 8:(4 * j) % 8 + 4, :],
                     in_=xr[4 * j:4 * j + 4].rearrange("t p d -> p t d"),
                     waits=ws, inc=f"xldA{j % 2}", inck=16)
            for t in range(NTF):
                j = t // 4
                wx = (f"xldA{j % 2}", 16 * (j // 2 + 1))
                S.op("vector", "reduce_sum", m_sum[:, t:t + 1], xst[:, t % 8, :],
                     axis=AX.X, waits=[wx], inc="vA")
                S.op("scalar", "activation", sq_scr[:], xst[:, t % 8, :], AF.Square,
                     accum_out=m_ssq[:, t:t + 1],
                     waits=[wx, ("gc", GC0), ("aA", t)],
                     inc="aA")
            S.op("vector", "tensor_scalar_mul", m_mean[:], m_sum[:], 1.0 / D,
                 waits=[("vA", None)], inc="vS")
            S.op("vector", "tensor_mul", m_negms[:], m_mean[:], m_mean[:],
                 waits=[("vS", None)], inc="vS")
            S.op("vector", "scalar_tensor_tensor", m_rstd[:], m_ssq[:], 1.0 / D,
                 m_negms[:], ALU.mult, ALU.subtract,
                 waits=[("aA", None), ("vS", None)], inc="vS")
            S.op("scalar", "activation", m_ssq[:], m_rstd[:], AF.Ln, bias=C_EPS,
                 waits=[("vS", None)], inc="aS")
            S.op("scalar", "activation", m_rstd[:], m_ssq[:], AF.Exp, scale=C_NHALF,
                 waits=[("aS", None)], inc="aS")
            S.op("vector", "scalar_tensor_tensor", m_negms[:], m_mean[:], -1.0,
                 m_rstd[:], ALU.mult, ALU.mult, waits=[("aS", None)], inc="vS")
            STATS1 = S.n("vS")

            for j in range(NTF // 4):
                ws = [("vA", None), ("aA", None)]
                if j >= 2:
                    ws.append(("vN", 4 * j - 4))
                S.op("sync", "dma_start",
                     out=xst[:, (4 * j) % 8:(4 * j) % 8 + 4, :],
                     in_=xr[4 * j:4 * j + 4].rearrange("t p d -> p t d"),
                     waits=ws, inc=f"xldB{j % 2}", inck=16)
            for t in range(NTF):
                jb = t // 4
                ws = [(f"xldB{jb % 2}", 16 * (jb // 2 + 1)), ("vS", STATS1)]
                if t >= 2:
                    ws.append(("pTr", t - 1))
                S.op("vector", "tensor_scalar", xnst[:, t % 2, :], xst[:, t % 8, :],
                     m_rstd[:, t:t + 1], m_negms[:, t:t + 1], ALU.mult, ALU.add,
                     waits=ws, inc="vN")
                for dt in range(NDT):
                    ws = [("vN", S.n("vN")), ("gc", GC0)]
                    if t >= 2:
                        ws.append(("vE", t - 1))
                    S.op("tensor", "transpose",
                         psT[t % 2][:, dt * 128:(dt + 1) * 128],
                         xnst[:, t % 2, dt * 128:(dt + 1) * 128], identB[:],
                         waits=ws if dt == 0 else (),
                         inc="pTr" if dt == NDT - 1 else None)
                S.op("vector", "tensor_copy",
                     xnT[:, :, t * 128:(t + 1) * 128],
                     psT[t % 2][:].rearrange("p (dt c) -> p dt c", c=128),
                     waits=[("pTr", t + 1)], inc="vE")
        XNT = S.n("vE")

        if "xnT" in dbga:
            S.op("gpsimd", "dma_start", out=dbga["xnT"], in_=xnT[:],
                 waits=[("vE", XNT)], inc=f"dbg0", inck=16)

        if last_phase < 2:
            return finish()

        with ExitStack() as c2:
            cc = c2.enter_context
            cqp = [cc(nc.sbuf_tensor(f"cqp{i}", [128, NDT, 128], BF16)) for i in range(2)]
            ckp = [cc(nc.sbuf_tensor(f"ckp{i}", [128, NDT, 128], BF16)) for i in range(2)]
            cvp = [cc(nc.sbuf_tensor(f"cvp{i}", [128, NDT, 128], BF16)) for i in range(2)]
            QTp = [cc(nc.sbuf_tensor(f"QTp{i}", [128, OWN], BF16)) for i in range(2)]
            KTp = [cc(nc.sbuf_tensor(f"KTp{i}", [128, SEQ], BF16)) for i in range(2)]
            Vp = [cc(nc.sbuf_tensor(f"Vp{i}", [128, 32, 130], BF16)) for i in range(2)]
            PT = cc(nc.sbuf_tensor("PT", [128, 3, 1024], BF16))
            lscr = cc(nc.sbuf_tensor("lscr", [128, 2, 512], F32))
            invl = cc(nc.sbuf_tensor("invl", [128, 2, 512], F32))
            stg = cc(nc.sbuf_tensor("stg", [128, 2, 512], BF16))
            psS = [cc(nc.psum_tensor(f"psS{i}", [128, 1024], F32)) for i in range(2)]
            psO = cc(nc.psum_tensor("psO", [128, 1024], F32))
            psP = [cc(nc.psum_tensor(f"psP{i}", [128, 512], F32)) for i in range(2)]

            for i in range(2):
                S.op("gpsimd", "memset", Vp[i][:], 1.0,
                     waits=[("vE", XNT)], inc="gc")
            GV = S.n("gc")

            groups = [(2 * g, 2) for g in range(16)]
            UPC = 32
            pair_end = {}
            proj_pP = {}
            proj_done = {}
            wpair_lv = {}

            def emit_wdma(p):
                bp_ = p % 2
                colsl_ = slice(128 * p, 128 * (p + 1))
                ws = [("g0a", None), ("g0b", None), ("gc", GV), ("vE", XNT)]
                if p >= 2:
                    ws.append(("pP", proj_pP[p - 2]))
                for cn, cmat, dst in (("q", cq, cqp[bp_]), ("k", ck, ckp[bp_]),
                                      ("v", cv, cvp[bp_])):
                    S.op("gpsimd", "dma_start", out=dst[:],
                         in_=cmat.rearrange("(t p) n -> p t n", p=128)[:, :, colsl_],
                         waits=ws, inc=f"wld{cn}{bp_}", inck=16)
                    ws = []
                wpair_lv[p] = [(f"wld{cn}{bp_}", 16 * (p // 2 + 1))
                               for cn in ("q", "k", "v")]

            def make_jobs(p):
                return [(p, "q", c) for c in range(NCH)] + \
                       [(p, "k", kc) for kc in range(NKC)] + \
                       [(p, "v", kt) for kt in range(32)]

            class ProjQ:

                def __init__(self, p, gates):
                    self.p = p
                    self.jobs = make_jobs(p)
                    self.j = 0
                    self.vp0 = S.n("vP")
                    self.gates = gates

                def emit_one(self):
                    p, kind, idx = self.jobs[self.j]
                    bp_ = p % 2
                    pj = psP[self.j % 2][:]
                    ws = list(wpair_lv[p])
                    if self.j >= 2:
                        ws.append(("vP", self.vp0 + self.j - 1))
                    ev_ws = [("pP", None)] + self.gates
                    if kind == "q":
                        for dt in range(NDT):
                            S.op("tensor", "matmul", pj,
                                 cqp[bp_][:, dt, :],
                                 xnT[:, dt, idx * 512:(idx + 1) * 512],
                                 start=(dt == 0), stop=(dt == NDT - 1),
                                 waits=ws if dt == 0 else (),
                                 inc="pP" if dt == NDT - 1 else None)
                        S.op("vector", "tensor_scalar",
                             QTp[bp_][:, idx * 512:(idx + 1) * 512], pj,
                             biasT[:, p:p + 1], None, ALU.add,
                             waits=ev_ws, inc="vP")
                    elif kind == "k":
                        for dt in range(NDT):
                            S.op("tensor", "matmul", pj,
                                 ckp[bp_][:, dt, :],
                                 xnT[:, dt, idx * 512:(idx + 1) * 512],
                                 start=(dt == 0), stop=(dt == NDT - 1),
                                 waits=ws if dt == 0 else (),
                                 inc="pP" if dt == NDT - 1 else None)
                        S.op("vector", "tensor_scalar",
                             KTp[bp_][:, idx * 512:(idx + 1) * 512], pj,
                             biasT[:, NDT + p:NDT + p + 1], None, ALU.add,
                             waits=ev_ws, inc="vP")
                    else:
                        pv = pj[:, 0:128]
                        for dt in range(NDT):
                            S.op("tensor", "matmul", pv,
                                 xnT[:, dt, idx * 128:(idx + 1) * 128],
                                 cvp[bp_][:, dt, :],
                                 start=(dt == 0), stop=(dt == NDT - 1),
                                 waits=ws if dt == 0 else (),
                                 inc="pP" if dt == NDT - 1 else None)
                        dstv = Vp[bp_][:, idx, :].rearrange(
                            "p (a b) -> p a b", b=65)[:, :, 0:64]
                        srcv = pv.rearrange("p (a b) -> p a b", b=64)
                        S.op("vector", "tensor_copy", dstv, srcv,
                             waits=ev_ws, inc="vP")
                    self.j += 1

                def drain_to(self, frac):
                    tgt = min(len(self.jobs), int(len(self.jobs) * frac + 0.999))
                    while self.j < tgt:
                        self.emit_one()

                def finish(self):
                    while self.j < len(self.jobs):
                        self.emit_one()
                    proj_pP[self.p] = S.n("pP")
                    proj_done[self.p] = S.n("vP")

            emit_wdma(0)
            if NPAIR > 1:
                emit_wdma(1)
            q0 = ProjQ(0, [])
            q0.finish()

            for p in range(NPAIR):
                bp = p % 2
                if p + 2 < NPAIR:
                    emit_wdma(p + 2)
                if p + 1 < NPAIR:
                    nq = ProjQ(p + 1, [("pQK", pair_end[p - 1]["pQK"]),
                                       ("pPV", pair_end[p - 1]["pPV"])]
                               if p > 0 else [])
                else:
                    nq = None

                exp_at = {}
                pv_at = {}
                recip_at = {}
                evac_at = {}
                pvb_free = {}
                pend = []

                def emit_pv(job):
                    u, cc_, pts, h = job
                    ws = [("aE", exp_at[u])]
                    if u % UPC <= 1:
                        if cc_ > 0:
                            gate = evac_at[cc_ - 1] if h == 0 else pvb_free[cc_ - 1]
                        else:
                            gate = S.n("vT")
                        ws.append(("vT", gate))
                    vsl = slice(65 * h, 65 * h + 65)
                    k0_, glen_ = groups[(u % UPC) // 2]
                    for i in range(glen_):
                        kt = k0_ + i
                        S.op("tensor", "matmul",
                             psO[0:65, 512 * h:512 * h + 512],
                             Vp[bp][:, kt, vsl],
                             PT[:, pts, i * 512:(i + 1) * 512],
                             start=(kt == 0), stop=(kt == 31),
                             waits=ws if i == 0 else (),
                             inc="pPV" if i == glen_ - 1 else None)
                    pv_at[u] = S.n("pPV")

                for c in range(NCH):
                    qsl = slice(c * 512, (c + 1) * 512)
                    for gi, (k0, glen) in enumerate(groups):
                        ua = c * UPC + 2 * gi
                        ub = ua + 1
                        wsa = []
                        wsb = []
                        if ua >= 2:
                            wsa.append(("aE", exp_at[ua - 2]))
                            wsb.append(("aE", exp_at[ub - 2]))
                        else:
                            wsa = [("vT", S.n("vT")), ("vP", proj_done[p])]
                            wsb = list(wsa)
                        if c > 0 and ua == c * UPC:
                            wsb.append(("vT", recip_at[c - 1]))
                        for h, slot, ws0 in ((0, 0, wsa), (1, 1, wsb)):
                            rowsl = slice(64 * h, 64 * (h + 1))
                            for i in range(glen):
                                kt = k0 + i
                                S.op("tensor", "matmul",
                                     psS[slot][:, i * 512:(i + 1) * 512],
                                     KTp[bp][rowsl, kt * 128:(kt + 1) * 128],
                                     QTp[bp][rowsl, qsl],
                                     start=True, stop=True,
                                     tile_position=(64 * h, 0),
                                     waits=ws0 if i == 0 else (),
                                     inc="pQK" if i == glen - 1 else None)
                        nqk_b = S.n("pQK")
                        while pend:
                            emit_pv(pend.pop(0))
                        for h, u in ((0, ua), (1, ub)):
                            ws = [("pQK", nqk_b - (1 if h == 0 else 0))]
                            if u >= 3:
                                ws.append(("pPV", pv_at[u - 3]))
                            S.op("scalar", "activation",
                                 PT[:, u % 3, 0:glen * 512],
                                 psS[u % 2][:, 0:glen * 512],
                                 AF.Exp, bias=C_M12, scale=C_EIGHTH,
                                 waits=ws, inc="aE")
                            exp_at[u] = S.n("aE")
                        pend.append((ua, c, ua % 3, 0))
                        pend.append((ub, c, ub % 3, 1))
                        if nq is not None:
                            nq.drain_to((2 * gi + 2) / UPC / NCH + c / NCH)
                    while pend:
                        emit_pv(pend.pop(0))
                    S.op("vector", "tensor_copy", lscr[64:65, 0, :],
                         psO[64:65, 0:512],
                         waits=[("pPV", None), ("pT", None)], inc="vT")
                    S.op("vector", "tensor_copy", lscr[64:65, 1, :],
                         psO[64:65, 512:1024], inc="vT")
                    S.op("tensor", "matmul", psS[1][0:64, 0:512],
                         onescolF[64:65, 0:64], lscr[64:65, 0, :],
                         start=True, stop=True,
                         waits=[("vT", None), ("aE", None)], inc="pT")
                    S.op("tensor", "matmul", psS[1][0:64, 512:1024],
                         onescolF[64:65, 0:64], lscr[64:65, 1, :],
                         start=True, stop=True, inc="pT")
                    S.op("vector", "reciprocal",
                         invl[0:64, :, :].rearrange("p a b -> p (a b)"),
                         psS[1][0:64, 0:1024],
                         waits=[("pT", None)], inc="vT")
                    recip_at[c] = S.n("vT")
                    S.op("vector", "tensor_tensor", stg[0:64, 0, :],
                         psO[0:64, 0:512], invl[0:64, 0, :], ALU.mult,
                         waits=[("vT", None)], inc="vT")
                    S.op("vector", "tensor_scalar", outT[0:64, p, qsl],
                         stg[0:64, 0, :],
                         biasT[0:64, 2 * NDT + p:2 * NDT + p + 1], None, ALU.add,
                         waits=[("vT", None)], inc="vT")
                    AEV = S.n("vT")
                    S.op("vector", "tensor_tensor", stg[0:64, 1, :],
                         psO[0:64, 512:1024], invl[0:64, 1, :], ALU.mult,
                         inc="vT")
                    pvb_free[c] = S.n("vT")
                    S.op("tensor", "matmul", psO[:, 0:512], shift64[0:64, 0:128],
                         stg[0:64, 1, :], start=True, stop=True,
                         waits=[("vT", None)], inc="pT")
                    S.op("vector", "tensor_scalar", outT[64:128, p, qsl],
                         psO[64:128, 0:512],
                         biasT[64:128, 2 * NDT + p:2 * NDT + p + 1], None,
                         ALU.add, waits=[("pT", None)], inc="vT")
                    evac_at[c] = S.n("vT")
                if nq is not None:
                    nq.finish()
                pair_end[p] = {"pQK": S.n("pQK"), "pPV": S.n("pPV"),
                               "vT": S.n("vT")}

                if p == 0:
                    if "qt" in dbga:
                        S.op("gpsimd", "dma_start", out=dbga["qt"], in_=QTp[0][:],
                             waits=[("vP", proj_done[0])], inc="dbg90", inck=16)
                    if "kt" in dbga:
                        S.op("gpsimd", "dma_start", out=dbga["kt"], in_=KTp[0][:],
                             waits=[("vP", proj_done[0])], inc="dbg91", inck=16)
                    if "vt" in dbga:
                        S.op("gpsimd", "dma_start", out=dbga["vt"],
                             in_=Vp[0][:].rearrange("p a b -> p (a b)"),
                             waits=[("vP", proj_done[0])], inc="dbg92", inck=16)
            ATT = S.counts()

        if "outT" in dbga:
            S.op("gpsimd", "dma_start", out=dbga["outT"], in_=outT[:],
                 waits=[("vT", ATT["vT"])], inc=f"dbg4", inck=16)

        xnT_stk.close()
        open_stks.remove(xnT_stk)

        if last_phase < 3:
            return finish()

        with ExitStack() as c3:
            cc = c3.enter_context
            co_sb = cc(nc.sbuf_tensor("co_sb", [128, NDT, D], BF16))
            aost = cc(nc.sbuf_tensor("aost", [128, NDT, 512], BF16))
            xo = cc(nc.sbuf_tensor("xo", [128, 8, D], F32))
            x1t = cc(nc.sbuf_tensor("x1t", [128, NTO, D], F32))
            xn2st = cc(nc.sbuf_tensor("xn2st", [128, 2, D], BF16))
            sq2 = cc(nc.sbuf_tensor("sq2", [128, D], F32))
            psC = [cc(nc.psum_tensor(f"psC{i}", [128, 512], F32)) for i in range(2)]
            psT3 = [cc(nc.psum_tensor(f"psT3{i}", [128, D], BF16)) for i in range(2)]

            S.op("gpsimd", "dma_start", out=co_sb[:],
                 in_=co.rearrange("(t p) n -> p t n", p=128),
                 waits=[("vT", ATT["vT"]), ("g0a", None), ("g0b", None)],
                 inc="wldc", inck=16)
            def load_xo(j):
                ws = [("vT", ATT["vT"])] if j <= 1 else [("vAdd", 4 * (j - 1))]
                S.op("sync", "dma_start",
                     out=xo[:, (4 * j) % 8:(4 * j) % 8 + 4, :],
                     in_=xr[4 * j:4 * j + 4].rearrange("t p d -> p t d"),
                     waits=ws, inc=f"xoL{j % 2}", inck=16)

            load_xo(0)
            load_xo(1)

            for c in range(NCH):
                qsl = slice(c * 512, (c + 1) * 512)
                for dout in range(NDT):
                    pj = psC[dout % 2][:]
                    ws = [("wldc", 16), ("vT", ATT["vT"])]
                    if c * NDT + dout >= 2:
                        ws.append(("aCo", S.n("aCo") - 1))
                    for dt in range(NDT):
                        S.op("tensor", "matmul", pj,
                             co_sb[:, dt, dout * 128:(dout + 1) * 128],
                             outT[:, dt, qsl],
                             start=(dt == 0), stop=(dt == NDT - 1),
                             waits=ws if dt == 0 else (),
                             inc="pCo" if dt == NDT - 1 else None)
                    S.op("scalar", "activation", aost[:, dout, :], pj, AF.Identity,
                         bias=biasT[:, 3 * NDT + dout:3 * NDT + dout + 1],
                         waits=[("pCo", None), ("pT3", None)], inc="aCo")
                ACO = S.n("aCo")
                for tt in range(4):
                    T = c * 4 + tt
                    for dt in range(NDT):
                        ws = [("aCo", ACO), ("vAdd", max(0, T - 1))]
                        S.op("tensor", "transpose",
                             psT3[T % 2][:, dt * 128:(dt + 1) * 128],
                             aost[:, dt, tt * 128:(tt + 1) * 128], identB[:],
                             waits=ws if dt == 0 else (),
                             inc="pT3" if dt == NDT - 1 else None)
                    S.op("vector", "tensor_tensor", x1t[:, T, :], xo[:, T % 8, :],
                         psT3[T % 2][:], ALU.add,
                         waits=[("pT3", None),
                                (f"xoL{(T // 4) % 2}", 16 * (T // 4 // 2 + 1))],
                         inc="vAdd")
                    S.op("sync", "dma_start", out=x1s[T], in_=x1t[:, T, :],
                         waits=[("vAdd", None)], inc=f"stX{T % 2}", inck=16)
                    S.op("vector", "reduce_sum", m_sum[:, T:T + 1], x1t[:, T, :],
                         axis=AX.X, waits=[("vAdd", None)], inc="vL")
                    S.op("scalar", "activation", sq2[:], x1t[:, T, :], AF.Square,
                         accum_out=m_ssq[:, T:T + 1],
                         waits=[("vAdd", None), ("aL", None)], inc="aL")
                if c + 2 < NCH:
                    load_xo(c + 2)
            s16 = slice(0, NTO)
            S.op("vector", "tensor_scalar_mul", m_mean[:, s16], m_sum[:, s16],
                 1.0 / D, waits=[("vL", None)], inc="vS2")
            S.op("vector", "tensor_mul", m_negms[:, s16], m_mean[:, s16],
                 m_mean[:, s16], waits=[("vS2", None)], inc="vS2")
            S.op("vector", "scalar_tensor_tensor", m_rstd[:, s16], m_ssq[:, s16],
                 1.0 / D, m_negms[:, s16], ALU.mult, ALU.subtract,
                 waits=[("aL", None), ("vS2", None)], inc="vS2")
            S.op("scalar", "activation", m_ssq[:, s16], m_rstd[:, s16], AF.Ln,
                 bias=C_EPS, waits=[("vS2", None)], inc="aS2")
            S.op("scalar", "activation", m_rstd[:, s16], m_ssq[:, s16], AF.Exp,
                 scale=C_NHALF, waits=[("aS2", None)], inc="aS2")
            S.op("vector", "scalar_tensor_tensor", m_negms[:, s16], m_mean[:, s16],
                 -1.0, m_rstd[:, s16], ALU.mult, ALU.mult,
                 waits=[("aS2", None)], inc="vS2")
            STATS2 = S.n("vS2")
            for T in range(NTO):
                ws = [("vS2", STATS2)]
                if T >= 2:
                    ws.append(("pT4", T - 1))
                S.op("vector", "tensor_scalar", xn2st[:, T % 2, :], x1t[:, T, :],
                     m_rstd[:, T:T + 1], m_negms[:, T:T + 1], ALU.mult, ALU.add,
                     waits=ws, inc="vN2")
                for dt in range(NDT):
                    ws = [("vN2", S.n("vN2"))]
                    if T >= 2:
                        ws.append(("vE2", T - 1))
                    S.op("tensor", "transpose",
                         psT3[T % 2][:, dt * 128:(dt + 1) * 128],
                         xn2st[:, T % 2, dt * 128:(dt + 1) * 128], identB[:],
                         waits=ws if dt == 0 else (),
                         inc="pT4" if dt == NDT - 1 else None)
                S.op("vector", "tensor_copy",
                     xn2T[:, :, T * 128:(T + 1) * 128],
                     psT3[T % 2][:].rearrange("p (dt c2) -> p dt c2", c2=128),
                     waits=[("pT4", T + 1)], inc="vE2")
            P3 = S.counts()

        if "xn2T" in dbga:
            S.op("gpsimd", "dma_start", out=dbga["xn2T"], in_=xn2T[:],
                 waits=[("vE2", P3["vE2"])], inc=f"dbg5", inck=16)

        outT_stk.close()
        open_stks.remove(outT_stk)

        if last_phase < 4:
            return finish()

        with ExitStack() as c4:
            cc = c4.enter_context
            identF = cc(nc.sbuf_tensor("identF", [128, 128], F32))
            lg_sb = cc(nc.sbuf_tensor("lg_sb", [8, NCH, 512], F32))
            lgt = cc(nc.sbuf_tensor("lgt", [128, NTO, E], F32))
            rt_is1 = cc(nc.sbuf_tensor("rt_is1", [128, NTO, E], F32))
            rt_is2 = cc(nc.sbuf_tensor("rt_is2", [128, NTO, E], F32))
            rt_sh = cc(nc.sbuf_tensor("rt_sh", [128, NTO, E], F32))
            rt_m1 = cc(nc.sbuf_tensor("rt_m1", [128, NTO], F32))
            rt_m2 = cc(nc.sbuf_tensor("rt_m2", [128, NTO], F32))
            rt_z = cc(nc.sbuf_tensor("rt_z", [128, NTO], F32))
            rt_e2 = cc(nc.sbuf_tensor("rt_e2", [128, NTO], F32))
            rt_w1 = cc(nc.sbuf_tensor("rt_w1", [128, NTO], F32))
            rt_w2 = cc(nc.sbuf_tensor("rt_w2", [128, NTO], F32))
            comb_bf = cc(nc.sbuf_tensor("comb_bf", [128, NTO, E], BF16))
            psG = cc(nc.psum_tensor("psG", [8, NCH * 512], F32))
            psR = cc(nc.psum_tensor("psR", [128, NTO * 8], F32))
            psR2 = cc(nc.psum_tensor("psR2", [8, NTO * 128], BF16))

            S.op("gpsimd", "memset", identF[:], 0.0,
                 waits=[("vE2", P3["vE2"]), ("stX0", 16 * (NTO // 2)),
                        ("stX1", 16 * (NTO // 2))], inc="gF")
            S.op("gpsimd", "affine_select", out=identF[:], in_=identF[:],
                 compare_op=ALU.not_equal, fill=1.0, base=0,
                 pattern=[[-1, 128]], channel_multiplier=1,
                 waits=[("gF", None)], inc="gF")

            for ch in range(NCH):
                for dt in range(NDT):
                    S.op("tensor", "matmul", psG[:, ch * 512:(ch + 1) * 512],
                         gw_sb[:, dt, :], xn2T[:, dt, ch * 512:(ch + 1) * 512],
                         start=(dt == 0), stop=(dt == NDT - 1),
                         waits=[("vE2", P3["vE2"]), ("g0b", None)]
                         if (ch == 0 and dt == 0) else (),
                         inc="pG" if dt == NDT - 1 else None)
            S.op("vector", "tensor_copy",
                 lg_sb[:].rearrange("p a b -> p (a b)"), psG[:],
                 waits=[("pG", None)], inc="vR")
            lg2 = lg_sb[:].rearrange("p a b -> p (a b)")
            for T in range(NTO):
                S.op("tensor", "transpose", psR[:, T * 8:(T + 1) * 8],
                     lg2[0:8, T * 128:(T + 1) * 128], identF[0:8, 0:8],
                     waits=[("vR", None), ("gF", None)] if T == 0 else (),
                     inc="pR" if T == NTO - 1 else None)
            S.op("vector", "tensor_copy", lgt[:].rearrange("p a b -> p (a b)"),
                 psR[:], waits=[("pR", None)], inc="vR")
            for T in range(NTO):
                S.op("vector", "reduce_max", rt_m1[:, T:T + 1], lgt[:, T, :],
                     axis=AX.X, waits=[("vR", None)], inc="vR")
                S.op("vector", "tensor_scalar", rt_is1[:, T, :], lgt[:, T, :],
                     rt_m1[:, T:T + 1], None, ALU.is_ge,
                     waits=[("vR", None)], inc="vR")
                S.op("vector", "scalar_tensor_tensor", rt_sh[:, T, :],
                     rt_is1[:, T, :], -1e30, lgt[:, T, :], ALU.mult, ALU.add,
                     waits=[("vR", None)], inc="vR")
                S.op("vector", "reduce_max", rt_m2[:, T:T + 1], rt_sh[:, T, :],
                     axis=AX.X, waits=[("vR", None)], inc="vR")
                S.op("vector", "tensor_scalar", rt_is2[:, T, :], rt_sh[:, T, :],
                     rt_m2[:, T:T + 1], None, ALU.is_ge,
                     waits=[("vR", None)], inc="vR")
                S.op("vector", "tensor_scalar", rt_sh[:, T, :], lgt[:, T, :],
                     rt_m1[:, T:T + 1], None, ALU.subtract,
                     waits=[("vR", None)], inc="vR")
            S.op("scalar", "activation", rt_sh[:], rt_sh[:], AF.Exp,
                 waits=[("vR", None), ("aL", P3["aL"])], inc="aR")
            S.op("vector", "reduce_sum", rt_z[:], rt_sh[:], axis=AX.X,
                 waits=[("aR", None)], inc="vR")
            S.op("vector", "tensor_sub", rt_e2[:], rt_m2[:], rt_m1[:],
                 waits=[("vR", None)], inc="vR")
            S.op("scalar", "activation", rt_e2[:], rt_e2[:], AF.Exp,
                 waits=[("vR", None)], inc="aR")
            S.op("vector", "scalar_tensor_tensor", rt_w1[:], rt_z[:], 1e-20,
                 rt_e2[:], ALU.mult, ALU.add,
                 waits=[("aR", None), ("vR", None)], inc="vR")
            S.op("vector", "tensor_scalar", rt_w1[:], rt_w1[:], 1.0, None, ALU.add,
                 waits=[("vR", None)], inc="vR")
            S.op("vector", "reciprocal", rt_w1[:], rt_w1[:],
                 waits=[("vR", None)], inc="vR")
            S.op("vector", "tensor_scalar", rt_w1[:], rt_w1[:], 2.5, None, ALU.mult,
                 waits=[("vR", None)], inc="vR")
            S.op("vector", "tensor_mul", rt_w2[:], rt_e2[:], rt_w1[:],
                 waits=[("vR", None)], inc="vR")
            b1, b2 = bass.broadcast_tensor_aps(
                rt_is1[:], rt_w1[:].rearrange("p (t o) -> p t o", o=1))
            S.op("vector", "tensor_tensor", rt_is1[:], b1, b2, ALU.mult,
                 waits=[("vR", None)], inc="vR")
            b1, b2 = bass.broadcast_tensor_aps(
                rt_is2[:], rt_w2[:].rearrange("p (t o) -> p t o", o=1))
            S.op("vector", "tensor_tensor", rt_is2[:], b1, b2, ALU.mult,
                 waits=[("vR", None)], inc="vR")
            S.op("vector", "tensor_add", lgt[:], rt_is1[:], rt_is2[:],
                 waits=[("vR", None)], inc="vR")
            if "comb" in dbga:
                S.op("gpsimd", "dma_start", out=dbga["comb"],
                     in_=lgt[:].rearrange("p a b -> p (a b)"),
                     waits=[("vR", None)], inc=f"dbg6", inck=16)
            S.op("vector", "tensor_copy", comb_bf[:], lgt[:],
                 waits=[("vR", None)], inc="vR")
            for T in range(NTO):
                S.op("tensor", "transpose", psR2[:, T * 128:(T + 1) * 128],
                     comb_bf[:, T, :], identB[:],
                     waits=[("vR", None)] if T == 0 else (),
                     inc="pR2" if T == NTO - 1 else None)
            S.op("vector", "tensor_copy", cT_sb[:], psR2[:],
                 waits=[("pR2", None)], inc="vR")
            P4 = S.counts()

        if last_phase < 5:
            return finish()

        yacc = En(nc.sbuf_tensor("yacc", [128, NDT, OWN], F32))
        with ExitStack() as c5:
            cc = c5.enter_context
            wgh = [cc(nc.sbuf_tensor(f"wgh{i}", [128, NDT, 768], BF16)) for i in range(2)]
            wuh = [cc(nc.sbuf_tensor(f"wuh{i}", [128, NDT, 768], BF16)) for i in range(2)]
            wdh = [cc(nc.sbuf_tensor(f"wdh{i}", [128, 6, D], BF16)) for i in range(2)]
            prod = cc(nc.sbuf_tensor("prod", [128, 6, OWN // 2], BF16))
            gs_sb = cc(nc.sbuf_tensor("gs_sb", [128, 2, 512], BF16))
            us_sb = cc(nc.sbuf_tensor("us_sb", [128, 2, 512], BF16))
            ce_sb = cc(nc.sbuf_tensor("ce_sb", [128, OWN], BF16))
            psGU = [cc(nc.psum_tensor(f"psGU{i}", [128, 512], F32)) for i in range(4)]
            psD = [cc(nc.psum_tensor(f"psD{i}", [128, 512], F32)) for i in range(2)]
            psCE = cc(nc.psum_tensor("psCE", [128, 512], F32))

            unit_end = {}
            jig = 0
            for ui, (kind, e, f0, nf) in enumerate(FFN_UNITS):
                bu = ui % 2
                ws = [("vT", ATT["vT"])]
                if ui >= 2:
                    ws.append(("pD", unit_end[ui - 2]["pD"]))
                    ws.append(("pGU", unit_end[ui - 2]["pGU"]))
                if kind == "moe":
                    srcg = wg[e].rearrange("(t p) f -> p t f", p=128)[:, :, f0 * 128:(f0 + nf) * 128]
                    srcu = wu[e].rearrange("(t p) f -> p t f", p=128)[:, :, f0 * 128:(f0 + nf) * 128]
                    srcd = wd[e, f0 * 128:(f0 + nf) * 128, :].rearrange("(t p) d2 -> p t d2", p=128)
                else:
                    srcg = swg.rearrange("(t p) f -> p t f", p=128)[:, :, f0 * 128:(f0 + nf) * 128]
                    srcu = swu.rearrange("(t p) f -> p t f", p=128)[:, :, f0 * 128:(f0 + nf) * 128]
                    srcd = swd[f0 * 128:(f0 + nf) * 128, :].rearrange("(t p) d2 -> p t d2", p=128)
                S.op("gpsimd", "dma_start", out=wgh[bu][:, :, 0:nf * 128], in_=srcg,
                     waits=ws, inc=f"wfg{bu}", inck=16)
                S.op("gpsimd", "dma_start", out=wuh[bu][:, :, 0:nf * 128], in_=srcu,
                     inc=f"wfu{bu}", inck=16)
                S.op("gpsimd", "dma_start", out=wdh[bu][:, 0:nf, :], in_=srcd,
                     inc=f"wfd{bu}", inck=16)
                lvl = 16 * (ui // 2 + 1)
                WUg = (f"wfg{bu}", lvl)
                WUu = (f"wfu{bu}", lvl)
                WUd = (f"wfd{bu}", lvl)

                if kind == "moe" and f0 == 0:
                    for ch in range(NCH):
                        ws = [("vUs", None), ("vCE", None)]
                        if ui == 0 and ch == 0:
                            ws += [("vR", P4["vR"])]
                        S.op("tensor", "matmul", psCE[:],
                             selmat[:, e, :], cT_sb[:, ch * 512:(ch + 1) * 512],
                             start=True, stop=True, waits=ws, inc="pCE")
                        S.op("vector", "tensor_copy",
                             ce_sb[:, ch * 512:(ch + 1) * 512], psCE[:],
                             waits=[("pCE", None)], inc="vCE")
                CE = S.n("vCE")

                down_q = []
                dji = [0]

                def emit_down(job):
                    dout, ch_, chl_, lvl = job
                    pd = psD[dji[0] % 2][:]
                    ws = [("vPr", lvl), WUd]
                    if ui == 0 and dji[0] < 2:
                        ws.append(("vR", P4["vR"]))
                    if ui > 0 or dji[0] >= 2:
                        ws.append(("vF2", S.n("vF2") - 1))
                    for ft_ in range(nf):
                        S.op("tensor", "matmul", pd,
                             wdh[bu][:, ft_, dout * 128:(dout + 1) * 128],
                             prod[:, ft_, chl_ * 512:(chl_ + 1) * 512],
                             start=(ft_ == 0), stop=(ft_ == nf - 1),
                             waits=ws if ft_ == 0 else (),
                             inc="pD" if ft_ == nf - 1 else None)
                    if ui == 0:
                        S.op("vector", "tensor_copy",
                             yacc[:, dout, ch_ * 512:(ch_ + 1) * 512], pd,
                             waits=[("pD", None)], inc="vF2")
                    else:
                        S.op("vector", "tensor_tensor",
                             yacc[:, dout, ch_ * 512:(ch_ + 1) * 512],
                             yacc[:, dout, ch_ * 512:(ch_ + 1) * 512], pd,
                             ALU.add, waits=[("pD", None), ("vF2", None)],
                             inc="vF2")
                    dji[0] += 1

                for th in range(2):
                    for chl in range(2):
                        ch = 2 * th + chl
                        for ft in range(nf):
                            pg = psGU[(2 * jig) % 4][:]
                            pu = psGU[(2 * jig + 1) % 4][:]
                            ws = [WUg, ("vE2", P3["vE2"])]
                            if ui == 0 and th == 0 and ft == 0 and chl == 0:
                                ws.append(("vR", P4["vR"]))
                            if jig >= 2:
                                ws.append(("aSil", jig - 1))
                            for dt in range(NDT):
                                S.op("tensor", "matmul", pg,
                                     wgh[bu][:, dt, ft * 128:(ft + 1) * 128],
                                     xn2T[:, dt, ch * 512:(ch + 1) * 512],
                                     start=(dt == 0), stop=(dt == NDT - 1),
                                     waits=ws if dt == 0 else (),
                                     inc="pGU" if dt == NDT - 1 else None)
                            ws2 = [WUu]
                            if jig >= 2:
                                ws2.append(("vUs", jig - 1))
                            for dt in range(NDT):
                                S.op("tensor", "matmul", pu,
                                     wuh[bu][:, dt, ft * 128:(ft + 1) * 128],
                                     xn2T[:, dt, ch * 512:(ch + 1) * 512],
                                     start=(dt == 0), stop=(dt == NDT - 1),
                                     waits=ws2 if dt == 0 else (),
                                     inc="pGU" if dt == NDT - 1 else None)
                            PGU = S.n("pGU")
                            S.op("scalar", "activation", gs_sb[:, jig % 2, :], pg,
                                 AF.Silu,
                                 waits=[("pGU", PGU - 1), ("vPr", max(0, jig - 1))],
                                 inc="aSil")
                            if kind == "moe":
                                S.op("vector", "tensor_tensor", us_sb[:, jig % 2, :],
                                     pu, ce_sb[:, ch * 512:(ch + 1) * 512], ALU.mult,
                                     waits=[("pGU", PGU), ("vCE", CE),
                                            ("vPr", max(0, S.n("vPr") - 1))],
                                     inc="vUs")
                            else:
                                S.op("vector", "tensor_copy", us_sb[:, jig % 2, :],
                                     pu,
                                     waits=[("pGU", PGU),
                                            ("vPr", max(0, S.n("vPr") - 1))],
                                     inc="vUs")
                            S.op("vector", "tensor_mul",
                                 prod[:, ft, chl * 512:(chl + 1) * 512],
                                 gs_sb[:, jig % 2, :], us_sb[:, jig % 2, :],
                                 waits=[("aSil", None), ("vUs", None), ("pD", None)],
                                 inc="vPr")
                            jig += 1
                            while down_q and len(down_q) > (nf - 1 - ft) * 2 - 1:
                                emit_down(down_q.pop(0))
                        lvl = S.n("vPr")
                        while down_q:
                            emit_down(down_q.pop(0))
                        down_q = [(dout, 2 * th + chl, chl, lvl)
                                  for dout in range(NDT)]
                while down_q:
                    emit_down(down_q.pop(0))
                unit_end[ui] = {"pD": S.n("pD"), "pGU": S.n("pGU")}
            P5 = S.counts()

        if "yacc" in dbga:
            S.op("gpsimd", "dma_start", out=dbga["yacc"], in_=yacc[:],
                 waits=[("vF2", P5["vF2"])], inc=f"dbg7", inck=16)

        if last_phase < 6:
            return finish()

        with ExitStack() as c6:
            cc = c6.enter_context
            ybf = cc(nc.sbuf_tensor("ybf", [128, NDT, 512], BF16))
            x1r = cc(nc.sbuf_tensor("x1r", [128, 4, D], F32))
            obuf = cc(nc.sbuf_tensor("obuf", [128, 2, D], F32))
            psT6 = [cc(nc.psum_tensor(f"psT6{i}", [128, D], BF16)) for i in range(2)]

            outr = out.rearrange("(t p) d -> t p d", p=128)

            def load_x1(T, ws):
                S.op("sync", "dma_start", out=x1r[:, T % 4, :], in_=x1s[T],
                     waits=ws, inc=f"x1L{T % 4}", inck=16)

            for T in range(4):
                load_x1(T, [("stX0", 16 * (NTO // 2)), ("stX1", 16 * (NTO // 2)),
                            ("vF2", P5["vF2"]), ("aSil", P5["aSil"]),
                            ("vPr", P5["vPr"])])
            for c in range(NCH):
                S.op("vector", "tensor_copy", ybf[:],
                     yacc[:, :, c * 512:(c + 1) * 512],
                     waits=[("vF2", P5["vF2"]), ("vO", None)], inc="vY")
                for tt in range(4):
                    T = c * 4 + tt
                    for dt in range(NDT):
                        ws = [("vY", None), ("vF2", P5["vF2"])]
                        if T >= 2:
                            ws.append(("vO", T - 1))
                        S.op("tensor", "transpose",
                             psT6[T % 2][:, dt * 128:(dt + 1) * 128],
                             ybf[:, dt, tt * 128:(tt + 1) * 128], identB[:],
                             waits=ws if dt == 0 else (),
                             inc="pT6" if dt == NDT - 1 else None)
                    S.op("vector", "tensor_tensor", obuf[:, T % 2, :],
                         x1r[:, T % 4, :], psT6[T % 2][:], ALU.add,
                         waits=[("pT6", None), (f"x1L{T % 4}", 16 * (T // 4 + 1)),
                                (f"stO{T % 2}", 16 * (T // 2))], inc="vO")
                    S.op("sync", "dma_start", out=outr[T], in_=obuf[:, T % 2, :],
                         waits=[("vO", None)], inc=f"stO{T % 2}", inck=16)
                    if T + 4 < NTO:
                        load_x1(T + 4, [("vO", None)])
            fin = [(d, None) for d in sorted(S.dma_sems)]
            S.op("sync", "nop", waits=fin)
            S.op("gpsimd", "nop", waits=fin)

        blk = En(nc.Block())
        S.emit(blk)

    return nc


def _circulant(c):
    c = np.asarray(c, np.float32)
    idx = (np.arange(D)[None, :] - np.arange(D)[:, None]) % D
    return np.ascontiguousarray(c[idx])


def _biasT(b):
    return np.ascontiguousarray(np.asarray(b, np.float32).reshape(NDT, 128).T)


def make_inputs(inputs):
    x = np.ascontiguousarray(np.asarray(inputs["L0_attn_x"], np.float32))
    base = {
        "cq": _circulant(inputs["L0_attn_wq_c"]),
        "ck": _circulant(inputs["L0_attn_wk_c"]),
        "cv": _circulant(inputs["L0_attn_wv_c"]),
        "co": _circulant(inputs["L0_attn_wo_c"]),
        "bT": np.ascontiguousarray(np.concatenate(
            [_biasT(inputs["L0_attn_wq_b"]), _biasT(inputs["L0_attn_wk_b"]),
             _biasT(inputs["L0_attn_wv_b"]), _biasT(inputs["L0_attn_wo_b"])],
            axis=1)),
        "gwT": np.ascontiguousarray(
            np.asarray(inputs["L0_ffn_gate_w"], np.float32).T),
        "wg": np.ascontiguousarray(np.asarray(inputs["L0_ffn_w_gate"], np.float32)),
        "wu": np.ascontiguousarray(np.asarray(inputs["L0_ffn_w_up"], np.float32)),
        "wd": np.ascontiguousarray(np.asarray(inputs["L0_ffn_w_down"], np.float32)),
        "swg": np.ascontiguousarray(np.asarray(inputs["L0_ffn_sw_gate"], np.float32)),
        "swu": np.ascontiguousarray(np.asarray(inputs["L0_ffn_sw_up"], np.float32)),
        "swd": np.ascontiguousarray(np.asarray(inputs["L0_ffn_sw_down"], np.float32)),
    }
    in_maps = []
    for core in range(8):
        b, h = core // 2, core % 2
        xb = x[b]
        m = dict(base)
        m["xf"] = np.ascontiguousarray(
            np.concatenate([xb[h * OWN:(h + 1) * OWN],
                            xb[(1 - h) * OWN:(2 - h) * OWN]], axis=0))
        in_maps.append(m)
    return in_maps


def kernel(**inputs):
    nc = build_program()
    in_maps = make_inputs(inputs)
    res = run_bass_kernel_spmd(nc, in_maps, core_ids=list(range(8)))
    y = np.empty((4, SEQ, D), np.float32)
    for core in range(8):
        b, h = core // 2, core % 2
        y[b, h * OWN:(h + 1) * OWN] = res.results[core]["out"]
    return y


# version 12 (final)
# speedup vs baseline: 1.1058x; 1.0052x over previous
import numpy as np
import ml_dtypes
from contextlib import ExitStack

import concourse.bass as bass
import concourse.mybir as mybir
from concourse.bass_utils import run_bass_kernel_spmd

F32 = mybir.dt.float32
BF16 = mybir.dt.bfloat16
AF = mybir.ActivationFunctionType
ALU = mybir.AluOpType
AX = mybir.AxisListType

SEQ, OWN, D = 4096, 2048, 1024
NDT = 8
NH, HD, NPAIR = 16, 64, 8
NTF, NTO = 32, 16
NCH = 4
NKC = 8
E, F, FS = 8, 1408, 2816
EPS = 1e-5

FFN_UNITS = []
for _e in range(E):
    FFN_UNITS.append(("moe", _e, 0, 6))
    FFN_UNITS.append(("moe", _e, 6, 5))
for _f0, _nf in [(0, 6), (6, 6), (12, 5), (17, 5)]:
    FFN_UNITS.append(("sh", None, _f0, _nf))


class Sched:
    ENGINES = ("sync", "tensor", "scalar", "vector", "gpsimd")

    def __init__(self, nc, ctx):
        self.nc, self.ctx = nc, ctx
        self.q = {e: [] for e in self.ENGINES}
        self.sems = {}
        self.dma_sems = set()

    def _sem(self, name):
        if name not in self.sems:
            self.sems[name] = [self.ctx.enter_context(self.nc.semaphore(name)), 0]
        return self.sems[name]

    def n(self, name):
        return self._sem(name)[1]

    def op(self, eng, meth, *args, waits=(), inc=None, inck=1, **kw):
        ws = []
        for s, v in waits:
            rec = self._sem(s)
            ws.append((rec[0], rec[1] if v is None else v))
        ih = None
        if inc is not None:
            rec = self._sem(inc)
            rec[1] += inck
            ih = (rec[0], inck)
            if inck == 16:
                self.dma_sems.add(inc)
        self.q[eng].append((meth, args, kw, ws, ih))

    def counts(self):
        return {k: v[1] for k, v in self.sems.items()}

    def emit(self, blk):
        for eng in self.ENGINES:
            lst = self.q[eng]
            if not lst:
                continue

            def runner(e, lst=lst):
                for meth, args, kw, ws, ih in lst:
                    for h, v in ws:
                        if v > 0:
                            e.wait_ge(h, v)
                    ins = getattr(e, meth)(*args, **kw)
                    if ih is not None:
                        ins.then_inc(*ih)
            getattr(blk, eng)(runner)
        self.q = {e: [] for e in self.ENGINES}


def build_program(dbg=(), last_phase=6):
    _dbgc = [0]
    nc = bass.Bass("TRN2", target_bir_lowering=False, debug=False)

    xf = nc.dram_tensor("xf", [SEQ, D], F32, kind="ExternalInput").ap()
    cq = nc.dram_tensor("cq", [D, D], F32, kind="ExternalInput").ap()
    ck = nc.dram_tensor("ck", [D, D], F32, kind="ExternalInput").ap()
    cv = nc.dram_tensor("cv", [D, D], F32, kind="ExternalInput").ap()
    co = nc.dram_tensor("co", [D, D], F32, kind="ExternalInput").ap()
    bT = nc.dram_tensor("bT", [128, 4 * NDT], F32, kind="ExternalInput").ap()
    gwT = nc.dram_tensor("gwT", [D, E], F32, kind="ExternalInput").ap()
    wg = nc.dram_tensor("wg", [E, D, F], F32, kind="ExternalInput").ap()
    wu = nc.dram_tensor("wu", [E, D, F], F32, kind="ExternalInput").ap()
    wd = nc.dram_tensor("wd", [E, F, D], F32, kind="ExternalInput").ap()
    swg = nc.dram_tensor("swg", [D, FS], F32, kind="ExternalInput").ap()
    swu = nc.dram_tensor("swu", [D, FS], F32, kind="ExternalInput").ap()
    swd = nc.dram_tensor("swd", [FS, D], F32, kind="ExternalInput").ap()
    out = nc.dram_tensor("out", [OWN, D], F32, kind="ExternalOutput").ap()
    x1s = nc.dram_tensor("x1s", [NTO, 128, D], F32, kind="Internal").ap()

    dbga = {}
    for name, shape, dt in [
        ("xnT", [128, NDT, SEQ], BF16),
        ("qt", [128, OWN], BF16), ("kt", [128, SEQ], BF16),
        ("vt", [128, 32 * 130], BF16),
        ("outT", [128, NPAIR, OWN], BF16),
        ("xn2T", [128, NDT, OWN], BF16),
        ("comb", [128, NTO * E], F32),
        ("yacc", [128, NDT, OWN], F32),
    ]:
        if name in dbg:
            dbga[name] = nc.dram_tensor("dbg_" + name, shape, dt,
                                        kind="ExternalOutput").ap()

    with ExitStack() as ctx:
        En = ctx.enter_context
        S = Sched(nc, ctx)

        consts = En(nc.sbuf_tensor("consts", [128, 8], F32))
        C_EPS = consts[:, 0:1]
        C_NHALF = consts[:, 1:2]
        C_EIGHTH = consts[:, 3:4]
        C_M12 = consts[:, 4:5]
        identB = En(nc.sbuf_tensor("identB", [128, 128], BF16))
        shift64 = En(nc.sbuf_tensor("shift64", [128, 128], BF16))
        onescolF = En(nc.sbuf_tensor("onescolF", [128, 128], F32))
        selmat = En(nc.sbuf_tensor("selmat", [8, E, 128], BF16))
        biasT = En(nc.sbuf_tensor("biasT", [128, 4 * NDT], F32))
        gw_sb = En(nc.sbuf_tensor("gw_sb", [128, NDT, E], BF16))
        m_sum = En(nc.sbuf_tensor("m_sum", [128, NTF], F32))
        m_ssq = En(nc.sbuf_tensor("m_ssq", [128, NTF], F32))
        m_mean = En(nc.sbuf_tensor("m_mean", [128, NTF], F32))
        m_rstd = En(nc.sbuf_tensor("m_rstd", [128, NTF], F32))
        m_negms = En(nc.sbuf_tensor("m_negms", [128, NTF], F32))
        cT_sb = En(nc.sbuf_tensor("cT_sb", [8, OWN], BF16))
        xn2T = En(nc.sbuf_tensor("xn2T", [128, NDT, OWN], BF16))

        S.op("gpsimd", "memset", consts[:, 0:1], EPS)
        S.op("gpsimd", "memset", consts[:, 1:2], -0.5)
        S.op("gpsimd", "memset", consts[:, 2:3], 1.0 / D)
        S.op("gpsimd", "memset", consts[:, 3:4], 0.125)
        S.op("gpsimd", "memset", consts[:, 4:5], -12.0)
        S.op("gpsimd", "memset", identB[:], 0.0, inc="gi")
        S.op("gpsimd", "affine_select", out=identB[:], in_=identB[:],
             compare_op=ALU.not_equal, fill=1.0, base=0,
             pattern=[[-1, 128]], channel_multiplier=1,
             waits=[("gi", None)], inc="gi")
        S.op("gpsimd", "memset", shift64[:], 0.0, inc="gi")
        S.op("gpsimd", "affine_select", out=shift64[:], in_=shift64[:],
             compare_op=ALU.not_equal, fill=1.0, base=-64,
             pattern=[[1, 128]], channel_multiplier=-1,
             waits=[("gi", None)], inc="gi")
        S.op("gpsimd", "memset", onescolF[:], 1.0)
        S.op("gpsimd", "memset", selmat[:], 0.0, inc="gi")
        S.op("gpsimd", "affine_select", out=selmat[:], in_=selmat[:],
             compare_op=ALU.not_equal, fill=1.0, base=0,
             pattern=[[-1, E], [0, 128]], channel_multiplier=1,
             waits=[("gi", None)], inc="gc")
        S.op("gpsimd", "dma_start", out=biasT[:], in_=bT, inc="g0a", inck=16)
        S.op("gpsimd", "dma_start", out=gw_sb[:],
             in_=gwT.rearrange("(t p) e -> p t e", p=128), inc="g0b", inck=16)
        GC0 = S.n("gc")

        outT_stk = ExitStack()
        outT = outT_stk.enter_context(
            nc.sbuf_tensor("outT", [128, NPAIR, OWN], BF16))
        xnT_stk = ExitStack()
        xnT = xnT_stk.enter_context(
            nc.sbuf_tensor("xnT", [128, NDT, SEQ], BF16))
        open_stks = [outT_stk, xnT_stk]

        def finish():
            dws = [(d, None) for d in sorted(S.dma_sems)]
            S.op("sync", "nop", waits=dws)
            S.op("gpsimd", "nop", waits=dws)
            for st in reversed(open_stks):
                st.close()
            b = En(nc.Block())
            S.emit(b)
            return nc

        xr = xf.rearrange("(t p) d -> t p d", p=128)

        with ExitStack() as c1:
            cc = c1.enter_context
            xst = cc(nc.sbuf_tensor("xst", [128, 8, D], F32))
            xnst = cc(nc.sbuf_tensor("xnst", [128, 2, D], BF16))
            sq_scr = cc(nc.sbuf_tensor("sq_scr", [128, D], F32))
            psT = [cc(nc.psum_tensor(f"psT{i}", [128, D], BF16)) for i in range(2)]

            for j in range(NTF // 4):
                ws = []
                if j >= 2:
                    ws = [("aA", 4 * j - 4), ("vA", 4 * j - 4)]
                S.op("sync", "dma_start",
                     out=xst[:, (4 * j) % 8:(4 * j) % 8 + 4, :],
                     in_=xr[4 * j:4 * j + 4].rearrange("t p d -> p t d"),
                     waits=ws, inc=f"xldA{j % 2}", inck=16)
            for t in range(NTF):
                j = t // 4
                wx = (f"xldA{j % 2}", 16 * (j // 2 + 1))
                S.op("vector", "reduce_sum", m_sum[:, t:t + 1], xst[:, t % 8, :],
                     axis=AX.X, waits=[wx], inc="vA")
                S.op("scalar", "activation", sq_scr[:], xst[:, t % 8, :], AF.Square,
                     accum_out=m_ssq[:, t:t + 1],
                     waits=[wx, ("gc", GC0), ("aA", t)],
                     inc="aA")
            S.op("vector", "tensor_scalar_mul", m_mean[:], m_sum[:], 1.0 / D,
                 waits=[("vA", None)], inc="vS")
            S.op("vector", "tensor_mul", m_negms[:], m_mean[:], m_mean[:],
                 waits=[("vS", None)], inc="vS")
            S.op("vector", "scalar_tensor_tensor", m_rstd[:], m_ssq[:], 1.0 / D,
                 m_negms[:], ALU.mult, ALU.subtract,
                 waits=[("aA", None), ("vS", None)], inc="vS")
            S.op("scalar", "activation", m_ssq[:], m_rstd[:], AF.Ln, bias=C_EPS,
                 waits=[("vS", None)], inc="aS")
            S.op("scalar", "activation", m_rstd[:], m_ssq[:], AF.Exp, scale=C_NHALF,
                 waits=[("aS", None)], inc="aS")
            S.op("vector", "scalar_tensor_tensor", m_negms[:], m_mean[:], -1.0,
                 m_rstd[:], ALU.mult, ALU.mult, waits=[("aS", None)], inc="vS")
            STATS1 = S.n("vS")

            for j in range(NTF // 4):
                ws = [("vA", min(NTF, 28 + 4 * j)), ("aA", min(NTF, 28 + 4 * j))]
                if j >= 2:
                    ws.append(("vN", 4 * j - 4))
                S.op("sync", "dma_start",
                     out=xst[:, (4 * j) % 8:(4 * j) % 8 + 4, :],
                     in_=xr[4 * j:4 * j + 4].rearrange("t p d -> p t d"),
                     waits=ws, inc=f"xldB{j % 2}", inck=16)
            for t in range(NTF):
                jb = t // 4
                ws = [(f"xldB{jb % 2}", 16 * (jb // 2 + 1)), ("vS", STATS1)]
                if t >= 2:
                    ws.append(("pTr", t - 1))
                S.op("vector", "tensor_scalar", xnst[:, t % 2, :], xst[:, t % 8, :],
                     m_rstd[:, t:t + 1], m_negms[:, t:t + 1], ALU.mult, ALU.add,
                     waits=ws, inc="vN")
                for dt in range(NDT):
                    ws = [("vN", S.n("vN")), ("gc", GC0)]
                    if t >= 2:
                        ws.append(("vE", t - 1))
                    S.op("tensor", "transpose",
                         psT[t % 2][:, dt * 128:(dt + 1) * 128],
                         xnst[:, t % 2, dt * 128:(dt + 1) * 128], identB[:],
                         waits=ws if dt == 0 else (),
                         inc="pTr" if dt == NDT - 1 else None)
                S.op("vector", "tensor_copy",
                     xnT[:, :, t * 128:(t + 1) * 128],
                     psT[t % 2][:].rearrange("p (dt c) -> p dt c", c=128),
                     waits=[("pTr", t + 1)], inc="vE")
        XNT = S.n("vE")

        if "xnT" in dbga:
            S.op("gpsimd", "dma_start", out=dbga["xnT"], in_=xnT[:],
                 waits=[("vE", XNT)], inc=f"dbg0", inck=16)

        if last_phase < 2:
            return finish()

        with ExitStack() as c2:
            cc = c2.enter_context
            cqp = [cc(nc.sbuf_tensor(f"cqp{i}", [128, NDT, 128], BF16)) for i in range(2)]
            ckp = [cc(nc.sbuf_tensor(f"ckp{i}", [128, NDT, 128], BF16)) for i in range(2)]
            cvp = [cc(nc.sbuf_tensor(f"cvp{i}", [128, NDT, 128], BF16)) for i in range(2)]
            QTp = [cc(nc.sbuf_tensor(f"QTp{i}", [128, OWN], BF16)) for i in range(2)]
            KTp = [cc(nc.sbuf_tensor(f"KTp{i}", [128, SEQ], BF16)) for i in range(2)]
            Vp = [cc(nc.sbuf_tensor(f"Vp{i}", [128, 32, 130], BF16)) for i in range(2)]
            PT = cc(nc.sbuf_tensor("PT", [128, 3, 1024], BF16))
            lscr = cc(nc.sbuf_tensor("lscr", [128, 2, 512], F32))
            invl = cc(nc.sbuf_tensor("invl", [128, 2, 512], F32))
            stg = cc(nc.sbuf_tensor("stg", [128, 2, 512], BF16))
            psS = [cc(nc.psum_tensor(f"psS{i}", [128, 1024], F32)) for i in range(2)]
            psO = cc(nc.psum_tensor("psO", [128, 1024], F32))
            psP = [cc(nc.psum_tensor(f"psP{i}", [128, 512], F32)) for i in range(2)]

            for i in range(2):
                S.op("gpsimd", "memset", Vp[i][:], 1.0,
                     waits=[("vE", XNT)], inc="gc")
            GV = S.n("gc")

            groups = [(2 * g, 2) for g in range(16)]
            UPC = 32
            pair_end = {}
            prev_aE = [0]
            prev_recip = [0]
            proj_pP = {}
            proj_done = {}
            wpair_lv = {}

            def emit_wdma(p):
                bp_ = p % 2
                colsl_ = slice(128 * p, 128 * (p + 1))
                ws = [("g0a", None), ("g0b", None), ("gc", GV), ("vE", XNT)]
                if p >= 2:
                    ws.append(("pP", proj_pP[p - 2]))
                for cn, cmat, dst in (("q", cq, cqp[bp_]), ("k", ck, ckp[bp_]),
                                      ("v", cv, cvp[bp_])):
                    S.op("gpsimd", "dma_start", out=dst[:],
                         in_=cmat.rearrange("(t p) n -> p t n", p=128)[:, :, colsl_],
                         waits=ws, inc=f"wld{cn}{bp_}", inck=16)
                    ws = []
                wpair_lv[p] = [(f"wld{cn}{bp_}", 16 * (p // 2 + 1))
                               for cn in ("q", "k", "v")]

            def make_jobs(p):
                return [(p, "q", c) for c in range(NCH)] + \
                       [(p, "k", kc) for kc in range(NKC)] + \
                       [(p, "v", kt) for kt in range(32)]

            class ProjQ:

                def __init__(self, p, gates):
                    self.p = p
                    self.jobs = make_jobs(p)
                    self.j = 0
                    self.vp0 = S.n("vP")
                    self.gates = gates

                def emit_one(self):
                    p, kind, idx = self.jobs[self.j]
                    bp_ = p % 2
                    pj = psP[self.j % 2][:]
                    ws = list(wpair_lv[p])
                    if self.j >= 2:
                        ws.append(("vP", self.vp0 + self.j - 1))
                    ev_ws = [("pP", None)] + self.gates
                    if kind == "q":
                        for dt in range(NDT):
                            S.op("tensor", "matmul", pj,
                                 cqp[bp_][:, dt, :],
                                 xnT[:, dt, idx * 512:(idx + 1) * 512],
                                 start=(dt == 0), stop=(dt == NDT - 1),
                                 waits=ws if dt == 0 else (),
                                 inc="pP" if dt == NDT - 1 else None)
                        S.op("vector", "tensor_scalar",
                             QTp[bp_][:, idx * 512:(idx + 1) * 512], pj,
                             biasT[:, p:p + 1], None, ALU.add,
                             waits=ev_ws, inc="vP")
                    elif kind == "k":
                        for dt in range(NDT):
                            S.op("tensor", "matmul", pj,
                                 ckp[bp_][:, dt, :],
                                 xnT[:, dt, idx * 512:(idx + 1) * 512],
                                 start=(dt == 0), stop=(dt == NDT - 1),
                                 waits=ws if dt == 0 else (),
                                 inc="pP" if dt == NDT - 1 else None)
                        S.op("vector", "tensor_scalar",
                             KTp[bp_][:, idx * 512:(idx + 1) * 512], pj,
                             biasT[:, NDT + p:NDT + p + 1], None, ALU.add,
                             waits=ev_ws, inc="vP")
                    else:
                        pv = pj[:, 0:128]
                        for dt in range(NDT):
                            S.op("tensor", "matmul", pv,
                                 xnT[:, dt, idx * 128:(idx + 1) * 128],
                                 cvp[bp_][:, dt, :],
                                 start=(dt == 0), stop=(dt == NDT - 1),
                                 waits=ws if dt == 0 else (),
                                 inc="pP" if dt == NDT - 1 else None)
                        dstv = Vp[bp_][:, idx, :].rearrange(
                            "p (a b) -> p a b", b=65)[:, :, 0:64]
                        srcv = pv.rearrange("p (a b) -> p a b", b=64)
                        S.op("vector", "tensor_copy", dstv, srcv,
                             waits=ev_ws, inc="vP")
                    self.j += 1

                def drain_to(self, frac):
                    tgt = min(len(self.jobs), int(len(self.jobs) * frac + 0.999))
                    while self.j < tgt:
                        self.emit_one()

                def finish(self):
                    while self.j < len(self.jobs):
                        self.emit_one()
                    proj_pP[self.p] = S.n("pP")
                    proj_done[self.p] = S.n("vP")

            emit_wdma(0)
            if NPAIR > 1:
                emit_wdma(1)
            q0 = ProjQ(0, [])
            q0.finish()

            for p in range(NPAIR):
                bp = p % 2
                if p + 2 < NPAIR:
                    emit_wdma(p + 2)
                if p + 1 < NPAIR:
                    nq = ProjQ(p + 1, [("pQK", pair_end[p - 1]["pQK"]),
                                       ("pPV", pair_end[p - 1]["pPV"])]
                               if p > 0 else [])
                else:
                    nq = None

                exp_at = {}
                pv_at = {}
                recip_at = {}
                evac_at = {}
                pvb_free = {}
                pend = []

                def emit_pv(job):
                    u, cc_, pts, h = job
                    ws = [("aE", exp_at[u])]
                    if u % UPC <= 1:
                        if cc_ > 0:
                            gate = evac_at[cc_ - 1] if h == 0 else pvb_free[cc_ - 1]
                        else:
                            gate = S.n("vT")
                        ws.append(("vT", gate))
                    vsl = slice(65 * h, 65 * h + 65)
                    k0_, glen_ = groups[(u % UPC) // 2]
                    for i in range(glen_):
                        kt = k0_ + i
                        S.op("tensor", "matmul",
                             psO[0:65, 512 * h:512 * h + 512],
                             Vp[bp][:, kt, vsl],
                             PT[:, pts, i * 512:(i + 1) * 512],
                             start=(kt == 0), stop=(kt == 31),
                             waits=ws if i == 0 else (),
                             inc="pPV" if i == glen_ - 1 else None)
                    pv_at[u] = S.n("pPV")

                for c in range(NCH):
                    qsl = slice(c * 512, (c + 1) * 512)
                    for gi, (k0, glen) in enumerate(groups):
                        ua = c * UPC + 2 * gi
                        ub = ua + 1
                        wsa = []
                        wsb = []
                        if ua >= 2:
                            wsa.append(("aE", exp_at[ua - 2]))
                            wsb.append(("aE", exp_at[ub - 2]))
                        else:
                            wsa = [("aE", prev_aE[0]), ("vP", proj_done[p])]
                            wsb = [("aE", prev_aE[0]), ("vP", proj_done[p]),
                                   ("vT", prev_recip[0])]
                        if c > 0 and ua == c * UPC:
                            wsb.append(("vT", recip_at[c - 1]))
                        for h, slot, ws0 in ((0, 0, wsa), (1, 1, wsb)):
                            rowsl = slice(64 * h, 64 * (h + 1))
                            for i in range(glen):
                                kt = k0 + i
                                S.op("tensor", "matmul",
                                     psS[slot][:, i * 512:(i + 1) * 512],
                                     KTp[bp][rowsl, kt * 128:(kt + 1) * 128],
                                     QTp[bp][rowsl, qsl],
                                     start=True, stop=True,
                                     tile_position=(64 * h, 0),
                                     waits=ws0 if i == 0 else (),
                                     inc="pQK" if i == glen - 1 else None)
                        nqk_b = S.n("pQK")
                        while pend:
                            emit_pv(pend.pop(0))
                        for h, u in ((0, ua), (1, ub)):
                            ws = [("pQK", nqk_b - (1 if h == 0 else 0))]
                            if u >= 3:
                                ws.append(("pPV", pv_at[u - 3]))
                            S.op("scalar", "activation",
                                 PT[:, u % 3, 0:glen * 512],
                                 psS[u % 2][:, 0:glen * 512],
                                 AF.Exp, bias=C_M12, scale=C_EIGHTH,
                                 waits=ws, inc="aE")
                            exp_at[u] = S.n("aE")
                        pend.append((ua, c, ua % 3, 0))
                        pend.append((ub, c, ub % 3, 1))
                        if nq is not None:
                            nq.drain_to((2 * gi + 2) / UPC / NCH + c / NCH)
                    while pend:
                        emit_pv(pend.pop(0))
                    S.op("vector", "tensor_copy", lscr[64:65, 0, :],
                         psO[64:65, 0:512],
                         waits=[("pPV", None), ("pT", None)], inc="vT")
                    S.op("vector", "tensor_copy", lscr[64:65, 1, :],
                         psO[64:65, 512:1024], inc="vT")
                    S.op("tensor", "matmul", psS[1][0:64, 0:512],
                         onescolF[64:65, 0:64], lscr[64:65, 0, :],
                         start=True, stop=True,
                         waits=[("vT", None), ("aE", None)], inc="pT")
                    S.op("tensor", "matmul", psS[1][0:64, 512:1024],
                         onescolF[64:65, 0:64], lscr[64:65, 1, :],
                         start=True, stop=True, inc="pT")
                    S.op("vector", "reciprocal",
                         invl[0:64, :, :].rearrange("p a b -> p (a b)"),
                         psS[1][0:64, 0:1024],
                         waits=[("pT", None)], inc="vT")
                    recip_at[c] = S.n("vT")
                    S.op("vector", "tensor_tensor", stg[0:64, 0, :],
                         psO[0:64, 0:512], invl[0:64, 0, :], ALU.mult,
                         waits=[("vT", None)], inc="vT")
                    S.op("vector", "tensor_scalar", outT[0:64, p, qsl],
                         stg[0:64, 0, :],
                         biasT[0:64, 2 * NDT + p:2 * NDT + p + 1], None, ALU.add,
                         waits=[("vT", None)], inc="vT")
                    AEV = S.n("vT")
                    S.op("vector", "tensor_tensor", stg[0:64, 1, :],
                         psO[0:64, 512:1024], invl[0:64, 1, :], ALU.mult,
                         inc="vT")
                    pvb_free[c] = S.n("vT")
                    S.op("tensor", "matmul", psO[:, 0:512], shift64[0:64, 0:128],
                         stg[0:64, 1, :], start=True, stop=True,
                         waits=[("vT", None)], inc="pT")
                    S.op("vector", "tensor_scalar", outT[64:128, p, qsl],
                         psO[64:128, 0:512],
                         biasT[64:128, 2 * NDT + p:2 * NDT + p + 1], None,
                         ALU.add, waits=[("pT", None)], inc="vT")
                    evac_at[c] = S.n("vT")
                if nq is not None:
                    nq.finish()
                pair_end[p] = {"pQK": S.n("pQK"), "pPV": S.n("pPV"),
                               "vT": S.n("vT")}
                prev_aE[0] = S.n("aE")
                prev_recip[0] = recip_at[NCH - 1]

                if p == 0:
                    if "qt" in dbga:
                        S.op("gpsimd", "dma_start", out=dbga["qt"], in_=QTp[0][:],
                             waits=[("vP", proj_done[0])], inc="dbg90", inck=16)
                    if "kt" in dbga:
                        S.op("gpsimd", "dma_start", out=dbga["kt"], in_=KTp[0][:],
                             waits=[("vP", proj_done[0])], inc="dbg91", inck=16)
                    if "vt" in dbga:
                        S.op("gpsimd", "dma_start", out=dbga["vt"],
                             in_=Vp[0][:].rearrange("p a b -> p (a b)"),
                             waits=[("vP", proj_done[0])], inc="dbg92", inck=16)
            ATT = S.counts()

        if "outT" in dbga:
            S.op("gpsimd", "dma_start", out=dbga["outT"], in_=outT[:],
                 waits=[("vT", ATT["vT"])], inc=f"dbg4", inck=16)

        xnT_stk.close()
        open_stks.remove(xnT_stk)

        if last_phase < 3:
            return finish()

        with ExitStack() as c3:
            cc = c3.enter_context
            co_sb = cc(nc.sbuf_tensor("co_sb", [128, NDT, D], BF16))
            aost = cc(nc.sbuf_tensor("aost", [128, NDT, 512], BF16))
            xo = cc(nc.sbuf_tensor("xo", [128, 8, D], F32))
            x1t = cc(nc.sbuf_tensor("x1t", [128, NTO, D], F32))
            xn2st = cc(nc.sbuf_tensor("xn2st", [128, 2, D], BF16))
            sq2 = cc(nc.sbuf_tensor("sq2", [128, D], F32))
            psC = [cc(nc.psum_tensor(f"psC{i}", [128, 512], F32)) for i in range(2)]
            psT3 = [cc(nc.psum_tensor(f"psT3{i}", [128, D], BF16)) for i in range(2)]

            S.op("gpsimd", "dma_start", out=co_sb[:],
                 in_=co.rearrange("(t p) n -> p t n", p=128),
                 waits=[("vT", ATT["vT"]), ("g0a", None), ("g0b", None)],
                 inc="wldc", inck=16)
            def load_xo(j):
                ws = [("vT", ATT["vT"])] if j <= 1 else [("vAdd", 4 * (j - 1))]
                S.op("sync", "dma_start",
                     out=xo[:, (4 * j) % 8:(4 * j) % 8 + 4, :],
                     in_=xr[4 * j:4 * j + 4].rearrange("t p d -> p t d"),
                     waits=ws, inc=f"xoL{j % 2}", inck=16)

            load_xo(0)
            load_xo(1)

            for c in range(NCH):
                qsl = slice(c * 512, (c + 1) * 512)
                for dout in range(NDT):
                    pj = psC[dout % 2][:]
                    ws = [("wldc", 16), ("vT", ATT["vT"])]
                    if c * NDT + dout >= 2:
                        ws.append(("aCo", S.n("aCo") - 1))
                    for dt in range(NDT):
                        S.op("tensor", "matmul", pj,
                             co_sb[:, dt, dout * 128:(dout + 1) * 128],
                             outT[:, dt, qsl],
                             start=(dt == 0), stop=(dt == NDT - 1),
                             waits=ws if dt == 0 else (),
                             inc="pCo" if dt == NDT - 1 else None)
                    S.op("scalar", "activation", aost[:, dout, :], pj, AF.Identity,
                         bias=biasT[:, 3 * NDT + dout:3 * NDT + dout + 1],
                         waits=[("pCo", None), ("pT3", None)], inc="aCo")
                ACO = S.n("aCo")
                for tt in range(4):
                    T = c * 4 + tt
                    for dt in range(NDT):
                        ws = [("aCo", ACO), ("vAdd", max(0, T - 1))]
                        S.op("tensor", "transpose",
                             psT3[T % 2][:, dt * 128:(dt + 1) * 128],
                             aost[:, dt, tt * 128:(tt + 1) * 128], identB[:],
                             waits=ws if dt == 0 else (),
                             inc="pT3" if dt == NDT - 1 else None)
                    S.op("vector", "tensor_tensor", x1t[:, T, :], xo[:, T % 8, :],
                         psT3[T % 2][:], ALU.add,
                         waits=[("pT3", None),
                                (f"xoL{(T // 4) % 2}", 16 * (T // 4 // 2 + 1))],
                         inc="vAdd")
                    S.op("sync", "dma_start", out=x1s[T], in_=x1t[:, T, :],
                         waits=[("vAdd", None)], inc=f"stX{T % 2}", inck=16)
                    S.op("vector", "reduce_sum", m_sum[:, T:T + 1], x1t[:, T, :],
                         axis=AX.X, waits=[("vAdd", None)], inc="vL")
                    S.op("scalar", "activation", sq2[:], x1t[:, T, :], AF.Square,
                         accum_out=m_ssq[:, T:T + 1],
                         waits=[("vAdd", None), ("aL", None)], inc="aL")
                if c + 2 < NCH:
                    load_xo(c + 2)
            s16 = slice(0, NTO)
            S.op("vector", "tensor_scalar_mul", m_mean[:, s16], m_sum[:, s16],
                 1.0 / D, waits=[("vL", None)], inc="vS2")
            S.op("vector", "tensor_mul", m_negms[:, s16], m_mean[:, s16],
                 m_mean[:, s16], waits=[("vS2", None)], inc="vS2")
            S.op("vector", "scalar_tensor_tensor", m_rstd[:, s16], m_ssq[:, s16],
                 1.0 / D, m_negms[:, s16], ALU.mult, ALU.subtract,
                 waits=[("aL", None), ("vS2", None)], inc="vS2")
            S.op("scalar", "activation", m_ssq[:, s16], m_rstd[:, s16], AF.Ln,
                 bias=C_EPS, waits=[("vS2", None)], inc="aS2")
            S.op("scalar", "activation", m_rstd[:, s16], m_ssq[:, s16], AF.Exp,
                 scale=C_NHALF, waits=[("aS2", None)], inc="aS2")
            S.op("vector", "scalar_tensor_tensor", m_negms[:, s16], m_mean[:, s16],
                 -1.0, m_rstd[:, s16], ALU.mult, ALU.mult,
                 waits=[("aS2", None)], inc="vS2")
            STATS2 = S.n("vS2")
            for T in range(NTO):
                ws = [("vS2", STATS2)]
                if T >= 2:
                    ws.append(("pT4", T - 1))
                S.op("vector", "tensor_scalar", xn2st[:, T % 2, :], x1t[:, T, :],
                     m_rstd[:, T:T + 1], m_negms[:, T:T + 1], ALU.mult, ALU.add,
                     waits=ws, inc="vN2")
                for dt in range(NDT):
                    ws = [("vN2", S.n("vN2"))]
                    if T >= 2:
                        ws.append(("vE2", T - 1))
                    S.op("tensor", "transpose",
                         psT3[T % 2][:, dt * 128:(dt + 1) * 128],
                         xn2st[:, T % 2, dt * 128:(dt + 1) * 128], identB[:],
                         waits=ws if dt == 0 else (),
                         inc="pT4" if dt == NDT - 1 else None)
                S.op("vector", "tensor_copy",
                     xn2T[:, :, T * 128:(T + 1) * 128],
                     psT3[T % 2][:].rearrange("p (dt c2) -> p dt c2", c2=128),
                     waits=[("pT4", T + 1)], inc="vE2")
            P3 = S.counts()

        if "xn2T" in dbga:
            S.op("gpsimd", "dma_start", out=dbga["xn2T"], in_=xn2T[:],
                 waits=[("vE2", P3["vE2"])], inc=f"dbg5", inck=16)

        outT_stk.close()
        open_stks.remove(outT_stk)

        if last_phase < 4:
            return finish()

        with ExitStack() as c4:
            cc = c4.enter_context
            identF = cc(nc.sbuf_tensor("identF", [128, 128], F32))
            lg_sb = cc(nc.sbuf_tensor("lg_sb", [8, NCH, 512], F32))
            lgt = cc(nc.sbuf_tensor("lgt", [128, NTO, E], F32))
            rt_is1 = cc(nc.sbuf_tensor("rt_is1", [128, NTO, E], F32))
            rt_is2 = cc(nc.sbuf_tensor("rt_is2", [128, NTO, E], F32))
            rt_sh = cc(nc.sbuf_tensor("rt_sh", [128, NTO, E], F32))
            rt_m1 = cc(nc.sbuf_tensor("rt_m1", [128, NTO], F32))
            rt_m2 = cc(nc.sbuf_tensor("rt_m2", [128, NTO], F32))
            rt_z = cc(nc.sbuf_tensor("rt_z", [128, NTO], F32))
            rt_e2 = cc(nc.sbuf_tensor("rt_e2", [128, NTO], F32))
            rt_w1 = cc(nc.sbuf_tensor("rt_w1", [128, NTO], F32))
            rt_w2 = cc(nc.sbuf_tensor("rt_w2", [128, NTO], F32))
            comb_bf = cc(nc.sbuf_tensor("comb_bf", [128, NTO, E], BF16))
            psG = cc(nc.psum_tensor("psG", [8, NCH * 512], F32))
            psR = cc(nc.psum_tensor("psR", [128, NTO * 8], F32))
            psR2 = cc(nc.psum_tensor("psR2", [8, NTO * 128], BF16))

            S.op("gpsimd", "memset", identF[:], 0.0,
                 waits=[("vE2", P3["vE2"]), ("stX0", 16 * (NTO // 2)),
                        ("stX1", 16 * (NTO // 2))], inc="gF")
            S.op("gpsimd", "affine_select", out=identF[:], in_=identF[:],
                 compare_op=ALU.not_equal, fill=1.0, base=0,
                 pattern=[[-1, 128]], channel_multiplier=1,
                 waits=[("gF", None)], inc="gF")

            for ch in range(NCH):
                for dt in range(NDT):
                    S.op("tensor", "matmul", psG[:, ch * 512:(ch + 1) * 512],
                         gw_sb[:, dt, :], xn2T[:, dt, ch * 512:(ch + 1) * 512],
                         start=(dt == 0), stop=(dt == NDT - 1),
                         waits=[("vE2", P3["vE2"]), ("g0b", None)]
                         if (ch == 0 and dt == 0) else (),
                         inc="pG" if dt == NDT - 1 else None)
            S.op("vector", "tensor_copy",
                 lg_sb[:].rearrange("p a b -> p (a b)"), psG[:],
                 waits=[("pG", None)], inc="vR")
            lg2 = lg_sb[:].rearrange("p a b -> p (a b)")
            for T in range(NTO):
                S.op("tensor", "transpose", psR[:, T * 8:(T + 1) * 8],
                     lg2[0:8, T * 128:(T + 1) * 128], identF[0:8, 0:8],
                     waits=[("vR", None), ("gF", None)] if T == 0 else (),
                     inc="pR" if T == NTO - 1 else None)
            S.op("vector", "tensor_copy", lgt[:].rearrange("p a b -> p (a b)"),
                 psR[:], waits=[("pR", None)], inc="vR")
            for T in range(NTO):
                S.op("vector", "reduce_max", rt_m1[:, T:T + 1], lgt[:, T, :],
                     axis=AX.X, waits=[("vR", None)], inc="vR")
                S.op("vector", "tensor_scalar", rt_is1[:, T, :], lgt[:, T, :],
                     rt_m1[:, T:T + 1], None, ALU.is_ge,
                     waits=[("vR", None)], inc="vR")
                S.op("vector", "scalar_tensor_tensor", rt_sh[:, T, :],
                     rt_is1[:, T, :], -1e30, lgt[:, T, :], ALU.mult, ALU.add,
                     waits=[("vR", None)], inc="vR")
                S.op("vector", "reduce_max", rt_m2[:, T:T + 1], rt_sh[:, T, :],
                     axis=AX.X, waits=[("vR", None)], inc="vR")
                S.op("vector", "tensor_scalar", rt_is2[:, T, :], rt_sh[:, T, :],
                     rt_m2[:, T:T + 1], None, ALU.is_ge,
                     waits=[("vR", None)], inc="vR")
                S.op("vector", "tensor_scalar", rt_sh[:, T, :], lgt[:, T, :],
                     rt_m1[:, T:T + 1], None, ALU.subtract,
                     waits=[("vR", None)], inc="vR")
            S.op("scalar", "activation", rt_sh[:], rt_sh[:], AF.Exp,
                 waits=[("vR", None), ("aL", P3["aL"])], inc="aR")
            S.op("vector", "reduce_sum", rt_z[:], rt_sh[:], axis=AX.X,
                 waits=[("aR", None)], inc="vR")
            S.op("vector", "tensor_sub", rt_e2[:], rt_m2[:], rt_m1[:],
                 waits=[("vR", None)], inc="vR")
            S.op("scalar", "activation", rt_e2[:], rt_e2[:], AF.Exp,
                 waits=[("vR", None)], inc="aR")
            S.op("vector", "scalar_tensor_tensor", rt_w1[:], rt_z[:], 1e-20,
                 rt_e2[:], ALU.mult, ALU.add,
                 waits=[("aR", None), ("vR", None)], inc="vR")
            S.op("vector", "tensor_scalar", rt_w1[:], rt_w1[:], 1.0, None, ALU.add,
                 waits=[("vR", None)], inc="vR")
            S.op("vector", "reciprocal", rt_w1[:], rt_w1[:],
                 waits=[("vR", None)], inc="vR")
            S.op("vector", "tensor_scalar", rt_w1[:], rt_w1[:], 2.5, None, ALU.mult,
                 waits=[("vR", None)], inc="vR")
            S.op("vector", "tensor_mul", rt_w2[:], rt_e2[:], rt_w1[:],
                 waits=[("vR", None)], inc="vR")
            b1, b2 = bass.broadcast_tensor_aps(
                rt_is1[:], rt_w1[:].rearrange("p (t o) -> p t o", o=1))
            S.op("vector", "tensor_tensor", rt_is1[:], b1, b2, ALU.mult,
                 waits=[("vR", None)], inc="vR")
            b1, b2 = bass.broadcast_tensor_aps(
                rt_is2[:], rt_w2[:].rearrange("p (t o) -> p t o", o=1))
            S.op("vector", "tensor_tensor", rt_is2[:], b1, b2, ALU.mult,
                 waits=[("vR", None)], inc="vR")
            S.op("vector", "tensor_add", lgt[:], rt_is1[:], rt_is2[:],
                 waits=[("vR", None)], inc="vR")
            if "comb" in dbga:
                S.op("gpsimd", "dma_start", out=dbga["comb"],
                     in_=lgt[:].rearrange("p a b -> p (a b)"),
                     waits=[("vR", None)], inc=f"dbg6", inck=16)
            S.op("vector", "tensor_copy", comb_bf[:], lgt[:],
                 waits=[("vR", None)], inc="vR")
            for T in range(NTO):
                S.op("tensor", "transpose", psR2[:, T * 128:(T + 1) * 128],
                     comb_bf[:, T, :], identB[:],
                     waits=[("vR", None)] if T == 0 else (),
                     inc="pR2" if T == NTO - 1 else None)
            S.op("vector", "tensor_copy", cT_sb[:], psR2[:],
                 waits=[("pR2", None)], inc="vR")
            P4 = S.counts()

        if last_phase < 5:
            return finish()

        yacc = En(nc.sbuf_tensor("yacc", [128, NDT, OWN], F32))
        with ExitStack() as c5:
            cc = c5.enter_context
            wgh = [cc(nc.sbuf_tensor(f"wgh{i}", [128, NDT, 768], BF16)) for i in range(2)]
            wuh = [cc(nc.sbuf_tensor(f"wuh{i}", [128, NDT, 768], BF16)) for i in range(2)]
            wdh = [cc(nc.sbuf_tensor(f"wdh{i}", [128, 6, D], BF16)) for i in range(2)]
            prod = cc(nc.sbuf_tensor("prod", [128, 6, OWN // 2], BF16))
            gs_sb = cc(nc.sbuf_tensor("gs_sb", [128, 2, 512], BF16))
            us_sb = cc(nc.sbuf_tensor("us_sb", [128, 2, 512], BF16))
            ce_sb = cc(nc.sbuf_tensor("ce_sb", [128, OWN], BF16))
            psGU = [cc(nc.psum_tensor(f"psGU{i}", [128, 512], F32)) for i in range(4)]
            psD = [cc(nc.psum_tensor(f"psD{i}", [128, 512], F32)) for i in range(2)]
            psCE = cc(nc.psum_tensor("psCE", [128, 512], F32))

            unit_end = {}
            jig = 0
            for ui, (kind, e, f0, nf) in enumerate(FFN_UNITS):
                bu = ui % 2
                ws = [("vT", ATT["vT"])]
                if ui >= 2:
                    ws.append(("pD", unit_end[ui - 2]["pD"]))
                    ws.append(("pGU", unit_end[ui - 2]["pGU"]))
                if kind == "moe":
                    srcg = wg[e].rearrange("(t p) f -> p t f", p=128)[:, :, f0 * 128:(f0 + nf) * 128]
                    srcu = wu[e].rearrange("(t p) f -> p t f", p=128)[:, :, f0 * 128:(f0 + nf) * 128]
                    srcd = wd[e, f0 * 128:(f0 + nf) * 128, :].rearrange("(t p) d2 -> p t d2", p=128)
                else:
                    srcg = swg.rearrange("(t p) f -> p t f", p=128)[:, :, f0 * 128:(f0 + nf) * 128]
                    srcu = swu.rearrange("(t p) f -> p t f", p=128)[:, :, f0 * 128:(f0 + nf) * 128]
                    srcd = swd[f0 * 128:(f0 + nf) * 128, :].rearrange("(t p) d2 -> p t d2", p=128)
                S.op("gpsimd", "dma_start", out=wgh[bu][:, :, 0:nf * 128], in_=srcg,
                     waits=ws, inc=f"wfg{bu}", inck=16)
                S.op("gpsimd", "dma_start", out=wuh[bu][:, :, 0:nf * 128], in_=srcu,
                     inc=f"wfu{bu}", inck=16)
                S.op("gpsimd", "dma_start", out=wdh[bu][:, 0:nf, :], in_=srcd,
                     inc=f"wfd{bu}", inck=16)
                lvl = 16 * (ui // 2 + 1)
                WUg = (f"wfg{bu}", lvl)
                WUu = (f"wfu{bu}", lvl)
                WUd = (f"wfd{bu}", lvl)

                if kind == "moe" and f0 == 0:
                    for ch in range(NCH):
                        ws = [("vUs", None), ("vCE", None)]
                        if ui == 0 and ch == 0:
                            ws += [("vR", P4["vR"])]
                        S.op("tensor", "matmul", psCE[:],
                             selmat[:, e, :], cT_sb[:, ch * 512:(ch + 1) * 512],
                             start=True, stop=True, waits=ws, inc="pCE")
                        S.op("vector", "tensor_copy",
                             ce_sb[:, ch * 512:(ch + 1) * 512], psCE[:],
                             waits=[("pCE", None)], inc="vCE")
                CE = S.n("vCE")

                down_q = []
                dji = [0]

                def emit_down(job):
                    dout, ch_, chl_, lvl = job
                    pd = psD[dji[0] % 2][:]
                    ws = [("vPr", lvl), WUd]
                    if ui == 0 and dji[0] < 2:
                        ws.append(("vR", P4["vR"]))
                    if ui > 0 or dji[0] >= 2:
                        ws.append(("vF2", S.n("vF2") - 1))
                    for ft_ in range(nf):
                        S.op("tensor", "matmul", pd,
                             wdh[bu][:, ft_, dout * 128:(dout + 1) * 128],
                             prod[:, ft_, chl_ * 512:(chl_ + 1) * 512],
                             start=(ft_ == 0), stop=(ft_ == nf - 1),
                             waits=ws if ft_ == 0 else (),
                             inc="pD" if ft_ == nf - 1 else None)
                    if ui == 0:
                        S.op("vector", "tensor_copy",
                             yacc[:, dout, ch_ * 512:(ch_ + 1) * 512], pd,
                             waits=[("pD", None)], inc="vF2")
                    else:
                        S.op("vector", "tensor_tensor",
                             yacc[:, dout, ch_ * 512:(ch_ + 1) * 512],
                             yacc[:, dout, ch_ * 512:(ch_ + 1) * 512], pd,
                             ALU.add, waits=[("pD", None), ("vF2", None)],
                             inc="vF2")
                    dji[0] += 1

                for th in range(2):
                    for chl in range(2):
                        ch = 2 * th + chl
                        for ft in range(nf):
                            pg = psGU[(2 * jig) % 4][:]
                            pu = psGU[(2 * jig + 1) % 4][:]
                            ws = [WUg, ("vE2", P3["vE2"])]
                            if ui == 0 and th == 0 and ft == 0 and chl == 0:
                                ws.append(("vR", P4["vR"]))
                            if jig >= 2:
                                ws.append(("aSil", jig - 1))
                            for dt in range(NDT):
                                S.op("tensor", "matmul", pg,
                                     wgh[bu][:, dt, ft * 128:(ft + 1) * 128],
                                     xn2T[:, dt, ch * 512:(ch + 1) * 512],
                                     start=(dt == 0), stop=(dt == NDT - 1),
                                     waits=ws if dt == 0 else (),
                                     inc="pGU" if dt == NDT - 1 else None)
                            ws2 = [WUu]
                            if jig >= 2:
                                ws2.append(("vUs", jig - 1))
                            for dt in range(NDT):
                                S.op("tensor", "matmul", pu,
                                     wuh[bu][:, dt, ft * 128:(ft + 1) * 128],
                                     xn2T[:, dt, ch * 512:(ch + 1) * 512],
                                     start=(dt == 0), stop=(dt == NDT - 1),
                                     waits=ws2 if dt == 0 else (),
                                     inc="pGU" if dt == NDT - 1 else None)
                            PGU = S.n("pGU")
                            S.op("scalar", "activation", gs_sb[:, jig % 2, :], pg,
                                 AF.Silu,
                                 waits=[("pGU", PGU - 1), ("vPr", max(0, jig - 1))],
                                 inc="aSil")
                            if kind == "moe":
                                S.op("vector", "tensor_tensor", us_sb[:, jig % 2, :],
                                     pu, ce_sb[:, ch * 512:(ch + 1) * 512], ALU.mult,
                                     waits=[("pGU", PGU), ("vCE", CE),
                                            ("vPr", max(0, S.n("vPr") - 1))],
                                     inc="vUs")
                            else:
                                S.op("vector", "tensor_copy", us_sb[:, jig % 2, :],
                                     pu,
                                     waits=[("pGU", PGU),
                                            ("vPr", max(0, S.n("vPr") - 1))],
                                     inc="vUs")
                            S.op("vector", "tensor_mul",
                                 prod[:, ft, chl * 512:(chl + 1) * 512],
                                 gs_sb[:, jig % 2, :], us_sb[:, jig % 2, :],
                                 waits=[("aSil", None), ("vUs", None), ("pD", None)],
                                 inc="vPr")
                            jig += 1
                            while down_q and len(down_q) > (nf - 1 - ft) * 2 - 1:
                                emit_down(down_q.pop(0))
                        lvl = S.n("vPr")
                        while down_q:
                            emit_down(down_q.pop(0))
                        down_q = [(dout, 2 * th + chl, chl, lvl)
                                  for dout in range(NDT)]
                while down_q:
                    emit_down(down_q.pop(0))
                unit_end[ui] = {"pD": S.n("pD"), "pGU": S.n("pGU")}
            P5 = S.counts()

        if "yacc" in dbga:
            S.op("gpsimd", "dma_start", out=dbga["yacc"], in_=yacc[:],
                 waits=[("vF2", P5["vF2"])], inc=f"dbg7", inck=16)

        if last_phase < 6:
            return finish()

        with ExitStack() as c6:
            cc = c6.enter_context
            ybf = cc(nc.sbuf_tensor("ybf", [128, NDT, 512], BF16))
            x1r = cc(nc.sbuf_tensor("x1r", [128, 4, D], F32))
            obuf = cc(nc.sbuf_tensor("obuf", [128, 2, D], F32))
            psT6 = [cc(nc.psum_tensor(f"psT6{i}", [128, D], BF16)) for i in range(2)]

            outr = out.rearrange("(t p) d -> t p d", p=128)

            def load_x1(T, ws):
                S.op("sync", "dma_start", out=x1r[:, T % 4, :], in_=x1s[T],
                     waits=ws, inc=f"x1L{T % 4}", inck=16)

            for T in range(4):
                load_x1(T, [("stX0", 16 * (NTO // 2)), ("stX1", 16 * (NTO // 2)),
                            ("vF2", P5["vF2"]), ("aSil", P5["aSil"]),
                            ("vPr", P5["vPr"])])
            for c in range(NCH):
                S.op("vector", "tensor_copy", ybf[:],
                     yacc[:, :, c * 512:(c + 1) * 512],
                     waits=[("vF2", P5["vF2"]), ("vO", None)], inc="vY")
                for tt in range(4):
                    T = c * 4 + tt
                    for dt in range(NDT):
                        ws = [("vY", None), ("vF2", P5["vF2"])]
                        if T >= 2:
                            ws.append(("vO", T - 1))
                        S.op("tensor", "transpose",
                             psT6[T % 2][:, dt * 128:(dt + 1) * 128],
                             ybf[:, dt, tt * 128:(tt + 1) * 128], identB[:],
                             waits=ws if dt == 0 else (),
                             inc="pT6" if dt == NDT - 1 else None)
                    S.op("vector", "tensor_tensor", obuf[:, T % 2, :],
                         x1r[:, T % 4, :], psT6[T % 2][:], ALU.add,
                         waits=[("pT6", None), (f"x1L{T % 4}", 16 * (T // 4 + 1)),
                                (f"stO{T % 2}", 16 * (T // 2))], inc="vO")
                    S.op("sync", "dma_start", out=outr[T], in_=obuf[:, T % 2, :],
                         waits=[("vO", None)], inc=f"stO{T % 2}", inck=16)
                    if T + 4 < NTO:
                        load_x1(T + 4, [("vO", None)])
            fin = [(d, None) for d in sorted(S.dma_sems)]
            S.op("sync", "nop", waits=fin)
            S.op("gpsimd", "nop", waits=fin)

        blk = En(nc.Block())
        S.emit(blk)

    return nc


def _circulant(c):
    c = np.asarray(c, np.float32)
    idx = (np.arange(D)[None, :] - np.arange(D)[:, None]) % D
    return np.ascontiguousarray(c[idx])


def _biasT(b):
    return np.ascontiguousarray(np.asarray(b, np.float32).reshape(NDT, 128).T)


def make_inputs(inputs):
    x = np.ascontiguousarray(np.asarray(inputs["L0_attn_x"], np.float32))
    base = {
        "cq": _circulant(inputs["L0_attn_wq_c"]),
        "ck": _circulant(inputs["L0_attn_wk_c"]),
        "cv": _circulant(inputs["L0_attn_wv_c"]),
        "co": _circulant(inputs["L0_attn_wo_c"]),
        "bT": np.ascontiguousarray(np.concatenate(
            [_biasT(inputs["L0_attn_wq_b"]), _biasT(inputs["L0_attn_wk_b"]),
             _biasT(inputs["L0_attn_wv_b"]), _biasT(inputs["L0_attn_wo_b"])],
            axis=1)),
        "gwT": np.ascontiguousarray(
            np.asarray(inputs["L0_ffn_gate_w"], np.float32).T),
        "wg": np.ascontiguousarray(np.asarray(inputs["L0_ffn_w_gate"], np.float32)),
        "wu": np.ascontiguousarray(np.asarray(inputs["L0_ffn_w_up"], np.float32)),
        "wd": np.ascontiguousarray(np.asarray(inputs["L0_ffn_w_down"], np.float32)),
        "swg": np.ascontiguousarray(np.asarray(inputs["L0_ffn_sw_gate"], np.float32)),
        "swu": np.ascontiguousarray(np.asarray(inputs["L0_ffn_sw_up"], np.float32)),
        "swd": np.ascontiguousarray(np.asarray(inputs["L0_ffn_sw_down"], np.float32)),
    }
    in_maps = []
    for core in range(8):
        b, h = core // 2, core % 2
        xb = x[b]
        m = dict(base)
        m["xf"] = np.ascontiguousarray(
            np.concatenate([xb[h * OWN:(h + 1) * OWN],
                            xb[(1 - h) * OWN:(2 - h) * OWN]], axis=0))
        in_maps.append(m)
    return in_maps


def kernel(**inputs):
    nc = build_program()
    in_maps = make_inputs(inputs)
    res = run_bass_kernel_spmd(nc, in_maps, core_ids=list(range(8)))
    y = np.empty((4, SEQ, D), np.float32)
    for core in range(8):
        b, h = core // 2, core % 2
        y[b, h * OWN:(h + 1) * OWN] = res.results[core]["out"]
    return y


# version 13
# speedup vs baseline: 1.1211x; 1.0138x over previous
import numpy as np
import ml_dtypes
from contextlib import ExitStack

import concourse.bass as bass
import concourse.mybir as mybir
from concourse.bass_utils import run_bass_kernel_spmd

F32 = mybir.dt.float32
BF16 = mybir.dt.bfloat16
AF = mybir.ActivationFunctionType
ALU = mybir.AluOpType
AX = mybir.AxisListType

SEQ, OWN, D = 4096, 2048, 1024
NDT = 8
NH, HD, NPAIR = 16, 64, 8
NTF, NTO = 32, 16
NCH = 4
NKC = 8
E, F, FS = 8, 1408, 2816
EPS = 1e-5

FFN_UNITS = []
for _e in range(E):
    FFN_UNITS.append(("moe", _e, 0, 6))
    FFN_UNITS.append(("moe", _e, 6, 5))
for _f0, _nf in [(0, 6), (6, 6), (12, 5), (17, 5)]:
    FFN_UNITS.append(("sh", None, _f0, _nf))


class Sched:
    ENGINES = ("sync", "tensor", "scalar", "vector", "gpsimd")

    def __init__(self, nc, ctx):
        self.nc, self.ctx = nc, ctx
        self.q = {e: [] for e in self.ENGINES}
        self.sems = {}
        self.dma_sems = set()

    def _sem(self, name):
        if name not in self.sems:
            self.sems[name] = [self.ctx.enter_context(self.nc.semaphore(name)), 0]
        return self.sems[name]

    def n(self, name):
        return self._sem(name)[1]

    def op(self, eng, meth, *args, waits=(), inc=None, inck=1, **kw):
        ws = []
        for s, v in waits:
            rec = self._sem(s)
            ws.append((rec[0], rec[1] if v is None else v))
        ih = None
        if inc is not None:
            rec = self._sem(inc)
            rec[1] += inck
            ih = (rec[0], inck)
            if inck == 16:
                self.dma_sems.add(inc)
        self.q[eng].append((meth, args, kw, ws, ih))

    def counts(self):
        return {k: v[1] for k, v in self.sems.items()}

    def emit(self, blk):
        for eng in self.ENGINES:
            lst = self.q[eng]
            if not lst:
                continue

            def runner(e, lst=lst):
                for meth, args, kw, ws, ih in lst:
                    for h, v in ws:
                        if v > 0:
                            e.wait_ge(h, v)
                    ins = getattr(e, meth)(*args, **kw)
                    if ih is not None:
                        ins.then_inc(*ih)
            getattr(blk, eng)(runner)
        self.q = {e: [] for e in self.ENGINES}


def build_program(dbg=(), last_phase=6):
    _dbgc = [0]
    nc = bass.Bass("TRN2", target_bir_lowering=False, debug=False)

    xf = nc.dram_tensor("xf", [SEQ, D], F32, kind="ExternalInput").ap()
    cq = nc.dram_tensor("cq", [D, D], F32, kind="ExternalInput").ap()
    ck = nc.dram_tensor("ck", [D, D], F32, kind="ExternalInput").ap()
    cv = nc.dram_tensor("cv", [D, D], F32, kind="ExternalInput").ap()
    co = nc.dram_tensor("co", [D, D], F32, kind="ExternalInput").ap()
    bT = nc.dram_tensor("bT", [128, 4 * NDT], F32, kind="ExternalInput").ap()
    gwT = nc.dram_tensor("gwT", [D, E], F32, kind="ExternalInput").ap()
    wg = nc.dram_tensor("wg", [E, D, F], F32, kind="ExternalInput").ap()
    wu = nc.dram_tensor("wu", [E, D, F], F32, kind="ExternalInput").ap()
    wd = nc.dram_tensor("wd", [E, F, D], F32, kind="ExternalInput").ap()
    swg = nc.dram_tensor("swg", [D, FS], F32, kind="ExternalInput").ap()
    swu = nc.dram_tensor("swu", [D, FS], F32, kind="ExternalInput").ap()
    swd = nc.dram_tensor("swd", [FS, D], F32, kind="ExternalInput").ap()
    out = nc.dram_tensor("out", [OWN, D], F32, kind="ExternalOutput").ap()
    x1s = nc.dram_tensor("x1s", [NTO, 128, D], F32, kind="Internal").ap()

    dbga = {}
    for name, shape, dt in [
        ("xnT", [128, NDT, SEQ], BF16),
        ("qt", [128, OWN], BF16), ("kt", [128, SEQ], BF16),
        ("vt", [128, 32 * 130], BF16),
        ("outT", [128, NPAIR, OWN], BF16),
        ("xn2T", [128, NDT, OWN], BF16),
        ("comb", [128, NTO * E], F32),
        ("yacc", [128, NDT, OWN], F32),
    ]:
        if name in dbg:
            dbga[name] = nc.dram_tensor("dbg_" + name, shape, dt,
                                        kind="ExternalOutput").ap()

    with ExitStack() as ctx:
        En = ctx.enter_context
        S = Sched(nc, ctx)

        consts = En(nc.sbuf_tensor("consts", [128, 8], F32))
        C_EPS = consts[:, 0:1]
        C_NHALF = consts[:, 1:2]
        C_EIGHTH = consts[:, 3:4]
        C_M12 = consts[:, 4:5]
        identB = En(nc.sbuf_tensor("identB", [128, 128], BF16))
        shift64 = En(nc.sbuf_tensor("shift64", [128, 128], BF16))
        onescolF = En(nc.sbuf_tensor("onescolF", [128, 128], F32))
        selmat = En(nc.sbuf_tensor("selmat", [8, E, 128], BF16))
        biasT = En(nc.sbuf_tensor("biasT", [128, 4 * NDT], F32))
        gw_sb = En(nc.sbuf_tensor("gw_sb", [128, NDT, E], BF16))
        m_sum = En(nc.sbuf_tensor("m_sum", [128, NTF], F32))
        m_ssq = En(nc.sbuf_tensor("m_ssq", [128, NTF], F32))
        m_mean = En(nc.sbuf_tensor("m_mean", [128, NTF], F32))
        m_rstd = En(nc.sbuf_tensor("m_rstd", [128, NTF], F32))
        m_negms = En(nc.sbuf_tensor("m_negms", [128, NTF], F32))
        cT_sb = En(nc.sbuf_tensor("cT_sb", [8, OWN], BF16))
        xn2T = En(nc.sbuf_tensor("xn2T", [128, NDT, OWN], BF16))

        S.op("gpsimd", "memset", consts[:, 0:1], EPS)
        S.op("gpsimd", "memset", consts[:, 1:2], -0.5)
        S.op("gpsimd", "memset", consts[:, 2:3], 1.0 / D)
        S.op("gpsimd", "memset", consts[:, 3:4], 0.125)
        S.op("gpsimd", "memset", consts[:, 4:5], -12.0)
        S.op("gpsimd", "memset", identB[:], 0.0, inc="gi")
        S.op("gpsimd", "affine_select", out=identB[:], in_=identB[:],
             compare_op=ALU.not_equal, fill=1.0, base=0,
             pattern=[[-1, 128]], channel_multiplier=1,
             waits=[("gi", None)], inc="gi")
        S.op("gpsimd", "memset", shift64[:], 0.0, inc="gi")
        S.op("gpsimd", "affine_select", out=shift64[:], in_=shift64[:],
             compare_op=ALU.not_equal, fill=1.0, base=-64,
             pattern=[[1, 128]], channel_multiplier=-1,
             waits=[("gi", None)], inc="gi")
        S.op("gpsimd", "memset", onescolF[:], 1.0)
        S.op("gpsimd", "memset", selmat[:], 0.0, inc="gi")
        S.op("gpsimd", "affine_select", out=selmat[:], in_=selmat[:],
             compare_op=ALU.not_equal, fill=1.0, base=0,
             pattern=[[-1, E], [0, 128]], channel_multiplier=1,
             waits=[("gi", None)], inc="gc")
        S.op("gpsimd", "dma_start", out=biasT[:], in_=bT, inc="g0a", inck=16)
        S.op("gpsimd", "dma_start", out=gw_sb[:],
             in_=gwT.rearrange("(t p) e -> p t e", p=128), inc="g0b", inck=16)
        GC0 = S.n("gc")

        outT_stk = ExitStack()
        outT = outT_stk.enter_context(
            nc.sbuf_tensor("outT", [128, NPAIR, OWN], BF16))
        xnT_stk = ExitStack()
        xnT = xnT_stk.enter_context(
            nc.sbuf_tensor("xnT", [128, NDT, SEQ], BF16))
        open_stks = [outT_stk, xnT_stk]

        def finish():
            dws = [(d, None) for d in sorted(S.dma_sems)]
            S.op("sync", "nop", waits=dws)
            S.op("gpsimd", "nop", waits=dws)
            for st in reversed(open_stks):
                st.close()
            b = En(nc.Block())
            S.emit(b)
            return nc

        xr = xf.rearrange("(t p) d -> t p d", p=128)

        with ExitStack() as c1:
            cc = c1.enter_context
            xst = cc(nc.sbuf_tensor("xst", [128, 8, D], F32))
            xnst = cc(nc.sbuf_tensor("xnst", [128, 2, D], BF16))
            sq_scr = cc(nc.sbuf_tensor("sq_scr", [128, D], F32))
            psT = [cc(nc.psum_tensor(f"psT{i}", [128, D], BF16)) for i in range(2)]

            for j in range(NTF // 4):
                ws = []
                if j >= 2:
                    ws = [("aA", 4 * j - 4), ("vA", 4 * j - 4)]
                S.op("sync", "dma_start",
                     out=xst[:, (4 * j) % 8:(4 * j) % 8 + 4, :],
                     in_=xr[4 * j:4 * j + 4].rearrange("t p d -> p t d"),
                     waits=ws, inc=f"xldA{j % 2}", inck=16)
            for t in range(NTF):
                j = t // 4
                wx = (f"xldA{j % 2}", 16 * (j // 2 + 1))
                S.op("vector", "reduce_sum", m_sum[:, t:t + 1], xst[:, t % 8, :],
                     axis=AX.X, waits=[wx], inc="vA")
                S.op("scalar", "activation", sq_scr[:], xst[:, t % 8, :], AF.Square,
                     accum_out=m_ssq[:, t:t + 1],
                     waits=[wx, ("gc", GC0), ("aA", t)],
                     inc="aA")
            S.op("vector", "tensor_scalar_mul", m_mean[:], m_sum[:], 1.0 / D,
                 waits=[("vA", None)], inc="vS")
            S.op("vector", "tensor_mul", m_negms[:], m_mean[:], m_mean[:],
                 waits=[("vS", None)], inc="vS")
            S.op("vector", "scalar_tensor_tensor", m_rstd[:], m_ssq[:], 1.0 / D,
                 m_negms[:], ALU.mult, ALU.subtract,
                 waits=[("aA", None), ("vS", None)], inc="vS")
            S.op("scalar", "activation", m_ssq[:], m_rstd[:], AF.Ln, bias=C_EPS,
                 waits=[("vS", None)], inc="aS")
            S.op("scalar", "activation", m_rstd[:], m_ssq[:], AF.Exp, scale=C_NHALF,
                 waits=[("aS", None)], inc="aS")
            S.op("vector", "scalar_tensor_tensor", m_negms[:], m_mean[:], -1.0,
                 m_rstd[:], ALU.mult, ALU.mult, waits=[("aS", None)], inc="vS")
            STATS1 = S.n("vS")

            for j in range(NTF // 4):
                ws = [("vA", min(NTF, 28 + 4 * j)), ("aA", min(NTF, 28 + 4 * j))]
                if j >= 2:
                    ws.append(("vN", 4 * j - 4))
                S.op("sync", "dma_start",
                     out=xst[:, (4 * j) % 8:(4 * j) % 8 + 4, :],
                     in_=xr[4 * j:4 * j + 4].rearrange("t p d -> p t d"),
                     waits=ws, inc=f"xldB{j % 2}", inck=16)
            for t in range(NTF):
                jb = t // 4
                ws = [(f"xldB{jb % 2}", 16 * (jb // 2 + 1)), ("vS", STATS1)]
                if t >= 2:
                    ws.append(("pTr", t - 1))
                S.op("vector", "tensor_scalar", xnst[:, t % 2, :], xst[:, t % 8, :],
                     m_rstd[:, t:t + 1], m_negms[:, t:t + 1], ALU.mult, ALU.add,
                     waits=ws, inc="vN")
                for dt in range(NDT):
                    ws = [("vN", S.n("vN")), ("gc", GC0)]
                    if t >= 2:
                        ws.append(("vE", t - 1))
                    S.op("tensor", "transpose",
                         psT[t % 2][:, dt * 128:(dt + 1) * 128],
                         xnst[:, t % 2, dt * 128:(dt + 1) * 128], identB[:],
                         waits=ws if dt == 0 else (),
                         inc="pTr" if dt == NDT - 1 else None)
                S.op("vector", "tensor_copy",
                     xnT[:, :, t * 128:(t + 1) * 128],
                     psT[t % 2][:].rearrange("p (dt c) -> p dt c", c=128),
                     waits=[("pTr", t + 1)], inc="vE")
        XNT = S.n("vE")

        if "xnT" in dbga:
            S.op("gpsimd", "dma_start", out=dbga["xnT"], in_=xnT[:],
                 waits=[("vE", XNT)], inc=f"dbg0", inck=16)

        if last_phase < 2:
            return finish()

        with ExitStack() as c2:
            cc = c2.enter_context
            cqp = [cc(nc.sbuf_tensor(f"cqp{i}", [128, NDT, 128], BF16)) for i in range(2)]
            ckp = [cc(nc.sbuf_tensor(f"ckp{i}", [128, NDT, 128], BF16)) for i in range(2)]
            cvp = [cc(nc.sbuf_tensor(f"cvp{i}", [128, NDT, 128], BF16)) for i in range(2)]
            QTp = [cc(nc.sbuf_tensor(f"QTp{i}", [128, OWN], BF16)) for i in range(2)]
            KTp = [cc(nc.sbuf_tensor(f"KTp{i}", [128, SEQ], BF16)) for i in range(2)]
            Vp = [cc(nc.sbuf_tensor(f"Vp{i}", [128, 32, 130], BF16)) for i in range(2)]
            PT = cc(nc.sbuf_tensor("PT", [128, 4, 1024], BF16))
            lscr = cc(nc.sbuf_tensor("lscr", [128, 2, 512], F32))
            invl = cc(nc.sbuf_tensor("invl", [128, 2, 512], F32))
            stg = cc(nc.sbuf_tensor("stg", [128, 2, 512], BF16))
            psS = [cc(nc.psum_tensor(f"psS{i}", [128, 1024], F32)) for i in range(2)]
            psO = cc(nc.psum_tensor("psO", [128, 1024], F32))
            psP = [cc(nc.psum_tensor(f"psP{i}", [128, 512], F32)) for i in range(2)]

            for i in range(2):
                S.op("gpsimd", "memset", Vp[i][:], 1.0,
                     waits=[("vE", XNT)], inc="gc")
            GV = S.n("gc")

            groups = [(2 * g, 2) for g in range(16)]
            UPC = 32
            pair_end = {}
            prev_aE = [0]
            prev_recip = [0]
            proj_pP = {}
            proj_done = {}
            wpair_lv = {}

            def emit_wdma(p):
                bp_ = p % 2
                colsl_ = slice(128 * p, 128 * (p + 1))
                ws = [("g0a", None), ("g0b", None), ("gc", GV), ("vE", XNT)]
                if p >= 2:
                    ws.append(("pP", proj_pP[p - 2]))
                for cn, cmat, dst in (("q", cq, cqp[bp_]), ("k", ck, ckp[bp_]),
                                      ("v", cv, cvp[bp_])):
                    S.op("gpsimd", "dma_start", out=dst[:],
                         in_=cmat.rearrange("(t p) n -> p t n", p=128)[:, :, colsl_],
                         waits=ws, inc=f"wld{cn}{bp_}", inck=16)
                    ws = []
                wpair_lv[p] = [(f"wld{cn}{bp_}", 16 * (p // 2 + 1))
                               for cn in ("q", "k", "v")]

            def make_jobs(p):
                return [(p, "q", c) for c in range(NCH)] + \
                       [(p, "k", kc) for kc in range(NKC)] + \
                       [(p, "v", kt) for kt in range(32)]

            class ProjQ:

                def __init__(self, p, gates):
                    self.p = p
                    self.jobs = make_jobs(p)
                    self.j = 0
                    self.vp0 = S.n("vP")
                    self.gates = gates

                def emit_one(self):
                    p, kind, idx = self.jobs[self.j]
                    bp_ = p % 2
                    pj = psP[self.j % 2][:]
                    ws = list(wpair_lv[p])
                    if self.j >= 2:
                        ws.append(("vP", self.vp0 + self.j - 1))
                    ev_ws = [("pP", None)] + self.gates
                    if kind == "q":
                        for dt in range(NDT):
                            S.op("tensor", "matmul", pj,
                                 cqp[bp_][:, dt, :],
                                 xnT[:, dt, idx * 512:(idx + 1) * 512],
                                 start=(dt == 0), stop=(dt == NDT - 1),
                                 waits=ws if dt == 0 else (),
                                 inc="pP" if dt == NDT - 1 else None)
                        S.op("vector", "tensor_scalar",
                             QTp[bp_][:, idx * 512:(idx + 1) * 512], pj,
                             biasT[:, p:p + 1], None, ALU.add,
                             waits=ev_ws, inc="vP")
                    elif kind == "k":
                        for dt in range(NDT):
                            S.op("tensor", "matmul", pj,
                                 ckp[bp_][:, dt, :],
                                 xnT[:, dt, idx * 512:(idx + 1) * 512],
                                 start=(dt == 0), stop=(dt == NDT - 1),
                                 waits=ws if dt == 0 else (),
                                 inc="pP" if dt == NDT - 1 else None)
                        S.op("vector", "tensor_scalar",
                             KTp[bp_][:, idx * 512:(idx + 1) * 512], pj,
                             biasT[:, NDT + p:NDT + p + 1], None, ALU.add,
                             waits=ev_ws, inc="vP")
                    else:
                        pv = pj[:, 0:128]
                        for dt in range(NDT):
                            S.op("tensor", "matmul", pv,
                                 xnT[:, dt, idx * 128:(idx + 1) * 128],
                                 cvp[bp_][:, dt, :],
                                 start=(dt == 0), stop=(dt == NDT - 1),
                                 waits=ws if dt == 0 else (),
                                 inc="pP" if dt == NDT - 1 else None)
                        dstv = Vp[bp_][:, idx, :].rearrange(
                            "p (a b) -> p a b", b=65)[:, :, 0:64]
                        srcv = pv.rearrange("p (a b) -> p a b", b=64)
                        S.op("vector", "tensor_copy", dstv, srcv,
                             waits=ev_ws, inc="vP")
                    self.j += 1

                def drain_to(self, frac):
                    tgt = min(len(self.jobs), int(len(self.jobs) * frac + 0.999))
                    while self.j < tgt:
                        self.emit_one()

                def finish(self):
                    while self.j < len(self.jobs):
                        self.emit_one()
                    proj_pP[self.p] = S.n("pP")
                    proj_done[self.p] = S.n("vP")

            emit_wdma(0)
            if NPAIR > 1:
                emit_wdma(1)
            q0 = ProjQ(0, [])
            q0.finish()

            for p in range(NPAIR):
                bp = p % 2
                if p + 2 < NPAIR:
                    emit_wdma(p + 2)
                if p + 1 < NPAIR:
                    nq = ProjQ(p + 1, [("pQK", pair_end[p - 1]["pQK"]),
                                       ("pPV", pair_end[p - 1]["pPV"])]
                               if p > 0 else [])
                else:
                    nq = None

                exp_at = {}
                pv_at = {}
                recip_at = {}
                evac_at = {}
                pvb_free = {}
                pend = []

                def emit_pv(job):
                    u, cc_, pts, h = job
                    ws = [("aE", exp_at[u])]
                    if u % UPC <= 1:
                        if cc_ > 0:
                            gate = evac_at[cc_ - 1] if h == 0 else pvb_free[cc_ - 1]
                        else:
                            gate = S.n("vT")
                        ws.append(("vT", gate))
                    vsl = slice(65 * h, 65 * h + 65)
                    k0_, glen_ = groups[(u % UPC) // 2]
                    for i in range(glen_):
                        kt = k0_ + i
                        S.op("tensor", "matmul",
                             psO[0:65, 512 * h:512 * h + 512],
                             Vp[bp][:, kt, vsl],
                             PT[:, pts, i * 512:(i + 1) * 512],
                             start=(kt == 0), stop=(kt == 31),
                             waits=ws if i == 0 else (),
                             inc="pPV" if i == glen_ - 1 else None)
                    pv_at[u] = S.n("pPV")

                for c in range(NCH):
                    qsl = slice(c * 512, (c + 1) * 512)
                    for gi, (k0, glen) in enumerate(groups):
                        ua = c * UPC + 2 * gi
                        ub = ua + 1
                        wsa = []
                        wsb = []
                        if ua >= 2:
                            wsa.append(("aE", exp_at[ua - 2]))
                            wsb.append(("aE", exp_at[ub - 2]))
                        else:
                            wsa = [("aE", prev_aE[0]), ("vP", proj_done[p])]
                            wsb = [("aE", prev_aE[0]), ("vP", proj_done[p]),
                                   ("vT", prev_recip[0])]
                        if c > 0 and ua == c * UPC:
                            wsb.append(("vT", recip_at[c - 1]))
                        for h, slot, ws0 in ((0, 0, wsa), (1, 1, wsb)):
                            rowsl = slice(64 * h, 64 * (h + 1))
                            for i in range(glen):
                                kt = k0 + i
                                S.op("tensor", "matmul",
                                     psS[slot][:, i * 512:(i + 1) * 512],
                                     KTp[bp][rowsl, kt * 128:(kt + 1) * 128],
                                     QTp[bp][rowsl, qsl],
                                     start=True, stop=True,
                                     tile_position=(64 * h, 0),
                                     waits=ws0 if i == 0 else (),
                                     inc="pQK" if i == glen - 1 else None)
                        nqk_b = S.n("pQK")
                        while pend:
                            emit_pv(pend.pop(0))
                        for h, u in ((0, ua), (1, ub)):
                            ws = [("pQK", nqk_b - (1 if h == 0 else 0))]
                            if u >= 4:
                                ws.append(("pPV", pv_at[u - 4]))
                            S.op("scalar", "activation",
                                 PT[:, u % 4, 0:glen * 512],
                                 psS[u % 2][:, 0:glen * 512],
                                 AF.Exp, bias=C_M12, scale=C_EIGHTH,
                                 waits=ws, inc="aE")
                            exp_at[u] = S.n("aE")
                        pend.append((ua, c, ua % 4, 0))
                        pend.append((ub, c, ub % 4, 1))
                        if nq is not None:
                            nq.drain_to((2 * gi + 2) / UPC / NCH + c / NCH)
                    while pend:
                        emit_pv(pend.pop(0))
                    S.op("vector", "tensor_copy", lscr[64:65, 0, :],
                         psO[64:65, 0:512],
                         waits=[("pPV", None), ("pT", None)], inc="vT")
                    S.op("vector", "tensor_copy", lscr[64:65, 1, :],
                         psO[64:65, 512:1024], inc="vT")
                    S.op("tensor", "matmul", psS[1][0:64, 0:512],
                         onescolF[64:65, 0:64], lscr[64:65, 0, :],
                         start=True, stop=True,
                         waits=[("vT", None), ("aE", None)], inc="pT")
                    S.op("tensor", "matmul", psS[1][0:64, 512:1024],
                         onescolF[64:65, 0:64], lscr[64:65, 1, :],
                         start=True, stop=True, inc="pT")
                    S.op("vector", "reciprocal",
                         invl[0:64, :, :].rearrange("p a b -> p (a b)"),
                         psS[1][0:64, 0:1024],
                         waits=[("pT", None)], inc="vT")
                    recip_at[c] = S.n("vT")
                    S.op("vector", "tensor_tensor", stg[0:64, 0, :],
                         psO[0:64, 0:512], invl[0:64, 0, :], ALU.mult,
                         waits=[("vT", None)], inc="vT")
                    S.op("vector", "tensor_scalar", outT[0:64, p, qsl],
                         stg[0:64, 0, :],
                         biasT[0:64, 2 * NDT + p:2 * NDT + p + 1], None, ALU.add,
                         waits=[("vT", None)], inc="vT")
                    AEV = S.n("vT")
                    S.op("vector", "tensor_tensor", stg[0:64, 1, :],
                         psO[0:64, 512:1024], invl[0:64, 1, :], ALU.mult,
                         inc="vT")
                    pvb_free[c] = S.n("vT")
                    S.op("tensor", "matmul", psO[:, 0:512], shift64[0:64, 0:128],
                         stg[0:64, 1, :], start=True, stop=True,
                         waits=[("vT", None)], inc="pT")
                    S.op("vector", "tensor_scalar", outT[64:128, p, qsl],
                         psO[64:128, 0:512],
                         biasT[64:128, 2 * NDT + p:2 * NDT + p + 1], None,
                         ALU.add, waits=[("pT", None)], inc="vT")
                    evac_at[c] = S.n("vT")
                if nq is not None:
                    nq.finish()
                pair_end[p] = {"pQK": S.n("pQK"), "pPV": S.n("pPV"),
                               "vT": S.n("vT")}
                prev_aE[0] = S.n("aE")
                prev_recip[0] = recip_at[NCH - 1]

                if p == 0:
                    if "qt" in dbga:
                        S.op("gpsimd", "dma_start", out=dbga["qt"], in_=QTp[0][:],
                             waits=[("vP", proj_done[0])], inc="dbg90", inck=16)
                    if "kt" in dbga:
                        S.op("gpsimd", "dma_start", out=dbga["kt"], in_=KTp[0][:],
                             waits=[("vP", proj_done[0])], inc="dbg91", inck=16)
                    if "vt" in dbga:
                        S.op("gpsimd", "dma_start", out=dbga["vt"],
                             in_=Vp[0][:].rearrange("p a b -> p (a b)"),
                             waits=[("vP", proj_done[0])], inc="dbg92", inck=16)
            ATT = S.counts()

        if "outT" in dbga:
            S.op("gpsimd", "dma_start", out=dbga["outT"], in_=outT[:],
                 waits=[("vT", ATT["vT"])], inc=f"dbg4", inck=16)

        xnT_stk.close()
        open_stks.remove(xnT_stk)

        if last_phase < 3:
            return finish()

        with ExitStack() as c3:
            cc = c3.enter_context
            co_sb = cc(nc.sbuf_tensor("co_sb", [128, NDT, D], BF16))
            aost = cc(nc.sbuf_tensor("aost", [128, NDT, 512], BF16))
            xo = cc(nc.sbuf_tensor("xo", [128, 8, D], F32))
            x1t = cc(nc.sbuf_tensor("x1t", [128, NTO, D], F32))
            xn2st = cc(nc.sbuf_tensor("xn2st", [128, 2, D], BF16))
            sq2 = cc(nc.sbuf_tensor("sq2", [128, D], F32))
            psC = [cc(nc.psum_tensor(f"psC{i}", [128, 512], F32)) for i in range(2)]
            psT3 = [cc(nc.psum_tensor(f"psT3{i}", [128, D], BF16)) for i in range(2)]

            S.op("gpsimd", "dma_start", out=co_sb[:],
                 in_=co.rearrange("(t p) n -> p t n", p=128),
                 waits=[("vT", ATT["vT"]), ("g0a", None), ("g0b", None)],
                 inc="wldc", inck=16)
            def load_xo(j):
                ws = [("vT", ATT["vT"])] if j <= 1 else [("vAdd", 4 * (j - 1))]
                S.op("sync", "dma_start",
                     out=xo[:, (4 * j) % 8:(4 * j) % 8 + 4, :],
                     in_=xr[4 * j:4 * j + 4].rearrange("t p d -> p t d"),
                     waits=ws, inc=f"xoL{j % 2}", inck=16)

            load_xo(0)
            load_xo(1)

            for c in range(NCH):
                qsl = slice(c * 512, (c + 1) * 512)
                for dout in range(NDT):
                    pj = psC[dout % 2][:]
                    ws = [("wldc", 16), ("vT", ATT["vT"])]
                    if c * NDT + dout >= 2:
                        ws.append(("aCo", S.n("aCo") - 1))
                    for dt in range(NDT):
                        S.op("tensor", "matmul", pj,
                             co_sb[:, dt, dout * 128:(dout + 1) * 128],
                             outT[:, dt, qsl],
                             start=(dt == 0), stop=(dt == NDT - 1),
                             waits=ws if dt == 0 else (),
                             inc="pCo" if dt == NDT - 1 else None)
                    S.op("scalar", "activation", aost[:, dout, :], pj, AF.Identity,
                         bias=biasT[:, 3 * NDT + dout:3 * NDT + dout + 1],
                         waits=[("pCo", None), ("pT3", None)], inc="aCo")
                ACO = S.n("aCo")
                for tt in range(4):
                    T = c * 4 + tt
                    for dt in range(NDT):
                        ws = [("aCo", ACO), ("vAdd", max(0, T - 1))]
                        S.op("tensor", "transpose",
                             psT3[T % 2][:, dt * 128:(dt + 1) * 128],
                             aost[:, dt, tt * 128:(tt + 1) * 128], identB[:],
                             waits=ws if dt == 0 else (),
                             inc="pT3" if dt == NDT - 1 else None)
                    S.op("vector", "tensor_tensor", x1t[:, T, :], xo[:, T % 8, :],
                         psT3[T % 2][:], ALU.add,
                         waits=[("pT3", None),
                                (f"xoL{(T // 4) % 2}", 16 * (T // 4 // 2 + 1))],
                         inc="vAdd")
                    S.op("sync", "dma_start", out=x1s[T], in_=x1t[:, T, :],
                         waits=[("vAdd", None)], inc=f"stX{T % 2}", inck=16)
                    S.op("vector", "reduce_sum", m_sum[:, T:T + 1], x1t[:, T, :],
                         axis=AX.X, waits=[("vAdd", None)], inc="vL")
                    S.op("scalar", "activation", sq2[:], x1t[:, T, :], AF.Square,
                         accum_out=m_ssq[:, T:T + 1],
                         waits=[("vAdd", None), ("aL", None)], inc="aL")
                if c + 2 < NCH:
                    load_xo(c + 2)
            s16 = slice(0, NTO)
            S.op("vector", "tensor_scalar_mul", m_mean[:, s16], m_sum[:, s16],
                 1.0 / D, waits=[("vL", None)], inc="vS2")
            S.op("vector", "tensor_mul", m_negms[:, s16], m_mean[:, s16],
                 m_mean[:, s16], waits=[("vS2", None)], inc="vS2")
            S.op("vector", "scalar_tensor_tensor", m_rstd[:, s16], m_ssq[:, s16],
                 1.0 / D, m_negms[:, s16], ALU.mult, ALU.subtract,
                 waits=[("aL", None), ("vS2", None)], inc="vS2")
            S.op("scalar", "activation", m_ssq[:, s16], m_rstd[:, s16], AF.Ln,
                 bias=C_EPS, waits=[("vS2", None)], inc="aS2")
            S.op("scalar", "activation", m_rstd[:, s16], m_ssq[:, s16], AF.Exp,
                 scale=C_NHALF, waits=[("aS2", None)], inc="aS2")
            S.op("vector", "scalar_tensor_tensor", m_negms[:, s16], m_mean[:, s16],
                 -1.0, m_rstd[:, s16], ALU.mult, ALU.mult,
                 waits=[("aS2", None)], inc="vS2")
            STATS2 = S.n("vS2")
            for T in range(NTO):
                ws = [("vS2", STATS2)]
                if T >= 2:
                    ws.append(("pT4", T - 1))
                S.op("vector", "tensor_scalar", xn2st[:, T % 2, :], x1t[:, T, :],
                     m_rstd[:, T:T + 1], m_negms[:, T:T + 1], ALU.mult, ALU.add,
                     waits=ws, inc="vN2")
                for dt in range(NDT):
                    ws = [("vN2", S.n("vN2"))]
                    if T >= 2:
                        ws.append(("vE2", T - 1))
                    S.op("tensor", "transpose",
                         psT3[T % 2][:, dt * 128:(dt + 1) * 128],
                         xn2st[:, T % 2, dt * 128:(dt + 1) * 128], identB[:],
                         waits=ws if dt == 0 else (),
                         inc="pT4" if dt == NDT - 1 else None)
                S.op("vector", "tensor_copy",
                     xn2T[:, :, T * 128:(T + 1) * 128],
                     psT3[T % 2][:].rearrange("p (dt c2) -> p dt c2", c2=128),
                     waits=[("pT4", T + 1)], inc="vE2")
            P3 = S.counts()

        if "xn2T" in dbga:
            S.op("gpsimd", "dma_start", out=dbga["xn2T"], in_=xn2T[:],
                 waits=[("vE2", P3["vE2"])], inc=f"dbg5", inck=16)

        outT_stk.close()
        open_stks.remove(outT_stk)

        if last_phase < 4:
            return finish()

        with ExitStack() as c4:
            cc = c4.enter_context
            identF = cc(nc.sbuf_tensor("identF", [128, 128], F32))
            lg_sb = cc(nc.sbuf_tensor("lg_sb", [8, NCH, 512], F32))
            lgt = cc(nc.sbuf_tensor("lgt", [128, NTO, E], F32))
            rt_is1 = cc(nc.sbuf_tensor("rt_is1", [128, NTO, E], F32))
            rt_is2 = cc(nc.sbuf_tensor("rt_is2", [128, NTO, E], F32))
            rt_sh = cc(nc.sbuf_tensor("rt_sh", [128, NTO, E], F32))
            rt_m1 = cc(nc.sbuf_tensor("rt_m1", [128, NTO], F32))
            rt_m2 = cc(nc.sbuf_tensor("rt_m2", [128, NTO], F32))
            rt_z = cc(nc.sbuf_tensor("rt_z", [128, NTO], F32))
            rt_e2 = cc(nc.sbuf_tensor("rt_e2", [128, NTO], F32))
            rt_w1 = cc(nc.sbuf_tensor("rt_w1", [128, NTO], F32))
            rt_w2 = cc(nc.sbuf_tensor("rt_w2", [128, NTO], F32))
            comb_bf = cc(nc.sbuf_tensor("comb_bf", [128, NTO, E], BF16))
            psG = cc(nc.psum_tensor("psG", [8, NCH * 512], F32))
            psR = cc(nc.psum_tensor("psR", [128, NTO * 8], F32))
            psR2 = cc(nc.psum_tensor("psR2", [8, NTO * 128], BF16))

            S.op("gpsimd", "memset", identF[:], 0.0,
                 waits=[("vE2", P3["vE2"]), ("stX0", 16 * (NTO // 2)),
                        ("stX1", 16 * (NTO // 2))], inc="gF")
            S.op("gpsimd", "affine_select", out=identF[:], in_=identF[:],
                 compare_op=ALU.not_equal, fill=1.0, base=0,
                 pattern=[[-1, 128]], channel_multiplier=1,
                 waits=[("gF", None)], inc="gF")

            for ch in range(NCH):
                for dt in range(NDT):
                    S.op("tensor", "matmul", psG[:, ch * 512:(ch + 1) * 512],
                         gw_sb[:, dt, :], xn2T[:, dt, ch * 512:(ch + 1) * 512],
                         start=(dt == 0), stop=(dt == NDT - 1),
                         waits=[("vE2", P3["vE2"]), ("g0b", None)]
                         if (ch == 0 and dt == 0) else (),
                         inc="pG" if dt == NDT - 1 else None)
            S.op("vector", "tensor_copy",
                 lg_sb[:].rearrange("p a b -> p (a b)"), psG[:],
                 waits=[("pG", None)], inc="vR")
            lg2 = lg_sb[:].rearrange("p a b -> p (a b)")
            for T in range(NTO):
                S.op("tensor", "transpose", psR[:, T * 8:(T + 1) * 8],
                     lg2[0:8, T * 128:(T + 1) * 128], identF[0:8, 0:8],
                     waits=[("vR", None), ("gF", None)] if T == 0 else (),
                     inc="pR" if T == NTO - 1 else None)
            S.op("vector", "tensor_copy", lgt[:].rearrange("p a b -> p (a b)"),
                 psR[:], waits=[("pR", None)], inc="vR")
            for T in range(NTO):
                S.op("vector", "reduce_max", rt_m1[:, T:T + 1], lgt[:, T, :],
                     axis=AX.X, waits=[("vR", None)], inc="vR")
                S.op("vector", "tensor_scalar", rt_is1[:, T, :], lgt[:, T, :],
                     rt_m1[:, T:T + 1], None, ALU.is_ge,
                     waits=[("vR", None)], inc="vR")
                S.op("vector", "scalar_tensor_tensor", rt_sh[:, T, :],
                     rt_is1[:, T, :], -1e30, lgt[:, T, :], ALU.mult, ALU.add,
                     waits=[("vR", None)], inc="vR")
                S.op("vector", "reduce_max", rt_m2[:, T:T + 1], rt_sh[:, T, :],
                     axis=AX.X, waits=[("vR", None)], inc="vR")
                S.op("vector", "tensor_scalar", rt_is2[:, T, :], rt_sh[:, T, :],
                     rt_m2[:, T:T + 1], None, ALU.is_ge,
                     waits=[("vR", None)], inc="vR")
                S.op("vector", "tensor_scalar", rt_sh[:, T, :], lgt[:, T, :],
                     rt_m1[:, T:T + 1], None, ALU.subtract,
                     waits=[("vR", None)], inc="vR")
            S.op("scalar", "activation", rt_sh[:], rt_sh[:], AF.Exp,
                 waits=[("vR", None), ("aL", P3["aL"])], inc="aR")
            S.op("vector", "reduce_sum", rt_z[:], rt_sh[:], axis=AX.X,
                 waits=[("aR", None)], inc="vR")
            S.op("vector", "tensor_sub", rt_e2[:], rt_m2[:], rt_m1[:],
                 waits=[("vR", None)], inc="vR")
            S.op("scalar", "activation", rt_e2[:], rt_e2[:], AF.Exp,
                 waits=[("vR", None)], inc="aR")
            S.op("vector", "scalar_tensor_tensor", rt_w1[:], rt_z[:], 1e-20,
                 rt_e2[:], ALU.mult, ALU.add,
                 waits=[("aR", None), ("vR", None)], inc="vR")
            S.op("vector", "tensor_scalar", rt_w1[:], rt_w1[:], 1.0, None, ALU.add,
                 waits=[("vR", None)], inc="vR")
            S.op("vector", "reciprocal", rt_w1[:], rt_w1[:],
                 waits=[("vR", None)], inc="vR")
            S.op("vector", "tensor_scalar", rt_w1[:], rt_w1[:], 2.5, None, ALU.mult,
                 waits=[("vR", None)], inc="vR")
            S.op("vector", "tensor_mul", rt_w2[:], rt_e2[:], rt_w1[:],
                 waits=[("vR", None)], inc="vR")
            b1, b2 = bass.broadcast_tensor_aps(
                rt_is1[:], rt_w1[:].rearrange("p (t o) -> p t o", o=1))
            S.op("vector", "tensor_tensor", rt_is1[:], b1, b2, ALU.mult,
                 waits=[("vR", None)], inc="vR")
            b1, b2 = bass.broadcast_tensor_aps(
                rt_is2[:], rt_w2[:].rearrange("p (t o) -> p t o", o=1))
            S.op("vector", "tensor_tensor", rt_is2[:], b1, b2, ALU.mult,
                 waits=[("vR", None)], inc="vR")
            S.op("vector", "tensor_add", lgt[:], rt_is1[:], rt_is2[:],
                 waits=[("vR", None)], inc="vR")
            if "comb" in dbga:
                S.op("gpsimd", "dma_start", out=dbga["comb"],
                     in_=lgt[:].rearrange("p a b -> p (a b)"),
                     waits=[("vR", None)], inc=f"dbg6", inck=16)
            S.op("vector", "tensor_copy", comb_bf[:], lgt[:],
                 waits=[("vR", None)], inc="vR")
            for T in range(NTO):
                S.op("tensor", "transpose", psR2[:, T * 128:(T + 1) * 128],
                     comb_bf[:, T, :], identB[:],
                     waits=[("vR", None)] if T == 0 else (),
                     inc="pR2" if T == NTO - 1 else None)
            S.op("vector", "tensor_copy", cT_sb[:], psR2[:],
                 waits=[("pR2", None)], inc="vR")
            P4 = S.counts()

        if last_phase < 5:
            return finish()

        yacc = En(nc.sbuf_tensor("yacc", [128, NDT, OWN], F32))
        with ExitStack() as c5:
            cc = c5.enter_context
            wgh = [cc(nc.sbuf_tensor(f"wgh{i}", [128, NDT, 768], BF16)) for i in range(2)]
            wuh = [cc(nc.sbuf_tensor(f"wuh{i}", [128, NDT, 768], BF16)) for i in range(2)]
            wdh = [cc(nc.sbuf_tensor(f"wdh{i}", [128, 6, D], BF16)) for i in range(2)]
            prod = cc(nc.sbuf_tensor("prod", [128, 6, OWN // 2], BF16))
            gs_sb = cc(nc.sbuf_tensor("gs_sb", [128, 2, 512], BF16))
            us_sb = cc(nc.sbuf_tensor("us_sb", [128, 2, 512], BF16))
            ce_sb = cc(nc.sbuf_tensor("ce_sb", [128, OWN], BF16))
            psGU = [cc(nc.psum_tensor(f"psGU{i}", [128, 512], F32)) for i in range(4)]
            psD = [cc(nc.psum_tensor(f"psD{i}", [128, 512], F32)) for i in range(2)]
            psCE = cc(nc.psum_tensor("psCE", [128, 512], F32))

            unit_end = {}
            jig = 0
            for ui, (kind, e, f0, nf) in enumerate(FFN_UNITS):
                bu = ui % 2
                ws = [("vT", ATT["vT"])]
                if ui >= 2:
                    ws.append(("pD", unit_end[ui - 2]["pD"]))
                    ws.append(("pGU", unit_end[ui - 2]["pGU"]))
                if kind == "moe":
                    srcg = wg[e].rearrange("(t p) f -> p t f", p=128)[:, :, f0 * 128:(f0 + nf) * 128]
                    srcu = wu[e].rearrange("(t p) f -> p t f", p=128)[:, :, f0 * 128:(f0 + nf) * 128]
                    srcd = wd[e, f0 * 128:(f0 + nf) * 128, :].rearrange("(t p) d2 -> p t d2", p=128)
                else:
                    srcg = swg.rearrange("(t p) f -> p t f", p=128)[:, :, f0 * 128:(f0 + nf) * 128]
                    srcu = swu.rearrange("(t p) f -> p t f", p=128)[:, :, f0 * 128:(f0 + nf) * 128]
                    srcd = swd[f0 * 128:(f0 + nf) * 128, :].rearrange("(t p) d2 -> p t d2", p=128)
                S.op("gpsimd", "dma_start", out=wgh[bu][:, :, 0:nf * 128], in_=srcg,
                     waits=ws, inc=f"wfg{bu}", inck=16)
                S.op("gpsimd", "dma_start", out=wuh[bu][:, :, 0:nf * 128], in_=srcu,
                     inc=f"wfu{bu}", inck=16)
                S.op("gpsimd", "dma_start", out=wdh[bu][:, 0:nf, :], in_=srcd,
                     inc=f"wfd{bu}", inck=16)
                lvl = 16 * (ui // 2 + 1)
                WUg = (f"wfg{bu}", lvl)
                WUu = (f"wfu{bu}", lvl)
                WUd = (f"wfd{bu}", lvl)

                if kind == "moe" and f0 == 0:
                    for ch in range(NCH):
                        ws = [("vUs", None), ("vCE", None)]
                        if ui == 0 and ch == 0:
                            ws += [("vR", P4["vR"])]
                        S.op("tensor", "matmul", psCE[:],
                             selmat[:, e, :], cT_sb[:, ch * 512:(ch + 1) * 512],
                             start=True, stop=True, waits=ws, inc="pCE")
                        S.op("vector", "tensor_copy",
                             ce_sb[:, ch * 512:(ch + 1) * 512], psCE[:],
                             waits=[("pCE", None)], inc="vCE")
                CE = S.n("vCE")

                down_q = []
                dji = [0]

                def emit_down(job):
                    dout, ch_, chl_, lvl = job
                    pd = psD[dji[0] % 2][:]
                    ws = [("vPr", lvl), WUd]
                    if ui == 0 and dji[0] < 2:
                        ws.append(("vR", P4["vR"]))
                    if ui > 0 or dji[0] >= 2:
                        ws.append(("vF2", S.n("vF2") - 1))
                    for ft_ in range(nf):
                        S.op("tensor", "matmul", pd,
                             wdh[bu][:, ft_, dout * 128:(dout + 1) * 128],
                             prod[:, ft_, chl_ * 512:(chl_ + 1) * 512],
                             start=(ft_ == 0), stop=(ft_ == nf - 1),
                             waits=ws if ft_ == 0 else (),
                             inc="pD" if ft_ == nf - 1 else None)
                    if ui == 0:
                        S.op("vector", "tensor_copy",
                             yacc[:, dout, ch_ * 512:(ch_ + 1) * 512], pd,
                             waits=[("pD", None)], inc="vF2")
                    else:
                        S.op("vector", "tensor_tensor",
                             yacc[:, dout, ch_ * 512:(ch_ + 1) * 512],
                             yacc[:, dout, ch_ * 512:(ch_ + 1) * 512], pd,
                             ALU.add, waits=[("pD", None), ("vF2", None)],
                             inc="vF2")
                    dji[0] += 1

                for th in range(2):
                    for chl in range(2):
                        ch = 2 * th + chl
                        for ft in range(nf):
                            pg = psGU[(2 * jig) % 4][:]
                            pu = psGU[(2 * jig + 1) % 4][:]
                            ws = [WUg, ("vE2", P3["vE2"])]
                            if ui == 0 and th == 0 and ft == 0 and chl == 0:
                                ws.append(("vR", P4["vR"]))
                            if jig >= 2:
                                ws.append(("aSil", jig - 1))
                            for dt in range(NDT):
                                S.op("tensor", "matmul", pg,
                                     wgh[bu][:, dt, ft * 128:(ft + 1) * 128],
                                     xn2T[:, dt, ch * 512:(ch + 1) * 512],
                                     start=(dt == 0), stop=(dt == NDT - 1),
                                     waits=ws if dt == 0 else (),
                                     inc="pGU" if dt == NDT - 1 else None)
                            ws2 = [WUu]
                            if jig >= 2:
                                ws2.append(("vUs", jig - 1))
                            for dt in range(NDT):
                                S.op("tensor", "matmul", pu,
                                     wuh[bu][:, dt, ft * 128:(ft + 1) * 128],
                                     xn2T[:, dt, ch * 512:(ch + 1) * 512],
                                     start=(dt == 0), stop=(dt == NDT - 1),
                                     waits=ws2 if dt == 0 else (),
                                     inc="pGU" if dt == NDT - 1 else None)
                            PGU = S.n("pGU")
                            S.op("scalar", "activation", gs_sb[:, jig % 2, :], pg,
                                 AF.Silu,
                                 waits=[("pGU", PGU - 1), ("vPr", max(0, jig - 1))],
                                 inc="aSil")
                            if kind == "moe":
                                S.op("vector", "tensor_tensor", us_sb[:, jig % 2, :],
                                     pu, ce_sb[:, ch * 512:(ch + 1) * 512], ALU.mult,
                                     waits=[("pGU", PGU), ("vCE", CE),
                                            ("vPr", max(0, S.n("vPr") - 1))],
                                     inc="vUs")
                            else:
                                S.op("vector", "tensor_copy", us_sb[:, jig % 2, :],
                                     pu,
                                     waits=[("pGU", PGU),
                                            ("vPr", max(0, S.n("vPr") - 1))],
                                     inc="vUs")
                            S.op("vector", "tensor_mul",
                                 prod[:, ft, chl * 512:(chl + 1) * 512],
                                 gs_sb[:, jig % 2, :], us_sb[:, jig % 2, :],
                                 waits=[("aSil", None), ("vUs", None), ("pD", None)],
                                 inc="vPr")
                            jig += 1
                            while down_q and len(down_q) > (nf - 1 - ft) * 2 - 1:
                                emit_down(down_q.pop(0))
                        lvl = S.n("vPr")
                        while down_q:
                            emit_down(down_q.pop(0))
                        down_q = [(dout, 2 * th + chl, chl, lvl)
                                  for dout in range(NDT)]
                while down_q:
                    emit_down(down_q.pop(0))
                unit_end[ui] = {"pD": S.n("pD"), "pGU": S.n("pGU")}
            P5 = S.counts()

        if "yacc" in dbga:
            S.op("gpsimd", "dma_start", out=dbga["yacc"], in_=yacc[:],
                 waits=[("vF2", P5["vF2"])], inc=f"dbg7", inck=16)

        if last_phase < 6:
            return finish()

        with ExitStack() as c6:
            cc = c6.enter_context
            ybf = cc(nc.sbuf_tensor("ybf", [128, NDT, 512], BF16))
            x1r = cc(nc.sbuf_tensor("x1r", [128, 4, D], F32))
            obuf = cc(nc.sbuf_tensor("obuf", [128, 2, D], F32))
            psT6 = [cc(nc.psum_tensor(f"psT6{i}", [128, D], BF16)) for i in range(2)]

            outr = out.rearrange("(t p) d -> t p d", p=128)

            def load_x1(T, ws):
                S.op("sync", "dma_start", out=x1r[:, T % 4, :], in_=x1s[T],
                     waits=ws, inc=f"x1L{T % 4}", inck=16)

            for T in range(4):
                load_x1(T, [("stX0", 16 * (NTO // 2)), ("stX1", 16 * (NTO // 2)),
                            ("vF2", P5["vF2"]), ("aSil", P5["aSil"]),
                            ("vPr", P5["vPr"])])
            for c in range(NCH):
                S.op("vector", "tensor_copy", ybf[:],
                     yacc[:, :, c * 512:(c + 1) * 512],
                     waits=[("vF2", P5["vF2"]), ("vO", None)], inc="vY")
                for tt in range(4):
                    T = c * 4 + tt
                    for dt in range(NDT):
                        ws = [("vY", None), ("vF2", P5["vF2"])]
                        if T >= 2:
                            ws.append(("vO", T - 1))
                        S.op("tensor", "transpose",
                             psT6[T % 2][:, dt * 128:(dt + 1) * 128],
                             ybf[:, dt, tt * 128:(tt + 1) * 128], identB[:],
                             waits=ws if dt == 0 else (),
                             inc="pT6" if dt == NDT - 1 else None)
                    S.op("vector", "tensor_tensor", obuf[:, T % 2, :],
                         x1r[:, T % 4, :], psT6[T % 2][:], ALU.add,
                         waits=[("pT6", None), (f"x1L{T % 4}", 16 * (T // 4 + 1)),
                                (f"stO{T % 2}", 16 * (T // 2))], inc="vO")
                    S.op("sync", "dma_start", out=outr[T], in_=obuf[:, T % 2, :],
                         waits=[("vO", None)], inc=f"stO{T % 2}", inck=16)
                    if T + 4 < NTO:
                        load_x1(T + 4, [("vO", None)])
            fin = [(d, None) for d in sorted(S.dma_sems)]
            S.op("sync", "nop", waits=fin)
            S.op("gpsimd", "nop", waits=fin)

        blk = En(nc.Block())
        S.emit(blk)

    return nc


def _circulant(c):
    c = np.asarray(c, np.float32)
    idx = (np.arange(D)[None, :] - np.arange(D)[:, None]) % D
    return np.ascontiguousarray(c[idx])


def _biasT(b):
    return np.ascontiguousarray(np.asarray(b, np.float32).reshape(NDT, 128).T)


def make_inputs(inputs):
    x = np.ascontiguousarray(np.asarray(inputs["L0_attn_x"], np.float32))
    base = {
        "cq": _circulant(inputs["L0_attn_wq_c"]),
        "ck": _circulant(inputs["L0_attn_wk_c"]),
        "cv": _circulant(inputs["L0_attn_wv_c"]),
        "co": _circulant(inputs["L0_attn_wo_c"]),
        "bT": np.ascontiguousarray(np.concatenate(
            [_biasT(inputs["L0_attn_wq_b"]), _biasT(inputs["L0_attn_wk_b"]),
             _biasT(inputs["L0_attn_wv_b"]), _biasT(inputs["L0_attn_wo_b"])],
            axis=1)),
        "gwT": np.ascontiguousarray(
            np.asarray(inputs["L0_ffn_gate_w"], np.float32).T),
        "wg": np.ascontiguousarray(np.asarray(inputs["L0_ffn_w_gate"], np.float32)),
        "wu": np.ascontiguousarray(np.asarray(inputs["L0_ffn_w_up"], np.float32)),
        "wd": np.ascontiguousarray(np.asarray(inputs["L0_ffn_w_down"], np.float32)),
        "swg": np.ascontiguousarray(np.asarray(inputs["L0_ffn_sw_gate"], np.float32)),
        "swu": np.ascontiguousarray(np.asarray(inputs["L0_ffn_sw_up"], np.float32)),
        "swd": np.ascontiguousarray(np.asarray(inputs["L0_ffn_sw_down"], np.float32)),
    }
    in_maps = []
    for core in range(8):
        b, h = core // 2, core % 2
        xb = x[b]
        m = dict(base)
        m["xf"] = np.ascontiguousarray(
            np.concatenate([xb[h * OWN:(h + 1) * OWN],
                            xb[(1 - h) * OWN:(2 - h) * OWN]], axis=0))
        in_maps.append(m)
    return in_maps


def kernel(**inputs):
    nc = build_program()
    in_maps = make_inputs(inputs)
    res = run_bass_kernel_spmd(nc, in_maps, core_ids=list(range(8)))
    y = np.empty((4, SEQ, D), np.float32)
    for core in range(8):
        b, h = core // 2, core % 2
        y[b, h * OWN:(h + 1) * OWN] = res.results[core]["out"]
    return y
